# version 12 (final)
# baseline (speedup 1.0000x reference)
import math
from contextlib import ExitStack

import numpy as np
import ml_dtypes

import concourse.bass as bass
import concourse.mybir as mybir
from concourse.bass_utils import run_bass_kernel_spmd

F32 = mybir.dt.float32
BF16 = mybir.dt.bfloat16
AF = mybir.ActivationFunctionType
ALU = mybir.AluOpType
AX = mybir.AxisListType
bf16 = ml_dtypes.bfloat16
f32 = np.float32

NCORES = 8
B, S, D = 4, 2048, 2048
HALF = S // 2
SCALE_ATT = 1.0 / math.sqrt(D)
EPS = 1e-5

ENG_SEM = {"tensor": "sT", "vector": "sV", "scalar": "sS", "gpsimd": "sP", "sync": "sY"}


class Prog:

    DMA_POOL = {"dS": 10, "dP": 6, "dV": 4, "dA": 4}

    def __init__(self, nc, stack):
        self.nc = nc
        self.q = {e: [] for e in ENG_SEM}
        self.sems = {}
        names = ["sT", "sV", "sS", "sP", "sY"] + [f"cc{i}" for i in range(4)]
        for qn, n in self.DMA_POOL.items():
            names += [f"{qn}{i}" for i in range(n)]
        for name in names:
            self.sems[name] = stack.enter_context(nc.semaphore(name))
        self.cnt = {k: 0 for k in self.sems}
        self.dma_idx = {qn: 0 for qn in self.DMA_POOL}
        self.deps = {}
        self.known = {e: {} for e in ENG_SEM}

    def _waits_for(self, eng, reads, writes):
        want = {}
        for k in reads:
            d = self.deps.get(k)
            if d and d["w"]:
                s, c = d["w"]
                want[s] = max(want.get(s, 0), c)
        for k in writes:
            d = self.deps.get(k)
            if d:
                if d["w"]:
                    s, c = d["w"]
                    want[s] = max(want.get(s, 0), c)
                for s, c in d["r"]:
                    want[s] = max(want.get(s, 0), c)
        kn = self.known[eng]
        out = {}
        for s, c in want.items():
            if kn.get(s, 0) < c:
                out[s] = c
                kn[s] = c
        return out

    def op(self, eng, build, reads=(), writes=(), dma=None):
        if dma:
            i = self.dma_idx[dma]
            self.dma_idx[dma] = (i + 1) % self.DMA_POOL[dma]
            sem_name = f"{dma}{i}"
            inc = 16
            reads = list(reads) + [f"__{sem_name}"]
            writes = list(writes) + [f"__{sem_name}"]
        else:
            sem_name = ENG_SEM[eng]
            inc = 1
        waits = self._waits_for(eng, reads, writes)
        self.cnt[sem_name] += inc
        myc = self.cnt[sem_name]
        sems = self.sems

        def closure(e, waits=waits, build=build, sem=sems[sem_name], inc=inc):
            for s, c in waits.items():
                e.wait_ge(sems[s], c)
            build(e).then_inc(sem, inc)

        self.q[eng].append(closure)
        for k in reads:
            self.deps.setdefault(k, {"w": None, "r": []})["r"].append((sem_name, myc))
        for k in writes:
            self.deps[k] = {"w": (sem_name, myc), "r": []}
        return (sem_name, myc)

    def merge_key(self, new, olds):
        rs = []
        for k in olds:
            d = self.deps.pop(k, None)
            if d:
                if d["w"]:
                    rs.append(d["w"])
                rs.extend(d["r"])
        self.deps[new] = {"w": None, "r": rs}

    def barrier_wait(self, eng, dep):
        s, c = dep
        if self.known[eng].get(s, 0) >= c:
            return
        self.known[eng][s] = c
        sems = self.sems
        self.q[eng].append(lambda e, s=s, c=c: e.wait_ge(sems[s], c))

    def finalize(self, blk, final_eng="sync"):
        for qn in ("dA", "dP"):
            for i in range(self.DMA_POOL[qn]):
                sn = f"{qn}{i}"
                if self.cnt[sn]:
                    self.barrier_wait(final_eng, (sn, self.cnt[sn]))
        for eng in ENG_SEM:
            lst = self.q[eng]
            if not lst:
                continue
            getattr(blk, eng)(lambda e, lst=list(lst): [f(e) for f in lst])

    def dma(self, eng, out, in_, reads=(), writes=(), q="dS"):
        return self.op(eng, lambda e: e.dma_start(out=out, in_=in_),
                       reads=reads, writes=writes, dma=q)

    def mm(self, out, lhsT, rhs, start, stop, reads=(), writes=()):
        return self.op("tensor",
                       lambda e: e.matmul(out, lhsT, rhs, start=start, stop=stop),
                       reads=reads, writes=writes)

    def act(self, out, in_, func, bias=0.0, scale=1.0, accum=None, reads=(), writes=()):
        return self.op("scalar",
                       lambda e: e.activation(out, in_, func, bias=bias, scale=scale,
                                              accum_out=accum),
                       reads=reads, writes=writes)

    def vop(self, fn, reads=(), writes=()):
        return self.op("vector", fn, reads=reads, writes=writes)

    def cc(self, idx, kind, op, groups, in_ap, out_ap, reads=(), writes=()):
        sem_name = f"cc{idx}"
        waits = self._waits_for("gpsimd", reads, writes)
        sems = self.sems

        def closure(e, waits=waits):
            for s, c in waits.items():
                e.wait_ge(sems[s], c)
            e.collective_compute(kind, op, replica_groups=groups,
                                 ins=[in_ap], outs=[out_ap]).then_inc(sems[sem_name], 1)

        self.q["gpsimd"].append(closure)
        for k in reads:
            self.deps.setdefault(k, {"w": None, "r": []})["r"].append((sem_name, 1))
        for k in writes:
            self.deps[k] = {"w": (sem_name, 1), "r": []}


def emit_ln(p, x_ap, x_key, xn_ap, xn_key, stats, slot):
    c = lambda t: stats[t][:, slot:slot + 1]
    ks = f"ln{slot}_"
    p.vop(lambda e: e.reduce_sum(c("sum"), x_ap, axis=AX.X),
          reads=[x_key], writes=[ks + "s"])
    p.act(xn_ap, x_ap, AF.Square, accum=c("sumsq"),
          reads=[x_key], writes=[ks + "q", xn_key])
    p.vop(lambda e: e.tensor_scalar_mul(c("nm"), c("sum"), -1.0 / D),
          reads=[ks + "s"], writes=[ks + "nm"])
    p.vop(lambda e: e.tensor_mul(c("msq"), c("nm"), c("nm")),
          reads=[ks + "nm"], writes=[ks + "m2"])
    p.vop(lambda e: e.scalar_tensor_tensor(c("var"), c("sumsq"), 1.0 / D, c("msq"),
                                           ALU.mult, ALU.subtract),
          reads=[ks + "q", ks + "m2"], writes=[ks + "v"])
    p.act(c("var"), c("var"), AF.Ln, bias=stats["cst"][:, 0:1],
          reads=[ks + "v", "cst"], writes=[ks + "v"])
    p.act(c("rstd"), c("var"), AF.Exp, scale=stats["cst"][:, 1:2],
          reads=[ks + "v", "cst"], writes=[ks + "r"])
    p.vop(lambda e: e.tensor_scalar(xn_ap, x_ap, c("nm"), c("rstd"), ALU.add, ALU.mult),
          reads=[x_key, ks + "nm", ks + "r"], writes=[xn_key])


def ln_stats_tensors(nc, stack, nslots=2):
    names = ["sum", "sumsq", "nm", "msq", "var", "rstd"]
    d = {n: stack.enter_context(nc.sbuf_tensor(f"lns_{n}", [128, nslots], F32))
         for n in names}
    d["cst"] = stack.enter_context(nc.sbuf_tensor("lns_cst", [128, 4], F32))
    return d


def emit_consts(p, stats):
    for i, v in enumerate([EPS, -0.5, SCALE_ATT, -1.0]):
        p.vop(lambda e, i=i, v=v: e.memset(stats["cst"][:, i:i + 1], v),
              writes=["cst"] if i == 3 else [])



def build_l0():
    nc = bass.Bass("TRN2", target_bir_lowering=False, debug=False)
    dtin = lambda n, sh, ty: nc.dram_tensor(n, sh, ty, kind="ExternalInput").ap()
    x0p = dtin("x0p", [S, D], F32)
    wq = dtin("wq", [16, 16, 128, 128], F32)
    wk = dtin("wk", [16, 16, 128, 128], F32)
    wv = dtin("wv", [16, 128, D], F32)
    wo = dtin("wo", [16, 16, 128, 128], F32)
    wbase = dtin("wbase", [16, 16, 128, 128], F32)
    bbase = dtin("bbase", [128, 16], F32)
    rwt = dtin("rwt", [128, 128], F32)
    at4 = dtin("at4", [16, 128, 128], F32)
    bwt = dtin("bwt", [16, 128, 128], F32)
    cosq = dtin("cosq", [16, 128, HALF], BF16)
    sinq = dtin("sinq", [16, 128, HALF], BF16)
    cosk = dtin("cosk", [16, 128, S], BF16)
    sink = dtin("sink", [16, 128, S], BF16)
    maskq = dtin("maskq", [8, 128, S], BF16)
    ident_i = dtin("ident_i", [128, 128], BF16)
    perm_i = dtin("perm_i", [128, 128], BF16)
    rep8_i = dtin("rep8_i", [8, 128], BF16)
    x2out = nc.dram_tensor("x2out", [8, 128, D], F32, kind="ExternalOutput").ap()
    vhbm = nc.dram_tensor("vhbm", [16, 128, D], BF16, kind="Internal").ap()
    x1hbm = nc.dram_tensor("x1hbm", [8, 128, D], F32, kind="Internal").ap()

    with ExitStack() as stack:
        sb = lambda n, sh, ty: stack.enter_context(nc.sbuf_tensor(n, sh, ty))
        psa = lambda n, sh, ty: stack.enter_context(nc.psum_tensor(n, sh, ty))
        A1 = sb("A1", [128, 32768], BF16)
        A2 = sb("A2", [128, 32768], BF16)
        A3 = sb("A3", [128, 16384], BF16)
        xt = sb("xt", [128, D], F32)
        xt2 = sb("xt2", [128, D], F32)
        xn_tm = sb("xn_tm", [128, D], BF16)
        stats = ln_stats_tensors(nc, stack)
        wstg = sb("wstg", [128, 1024], F32)
        wbf = sb("wbf", [128, 2048], BF16)
        bw_f32 = sb("bw_f32", [128, 128], F32)
        vt_st = [sb(f"vt{i}", [128, D], BF16) for i in range(2)]
        qk_raw = sb("qk_raw", [128, 512], BF16)
        rope_t1 = sb("rope_t1", [128, 512], BF16)
        cs_t = sb("cs_t", [128, S], BF16)
        sn_t = sb("sn_t", [128, S], BF16)
        colv = sb("colv", [128, 8], F32)
        bbase_sb = sb("bbase_sb", [128, 16], F32)
        rw_sb = sb("rw_sb", [128, 16 * 8], BF16)
        oh_tm = sb("oh_tm", [128, 8], BF16)
        ohT = A1[0:8, 0:1024]
        of_sb = A1[:, 14336:14848]
        bwbf = A1[:, 14848:14976]
        ident = sb("ident", [128, 128], BF16)
        perm = sb("perm", [128, 128], BF16)
        repm = sb("repm", [8, 128], BF16)
        pf = [psa(f"pf{i}", [128, 512], F32) for i in range(6)]
        pb = [psa(f"pb{i}", [128, 1024], BF16) for i in range(2)]

        blk = stack.enter_context(nc.Block())
        p = Prog(nc, stack)
        emit_consts(p, stats)

        xnf = lambda k: slice(k * S, (k + 1) * S)
        KF = lambda k: slice(k * S, (k + 1) * S)
        QF = lambda k: slice(k * HALF, (k + 1) * HALF)

        p.dma("sync", ident[:, :], ident_i[:, :], writes=["ident"])
        p.dma("sync", perm[:, :], perm_i[:, :], writes=["perm"])
        p.dma("sync", repm[:, :], rep8_i[:, :], writes=["repm"])
        p.dma("sync", bbase_sb[:, :], bbase[:, :], writes=["bbase"])
        p.dma("sync", wstg[:, 0:128], rwt[:, :], writes=["wstg"])
        p.vop(lambda e: e.tensor_copy(rw_sb[:, :], wstg[:, 0:128]), reads=["wstg"], writes=["rw"])

        for rb in range(16):
            p.dma("sync", xt[:, :], x0p[rb * 128:(rb + 1) * 128, :], writes=["xt"])
            emit_ln(p, xt[:, :], "xt", xn_tm[:, :], "xn_tm", stats, rb % 2)
            for dtl in range(16):
                pbk = f"pb{dtl % 2}"
                pbt = pb[dtl % 2]
                p.op("tensor", lambda e, s=xn_tm[:, dtl * 128:(dtl + 1) * 128],
                     o=pbt[:, 0:128]: e.transpose(o, s, ident[:, :]),
                     reads=["xn_tm", "ident", pbk + "r"], writes=[pbk])
                p.vop(lambda e, i=pbt[:, 0:128],
                      o=A1[:, dtl * S + rb * 128: dtl * S + (rb + 1) * 128]:
                      e.tensor_copy(o, i),
                      reads=[pbk], writes=[f"xnf{dtl}", pbk + "r"])

        for k in range(16):
            for hh in range(2):
                p.dma("sync", wstg[:, :], wv[k][:, hh * 1024:(hh + 1) * 1024],
                      writes=["wstg"])
                p.vop(lambda e, o=A2[:, k * S + hh * 1024: k * S + (hh + 1) * 1024]:
                      e.tensor_copy(o, wstg[:, :]),
                      reads=["wstg"], writes=[f"wvb{k}"])
        for rb in range(16):
            vbuf = vt_st[rb % 2]
            vkey = f"vt{rb % 2}"
            for dvc in range(4):
                bank = pf[dvc % 2]
                bk = f"PF{dvc % 2}"
                for k in range(16):
                    p.mm(bank[:, :], A1[:, k * S + rb * 128: k * S + rb * 128 + 128],
                         A2[:, k * S + dvc * 512: k * S + dvc * 512 + 512],
                         start=(k == 0), stop=(k == 15),
                         reads=[f"xnf{k}", f"wvb{k}", bk + "r"],
                         writes=[bk] if k == 15 else [])
                p.vop(lambda e, i=bank[:, :], o=vbuf[:, dvc * 512:(dvc + 1) * 512]:
                      e.tensor_copy(o, i),
                      reads=[bk], writes=[vkey + "w", bk + "r"])
            p.dma("gpsimd", vhbm[rb], vbuf[:, :], reads=[vkey + "w"], writes=[f"vhbm{rb}", vkey], q="dP")

        p.merge_key("kfA2", [f"wvb{k}" for k in range(16)])

        def proj_rope(wdram, cosd, sind, ntc, dstA, dstCol, dstKey, tag):
            for m in range(16):
                for hh in range(2):
                    p.dma("sync", wstg[:, :].rearrange("p (k j) -> p k j", k=8),
                          wdram[m][hh * 8:(hh + 1) * 8].rearrange("k pp j -> pp k j"),
                          writes=["wstg"])
                    p.vop(lambda e, o=wbf[:, hh * 1024:(hh + 1) * 1024]:
                          e.tensor_copy(o, wstg[:, :]),
                          reads=["wstg"], writes=["wbf"])
                p.dma("sync", cs_t[:, 0:ntc * 512], cosd[m], writes=["cs"])
                p.dma("sync", sn_t[:, 0:ntc * 512], sind[m], writes=["sn"])
                for tc in range(ntc):
                    bank = pf[tc % 2]
                    bk = f"PF{tc % 2}"
                    for k in range(16):
                        p.mm(bank[:, :], wbf[:, k * 128:(k + 1) * 128],
                             A1[:, k * S + tc * 512: k * S + tc * 512 + 512],
                             start=(k == 0), stop=(k == 15),
                             reads=["wbf", f"xnf{k}", bk + "r"],
                             writes=[bk] if k == 15 else [])
                    p.act(qk_raw[:, :], bank[:, :], AF.Copy,
                          reads=[bk], writes=["qkr", bk + "r"])
                    p.op("tensor", lambda e, o=pf[3][:, :]:
                         e.matmul(o, perm[:, :], qk_raw[:, :], start=True, stop=True),
                         reads=["perm", "qkr", "PF3r"], writes=["PF3"])
                    p.vop(lambda e, c=cs_t[:, tc * 512:(tc + 1) * 512]:
                          e.tensor_mul(rope_t1[:, :], qk_raw[:, :], c),
                          reads=["qkr", "cs"], writes=["rt1"])
                    p.vop(lambda e, sl=sn_t[:, tc * 512:(tc + 1) * 512],
                          i=pf[3][:, :]: e.tensor_mul(qk_raw[:, :], i, sl),
                          reads=["PF3", "sn"], writes=["qkr", "PF3r"])
                    p.vop(lambda e, o=dstA[:, m * dstCol + tc * 512:
                                          m * dstCol + tc * 512 + 512]:
                          e.tensor_add(o, rope_t1[:, :], qk_raw[:, :]),
                          reads=["rt1", "qkr"], writes=[dstKey(m)])

        proj_rope(wk, cosk, sink, 4, A2, S, lambda m: f"kf{m}", "K")
        proj_rope(wq, cosq, sinq, 2, A3, HALF, lambda m: f"qf{m}", "Q")

        p.merge_key("attA1", [f"xnf{k}" for k in range(16)] + ["xn_tm"])
        PT = lambda kt: slice(kt * 512, kt * 512 + 512)
        ATT = lambda k: slice(16384 + k * HALF, 16384 + (k + 1) * HALF)
        PRB = slice(8192, 10240)
        PRB2 = slice(10240, 12288)
        MSK = slice(12288, 14336)

        first_att = ["attA1"]
        for qs in range(2):
            for qc4 in range(4):
                qc = qs * 4 + qc4
                p.dma("sync", A1[:, MSK], maskq[qc], writes=["mask"] + first_att)
                first_att = []
                for kc in range(4):
                    bank = pf[kc]
                    bk = f"PF{kc}"
                    for k in range(16):
                        p.mm(bank[:, :], A3[:, k * HALF + qc * 128: k * HALF + qc * 128 + 128],
                             A2[:, k * S + kc * 512: k * S + kc * 512 + 512],
                             start=(k == 0), stop=(k == 15),
                             reads=[f"qf{k}", f"kf{k}", bk + "r"],
                             writes=[bk] if k == 15 else [])
                    p.vop(lambda e, i=bank[:, :], o=colv[:, kc:kc + 1]:
                          e.reduce_max(o, i, axis=AX.X),
                          reads=[bk], writes=[f"mx{kc}"])
                p.vop(lambda e: e.tensor_max(colv[:, 4:5], colv[:, 0:1], colv[:, 1:2]),
                      reads=["mx0", "mx1"], writes=["mx01"])
                p.vop(lambda e: e.tensor_max(colv[:, 5:6], colv[:, 2:3], colv[:, 3:4]),
                      reads=["mx2", "mx3"], writes=["mx23"])
                p.vop(lambda e: e.tensor_max(colv[:, 6:7], colv[:, 4:5], colv[:, 5:6]),
                      reads=["mx01", "mx23"], writes=["mx"])
                p.vop(lambda e: e.tensor_scalar_mul(colv[:, 1:2], colv[:, 6:7], -SCALE_ATT),
                      reads=["mx"], writes=["nmx", "mx1"])
                for kc in range(4):
                    bank = pf[kc]
                    bk = f"PF{kc}"
                    p.act(A1[:, 8192 + kc * 512:8192 + (kc + 1) * 512], bank[:, :],
                          AF.Exp, bias=colv[:, 1:2], scale=stats["cst"][:, 2:3],
                          reads=[bk, "nmx", "cst"], writes=["prb", bk + "r"])
                p.vop(lambda e, i=A1[:, PRB], mk=A1[:, MSK], o=A1[:, PRB2]:
                      e.tensor_mul(o, i, mk),
                      reads=["prb", "mask"], writes=["prb2"])
                p.vop(lambda e, i=A1[:, PRB2]: e.reduce_sum(colv[:, 2:3], i, axis=AX.X),
                      reads=["prb2"], writes=["den", "mx2"])
                p.vop(lambda e: e.reciprocal(colv[:, 3:4], colv[:, 2:3]),
                      reads=["den"], writes=["rden", "mx3"])
                p.vop(lambda e, i=A1[:, PRB2], o=A1[:, PRB]:
                      e.tensor_scalar_mul(o, i, colv[:, 3:4]),
                      reads=["prb2", "rden", "prb"], writes=["prb"])
                for kt in range(16):
                    pbk = f"pb{kt % 2}"
                    pbt = pb[kt % 2]
                    p.op("tensor", lambda e, s=A1[:, 8192 + kt * 128:8192 + kt * 128 + 128],
                         o=pbt[:, 0:128]: e.transpose(o, s, ident[:, :]),
                         reads=["prb", "ident", pbk + "r"], writes=[pbk])
                    p.vop(lambda e, i=pbt[:, 0:128],
                          o=A1[:, kt * 512 + qc4 * 128: kt * 512 + qc4 * 128 + 128]:
                          e.tensor_copy(o, i),
                          reads=[pbk], writes=[f"pT{kt}", pbk + "r"])
            for dvg in range(4):
                for kt in range(16):
                    vbuf = vt_st[kt % 2]
                    vkey = f"vt{kt % 2}"
                    p.dma("sync", vbuf[:, 0:512], vhbm[kt][:, dvg * 512:(dvg + 1) * 512],
                          reads=[f"vhbm{kt}"], writes=[vkey])
                    for dvq in range(4):
                        bank = pf[2 + dvq]
                        bk = f"PF{2 + dvq}"
                        p.mm(bank[:, :], vbuf[:, dvq * 128:(dvq + 1) * 128],
                             A1[:, PT(kt)],
                             start=(kt == 0), stop=(kt == 15),
                             reads=[vkey, f"pT{kt}", bk + "r"],
                             writes=[bk] if kt == 15 else [])
                for dvq in range(4):
                    dvt = dvg * 4 + dvq
                    bank = pf[2 + dvq]
                    bk = f"PF{2 + dvq}"
                    p.vop(lambda e, i=bank[:, :],
                          o=A1[:, 16384 + dvt * HALF + qs * 512:16384 + dvt * HALF + qs * 512 + 512]:
                          e.tensor_copy(o, i),
                          reads=[bk], writes=[f"att{dvt}", bk + "r"])

        for tc in range(2):
            p.merge_key(f"otm{tc}", [f"pT{kt}" for kt in range(16)] if tc == 0 else [f"otmdone{0}"])
            for m in range(16):
                for hh in range(2):
                    p.dma("sync", wstg[:, :].rearrange("p (k j) -> p k j", k=8),
                          wo[m][hh * 8:(hh + 1) * 8].rearrange("k pp j -> pp k j"),
                          writes=["wstg"])
                    p.vop(lambda e, o=wbf[:, hh * 1024:(hh + 1) * 1024]:
                          e.tensor_copy(o, wstg[:, :]),
                          reads=["wstg"], writes=["wbf"])
                bank = pf[m % 2]
                bk = f"PF{m % 2}"
                for k in range(16):
                    p.mm(bank[:, :], wbf[:, k * 128:(k + 1) * 128],
                         A1[:, 16384 + k * HALF + tc * 512:16384 + k * HALF + tc * 512 + 512],
                         start=(k == 0), stop=(k == 15),
                         reads=["wbf", f"att{k}", bk + "r"],
                         writes=[bk] if k == 15 else [])
                p.act(of_sb[:, :], bank[:, :], AF.Copy, reads=[bk], writes=["of", bk + "r"])
                for sub in range(4):
                    pbk = f"pb{sub % 2}"
                    pbt = pb[sub % 2]
                    p.op("tensor", lambda e, s=of_sb[:, sub * 128:(sub + 1) * 128],
                         o=pbt[:, 0:128]: e.transpose(o, s, ident[:, :]),
                         reads=["of", "ident", pbk + "r"], writes=[pbk])
                    p.vop(lambda e, i=pbt[:, 0:128],
                          o=A1[:, sub * 2048 + m * 128: sub * 2048 + m * 128 + 128]:
                          e.tensor_copy(o, i),
                          reads=[pbk], writes=[f"otm{tc}", pbk + "r"])
            for sub in range(4):
                rb = tc * 4 + sub
                p.dma("sync", xt[:, :], x0p[rb * 128:(rb + 1) * 128, :], writes=["xt"])
                p.vop(lambda e, i=A1[:, sub * 2048:(sub + 1) * 2048]:
                      e.tensor_add(xt2[:, :], xt[:, :], i),
                      reads=["xt", f"otm{tc}"], writes=["xt2"])
                p.dma("gpsimd", x1hbm[rb], xt2[:, :], reads=["xt2"], writes=[f"x1h{rb}", "xt2r"], q="dP")
            p.deps[f"otmdone{0}"] = {"w": None, "r": list(p.deps[f"otm{tc}"]["r"])}

        p.merge_key("xnf1A3", [f"qf{m}" for m in range(16)])
        p.merge_key("ffnA2", [f"kf{m}" for m in range(16)])
        for rb in range(8):
            p.dma("sync", xt[:, :], x1hbm[rb], reads=[f"x1h{rb}", "xt2r"], writes=["xt"])
            emit_ln(p, xt[:, :], "xt", xn_tm[:, :], "xn_tm", stats, rb % 2)
            for dtl in range(16):
                pbk = f"pb{dtl % 2}"
                pbt = pb[dtl % 2]
                p.op("tensor", lambda e, s=xn_tm[:, dtl * 128:(dtl + 1) * 128],
                     o=pbt[:, 0:128]: e.transpose(o, s, ident[:, :]),
                     reads=["xn_tm", "ident", pbk + "r"], writes=[pbk])
                p.vop(lambda e, i=pbt[:, 0:128],
                      o=A3[:, dtl * HALF + rb * 128: dtl * HALF + rb * 128 + 128]:
                      e.tensor_copy(o, i),
                      reads=[pbk], writes=["xnf1A3", pbk + "r"])
            bank = pf[4]
            bk = "PF4"
            for k in range(16):
                p.mm(bank[:, 0:8], A3[:, k * HALF + rb * 128: k * HALF + rb * 128 + 128],
                     rw_sb[:, k * 8:(k + 1) * 8],
                     start=(k == 0), stop=(k == 15),
                     reads=["xnf1A3", "rw", bk + "r"],
                     writes=[bk] if k == 15 else [])
            p.vop(lambda e, i=bank[:, 0:8]: e.reduce_max(colv[:, 4:5], i, axis=AX.X),
                  reads=[bk], writes=["rmx"])
            p.vop(lambda e, i=bank[:, 0:8]: e.tensor_scalar(oh_tm[:, :], i, colv[:, 4:5], 2.0,
                                                            ALU.is_ge, ALU.mult),
                  reads=[bk, "rmx"], writes=["oh", bk + "r"])
            p.op("tensor", lambda e, o=pb[0][0:8, 0:128]:
                 e.transpose(o, oh_tm[:, :], ident[:, :]),
                 reads=["oh", "ident", "pb0r"], writes=["pb0"])
            p.vop(lambda e, i=pb[0][0:8, 0:128], o=ohT[:, rb * 128:(rb + 1) * 128]:
                  e.tensor_copy(o, i),
                  reads=["pb0"], writes=["ohT", "pb0r"])
        for rc in range(2):
            p.op("tensor", lambda e, o=pf[3][:, :],
                 r=ohT[:, rc * 512:(rc + 1) * 512]:
                 e.matmul(o, repm[:, :], r, start=True, stop=True),
                 reads=["repm", "ohT", "PF3r"], writes=["PF3"])
            p.vop(lambda e, i=pf[3][:, :], o=A2[:, rc * 512:rc * 512 + 512]:
                  e.tensor_copy(o, i),
                  reads=["PF3"], writes=["maskfm", "PF3r"] + (["ffnA2"] if rc == 0 else []))
        for hh in range(2):
            p.dma("sync", wstg[:, :].rearrange("p (k j) -> p k j", k=8),
                  at4[hh * 8:(hh + 1) * 8].rearrange("k pp j -> pp k j"),
                  writes=["wstg"])
            p.vop(lambda e, o=wbf[:, hh * 1024:(hh + 1) * 1024]:
                  e.tensor_copy(o, wstg[:, :]),
                  reads=["wstg"], writes=["wbf"])
        for rc in range(2):
            bank = pf[4]
            bk = "PF4"
            for k in range(16):
                p.mm(bank[:, :], wbf[:, k * 128:(k + 1) * 128],
                     A3[:, k * HALF + rc * 512: k * HALF + rc * 512 + 512],
                     start=(k == 0), stop=(k == 15),
                     reads=["wbf", "xnf1A3", bk + "r"],
                     writes=[bk] if k == 15 else [])
            p.vop(lambda e, i=bank[:, :], mk=A2[:, rc * 512:rc * 512 + 512],
                  o=A2[:, 1024 + rc * 512:1024 + rc * 512 + 512]:
                  e.tensor_mul(o, i, mk),
                  reads=[bk, "maskfm"], writes=["hm", bk + "r"])
        for tc in range(2):
            p.merge_key(f"otm2_{tc}", ["maskfm"] if tc == 0 else ["otm2done"])
            for m in range(16):
                for hh in range(2):
                    p.dma("sync", wstg[:, :].rearrange("p (k j) -> p k j", k=8),
                          wbase[m][hh * 8:(hh + 1) * 8].rearrange("k pp j -> pp k j"),
                          writes=["wstg"])
                    p.vop(lambda e, o=wbf[:, hh * 1024:(hh + 1) * 1024]:
                          e.tensor_copy(o, wstg[:, :]),
                          reads=["wstg"], writes=["wbf"])
                p.dma("sync", bw_f32[:, :], bwt[m], writes=["bwstg"])
                p.vop(lambda e: e.tensor_copy(bwbf[:, :], bw_f32[:, :]),
                      reads=["bwstg"], writes=["bwbf"])
                bank = pf[m % 2]
                bk = f"PF{m % 2}"
                for k in range(16):
                    p.mm(bank[:, :], wbf[:, k * 128:(k + 1) * 128],
                         A3[:, k * HALF + tc * 512: k * HALF + tc * 512 + 512],
                         start=(k == 0), stop=False,
                         reads=["wbf", "xnf1A3", bk + "r"], writes=[])
                p.mm(bank[:, :], bwbf[:, :],
                     A2[:, 1024 + tc * 512:1024 + tc * 512 + 512],
                     start=False, stop=True,
                     reads=["bwbf", "hm", bk + "r"], writes=[bk])
                p.act(of_sb[:, :], bank[:, :], AF.Identity, bias=bbase_sb[:, m:m + 1],
                      reads=[bk, "bbase"], writes=["of", bk + "r"])
                for sub in range(4):
                    pbk = f"pb{sub % 2}"
                    pbt = pb[sub % 2]
                    p.op("tensor", lambda e, s=of_sb[:, sub * 128:(sub + 1) * 128],
                         o=pbt[:, 0:128]: e.transpose(o, s, ident[:, :]),
                         reads=["of", "ident", pbk + "r"], writes=[pbk])
                    p.vop(lambda e, i=pbt[:, 0:128],
                          o=A2[:, 2048 + sub * 2048 + m * 128: 2048 + sub * 2048 + m * 128 + 128]:
                          e.tensor_copy(o, i),
                          reads=[pbk], writes=[f"otm2_{tc}", pbk + "r"])
            for sub in range(4):
                rb = tc * 4 + sub
                p.dma("sync", xt[:, :], x1hbm[rb], reads=[f"x1h{rb}"], writes=["xt"])
                p.vop(lambda e, i=A2[:, 2048 + sub * 2048: 2048 + (sub + 1) * 2048]:
                      e.tensor_add(xt2[:, :], xt[:, :], i),
                      reads=["xt", f"otm2_{tc}"], writes=["xt2"])
                p.dma("gpsimd", x2out[rb], xt2[:, :], reads=["xt2"], writes=["xt2r"], q="dP")
            p.deps["otm2done"] = {"w": None, "r": list(p.deps[f"otm2_{tc}"]["r"])}

        p.finalize(blk)
    return nc



def build_c():
    nc = bass.Bass("TRN2", target_bir_lowering=False, debug=False)
    dtin = lambda n, sh, ty: nc.dram_tensor(n, sh, ty, kind="ExternalInput").ap()
    x2f = dtin("x2f", [16, 128, D], F32)
    winx = dtin("winx", [16, 16, 128, 128], F32)
    winz = dtin("winz", [16, 16, 128, 128], F32)
    wxdt = dtin("wxdt", [16, 128, 128], F32)
    wxB = dtin("wxB", [16, 128, 16], F32)
    wxC = dtin("wxC", [16, 128, 16], F32)
    convw = dtin("convw", [128, 64], F32)
    convb = dtin("convb", [128, 16], F32)
    ident_i = dtin("ident_i", [128, 128], BF16)
    xc_o = nc.dram_tensor("xc_o", [16, 128, S], BF16, kind="ExternalOutput").ap()
    zs_o = nc.dram_tensor("zs_o", [16, 128, S], BF16, kind="ExternalOutput").ap()
    xdt_o = nc.dram_tensor("xdt_o", [128, S], F32, kind="ExternalOutput").ap()
    xB_o = nc.dram_tensor("xB_o", [16, S], F32, kind="ExternalOutput").ap()
    xC_o = nc.dram_tensor("xC_o", [16, S], F32, kind="ExternalOutput").ap()

    with ExitStack() as stack:
        sb = lambda n, sh, ty: stack.enter_context(nc.sbuf_tensor(n, sh, ty))
        psa = lambda n, sh, ty: stack.enter_context(nc.psum_tensor(n, sh, ty))
        A1 = sb("A1", [128, 32768], BF16)
        xt = sb("xt", [128, D], F32)
        xn_tm = sb("xn_tm", [128, D], BF16)
        stats = ln_stats_tensors(nc, stack)
        wstg = sb("wstg", [128, 1024], F32)
        wbf = sb("wbf", [128, 2048], BF16)
        wsm = sb("wsm", [128, 160], BF16)
        wsm_f = sb("wsm_f", [128, 160], F32)
        xraw = sb("xraw", [128, S], BF16)
        cnvA = sb("cnvA", [128, S], BF16)
        cnvB = sb("cnvB", [128, S], BF16)
        xct = sb("xct", [128, S], BF16)
        zst = sb("zst", [128, S], BF16)
        cw_sb = sb("cw_sb", [128, 64], F32)
        cb_sb = sb("cb_sb", [128, 16], F32)
        xdt_sb = sb("xdt_sb", [128, S], F32)
        xB_sb = sb("xB_sb", [16, S], F32)
        xC_sb = sb("xC_sb", [16, S], F32)
        ident = sb("ident", [128, 128], BF16)
        pf = [psa(f"pf{i}", [128, 512], F32) for i in range(6)]
        pb = [psa(f"pb{i}", [128, 1024], BF16) for i in range(2)]

        blk = stack.enter_context(nc.Block())
        p = Prog(nc, stack)
        emit_consts(p, stats)

        p.dma("sync", ident[:, :], ident_i[:, :], writes=["ident"])
        p.dma("sync", cw_sb[:, :], convw[:, :], writes=["cw"])
        p.dma("sync", cb_sb[:, :], convb[:, :], writes=["cb"])
        p.vop(lambda e: e.memset(xdt_sb[:, :], 0.0), writes=["xdt"])
        p.vop(lambda e: e.memset(xB_sb[:, :], 0.0), writes=["xB"])
        p.vop(lambda e: e.memset(xC_sb[:, :], 0.0), writes=["xC"])

        for rb in range(16):
            p.dma("sync", xt[:, :], x2f[rb], writes=["xt"])
            emit_ln(p, xt[:, :], "xt", xn_tm[:, :], "xn_tm", stats, rb % 2)
            for dtl in range(16):
                pbk = f"pb{dtl % 2}"
                pbt = pb[dtl % 2]
                p.op("tensor", lambda e, s=xn_tm[:, dtl * 128:(dtl + 1) * 128],
                     o=pbt[:, 0:128]: e.transpose(o, s, ident[:, :]),
                     reads=["xn_tm", "ident", pbk + "r"], writes=[pbk])
                p.vop(lambda e, i=pbt[:, 0:128],
                      o=A1[:, dtl * S + rb * 128: dtl * S + (rb + 1) * 128]:
                      e.tensor_copy(o, i),
                      reads=[pbk], writes=[f"xnf{dtl}", pbk + "r"])

        def proj(wdram, m, dstT, dstKey, func, biascol, bias_reads):
            for hh in range(2):
                p.dma("sync", wstg[:, :].rearrange("p (k j) -> p k j", k=8),
                      wdram[m][hh * 8:(hh + 1) * 8].rearrange("k pp j -> pp k j"),
                      writes=["wstg"])
                p.vop(lambda e, o=wbf[:, hh * 1024:(hh + 1) * 1024]:
                      e.tensor_copy(o, wstg[:, :]),
                      reads=["wstg"], writes=["wbf"])
            for lch in range(4):
                bank = pf[lch % 2]
                bk = f"PF{lch % 2}"
                for k in range(16):
                    p.mm(bank[:, :], wbf[:, k * 128:(k + 1) * 128],
                         A1[:, k * S + lch * 512: k * S + lch * 512 + 512],
                         start=(k == 0), stop=(k == 15),
                         reads=["wbf", f"xnf{k}", bk + "r"],
                         writes=[bk] if k == 15 else [])
                p.act(dstT[:, lch * 512:(lch + 1) * 512], bank[:, :], func,
                      bias=biascol if biascol is not None else 0.0,
                      reads=[bk] + bias_reads, writes=[dstKey, bk + "r"])

        for dxt in range(16):
            proj(winx, dxt, xraw, "xraw", AF.Copy, None, [])
            wc = lambda j: cw_sb[:, dxt * 4 + j:dxt * 4 + j + 1]
            p.vop(lambda e, w=wc(3): e.tensor_scalar_mul(cnvA[:, :], xraw[:, :], w),
                  reads=["xraw", "cw"], writes=["cA"])
            p.vop(lambda e, w=wc(2): e.scalar_tensor_tensor(cnvB[:, 1:S], xraw[:, 0:S - 1], w,
                                                            cnvA[:, 1:S], ALU.mult, ALU.add),
                  reads=["xraw", "cA", "cw"], writes=["cB"])
            p.vop(lambda e: e.tensor_copy(cnvB[:, 0:1], cnvA[:, 0:1]),
                  reads=["cA"], writes=["cB"])
            p.vop(lambda e, w=wc(1): e.scalar_tensor_tensor(cnvA[:, 2:S], xraw[:, 0:S - 2], w,
                                                            cnvB[:, 2:S], ALU.mult, ALU.add),
                  reads=["xraw", "cB", "cw", "cA"], writes=["cA"])
            p.vop(lambda e: e.tensor_copy(cnvA[:, 0:2], cnvB[:, 0:2]),
                  reads=["cB"], writes=["cA"])
            p.vop(lambda e, w=wc(0): e.scalar_tensor_tensor(cnvB[:, 3:S], xraw[:, 0:S - 3], w,
                                                            cnvA[:, 3:S], ALU.mult, ALU.add),
                  reads=["xraw", "cA", "cw", "cB"], writes=["cB"])
            p.vop(lambda e: e.tensor_copy(cnvB[:, 0:3], cnvA[:, 0:3]),
                  reads=["cA"], writes=["cB"])
            p.act(xct[:, :], cnvB[:, :], AF.Silu, bias=cb_sb[:, dxt:dxt + 1],
                  reads=["cB", "cb"], writes=["xct"])
            p.dma("gpsimd", xc_o[dxt], xct[:, :], reads=["xct"], writes=[f"xc{dxt}"], q="dP")
            proj(winz, dxt, zst, "zst", AF.Silu, None, [])
            p.dma("gpsimd", zs_o[dxt], zst[:, :], reads=["zst"], writes=[f"zs{dxt}"], q="dP")
            p.dma("sync", wsm_f[:, 0:128], wxdt[dxt], writes=["wsmf"])
            p.dma("sync", wsm_f[:, 128:144], wxB[dxt], writes=["wsmf2"])
            p.dma("sync", wsm_f[:, 144:160], wxC[dxt], writes=["wsmf3"])
            p.vop(lambda e: e.tensor_copy(wsm[:, :], wsm_f[:, :]),
                  reads=["wsmf", "wsmf2", "wsmf3"], writes=["wsm"])
            for lch in range(4):
                p.mm(pf[2][:, :], wsm[:, 0:128], xct[:, lch * 512:(lch + 1) * 512],
                     start=True, stop=True, reads=["wsm", "xct", "PF2r"], writes=["PF2"])
                p.vop(lambda e, o=xdt_sb[:, lch * 512:(lch + 1) * 512], i=pf[2][:, :]:
                      e.tensor_add(o, o, i),
                      reads=["PF2", "xdt"], writes=["xdt", "PF2r"])
                p.mm(pf[3][0:16, :], wsm[:, 128:144], xct[:, lch * 512:(lch + 1) * 512],
                     start=True, stop=True, reads=["wsm", "xct", "PF3r"], writes=["PF3"])
                p.vop(lambda e, o=xB_sb[:, lch * 512:(lch + 1) * 512], i=pf[3][0:16, :]:
                      e.tensor_add(o, o, i),
                      reads=["PF3", "xB"], writes=["xB", "PF3r"])
                p.mm(pf[4][0:16, :], wsm[:, 144:160], xct[:, lch * 512:(lch + 1) * 512],
                     start=True, stop=True, reads=["wsm", "xct", "PF4r"], writes=["PF4"])
                p.vop(lambda e, o=xC_sb[:, lch * 512:(lch + 1) * 512], i=pf[4][0:16, :]:
                      e.tensor_add(o, o, i),
                      reads=["PF4", "xC"], writes=["xC", "PF4r"])

        p.dma("gpsimd", xdt_o[:, :], xdt_sb[:, :], reads=["xdt"], writes=["xdto"], q="dA")
        p.dma("gpsimd", xB_o[:, :], xB_sb[:, :], reads=["xB"], writes=["xBo"], q="dA")
        p.dma("gpsimd", xC_o[:, :], xC_sb[:, :], reads=["xC"], writes=["xCo"], q="dA")
        p.finalize(blk)
    return nc



def build_d():
    nc = bass.Bass("TRN2", target_bir_lowering=False, debug=False)
    dtin = lambda n, sh, ty: nc.dram_tensor(n, sh, ty, kind="ExternalInput").ap()
    xc_i = dtin("xc_i", [16, 128, S], BF16)
    zs_i = dtin("zs_i", [16, 128, S], BF16)
    xdt0 = dtin("xdt0", [128, S], F32)
    xdt1 = dtin("xdt1", [128, S], F32)
    xB0 = dtin("xB0", [16, S], F32)
    xB1 = dtin("xB1", [16, S], F32)
    xC0 = dtin("xC0", [16, S], F32)
    xC1 = dtin("xC1", [16, S], F32)
    wdt = dtin("wdt", [128, S], F32)
    bdt = dtin("bdt", [128, 16], F32)
    alog = dtin("alog", [128, 256], F32)
    dsk = dtin("dsk", [128, 16], F32)
    ones_i = dtin("ones_i", [1, 128], BF16)
    yg_o = nc.dram_tensor("yg_o", [16, 128, S], BF16, kind="ExternalOutput").ap()
    bc_h = nc.dram_tensor("bc_h", [16, S], BF16, kind="Internal").ap()
    cc_h = nc.dram_tensor("cc_h", [16, S], BF16, kind="Internal").ap()

    LC = 1024

    with ExitStack() as stack:
        sb = lambda n, sh, ty: stack.enter_context(nc.sbuf_tensor(n, sh, ty))
        psa = lambda n, sh, ty: stack.enter_context(nc.psum_tensor(n, sh, ty))
        BB = sb("BB", [128, 16 * LC], BF16)
        CB = sb("CB", [128, 16 * LC], BF16)
        stats = ln_stats_tensors(nc, stack)
        dtc = sb("dtc", [128, S], F32)
        dt_bf = sb("dt_bf", [128, S], BF16)
        bcc = sb("bcc", [16, S], F32)
        ccc = sb("ccc", [16, S], F32)
        bcb = sb("bcb", [16, S], BF16)
        ccb = sb("ccb", [16, S], BF16)
        bfl = sb("bfl", [1, LC], BF16)
        wdt_bf = sb("wdt_bf", [128, S], BF16)
        wstg = sb("wstg", [128, 1024], F32)
        acol = sb("acol", [128, 256], F32)
        bdt_sb = sb("bdt_sb", [128, 16], F32)
        dsk_sb = sb("dsk_sb", [128, 16], F32)
        delta = sb("delta", [128, LC], F32)
        absx = sb("absx", [128, 512], F32)
        relux = sb("relux", [128, 512], F32)
        expx = sb("expx", [128, 512], F32)
        xct = sb("xct", [128, LC], BF16)
        zst = sb("zst", [128, LC], BF16)
        dxb = sb("dxb", [128, LC], BF16)
        dA = [sb(f"dA{i}", [128, LC], BF16) for i in range(2)]
        ub = [sb(f"ub{i}", [128, LC], BF16) for i in range(2)]
        sn = [sb(f"sn{i}", [128, LC], BF16) for i in range(2)]
        csb = sb("csb", [128, LC], BF16)
        yacc = sb("yacc", [128, LC], BF16)
        y2 = sb("y2", [128, LC], BF16)
        stc = sb("stc", [128, 256], F32)
        ones1 = sb("ones1", [1, 128], BF16)
        pf = [psa(f"pf{i}", [128, 512], F32) for i in range(6)]

        blk = stack.enter_context(nc.Block())
        p = Prog(nc, stack)
        emit_consts(p, stats)

        p.dma("sync", ones1[:, :], ones_i[:, :], writes=["ones"])
        p.dma("sync", bdt_sb[:, :], bdt[:, :], writes=["bdt"])
        p.dma("sync", dsk_sb[:, :], dsk[:, :], writes=["dsk"])
        p.dma("sync", acol[:, :], alog[:, :], writes=["acol"])
        p.act(acol[:, :], acol[:, :], AF.Exp, reads=["acol"], writes=["acol"])
        p.vop(lambda e: e.tensor_scalar_mul(acol[:, :], acol[:, :], -1.0),
              reads=["acol"], writes=["acol"])
        p.dma("sync", dtc[:, :], xdt0[:, :], writes=["dtc"])
        p.dma("sync", wstg[:, :], xdt1[:, 0:1024], writes=["wstg"])
        p.vop(lambda e: e.tensor_add(dtc[:, 0:1024], dtc[:, 0:1024], wstg[:, :]),
              reads=["wstg", "dtc"], writes=["dtc"])
        p.dma("sync", wstg[:, :], xdt1[:, 1024:2048], writes=["wstg"])
        p.vop(lambda e: e.tensor_add(dtc[:, 1024:2048], dtc[:, 1024:2048], wstg[:, :]),
              reads=["wstg", "dtc"], writes=["dtc"])
        p.vop(lambda e: e.tensor_copy(dt_bf[:, :], dtc[:, :]), reads=["dtc"], writes=["dtbf"])
        p.dma("sync", bcc[:, :], xB0[:, :], writes=["bcc"])
        p.dma("sync", ccc[:, :], xC0[:, :], writes=["ccc"])
        xB1_st = sb("xB1_st", [16, S], F32)
        p.dma("sync", xB1_st[:, :], xB1[:, :], writes=["xb1s"])
        p.vop(lambda e: e.tensor_add(bcc[:, :], bcc[:, :], xB1_st[:, :]),
              reads=["xb1s", "bcc"], writes=["bcc"])
        p.dma("sync", xB1_st[:, :], xC1[:, :], writes=["xb1s"])
        p.vop(lambda e: e.tensor_add(ccc[:, :], ccc[:, :], xB1_st[:, :]),
              reads=["xb1s", "ccc"], writes=["ccc"])
        p.vop(lambda e: e.tensor_copy(bcb[:, :], bcc[:, :]), reads=["bcc"], writes=["bcb"])
        p.vop(lambda e: e.tensor_copy(ccb[:, :], ccc[:, :]), reads=["ccc"], writes=["ccb"])
        p.dma("gpsimd", bc_h[:, :], bcb[:, :], reads=["bcb"], writes=["bch"], q="dP")
        p.dma("gpsimd", cc_h[:, :], ccb[:, :], reads=["ccb"], writes=["cch"], q="dP")
        p.dma("sync", wstg[:, :], wdt[:, 0:1024], writes=["wstg"])
        p.vop(lambda e: e.tensor_copy(wdt_bf[:, 0:1024], wstg[:, :]),
              reads=["wstg"], writes=["wdtb"])
        p.dma("sync", wstg[:, :], wdt[:, 1024:2048], writes=["wstg"])
        p.vop(lambda e: e.tensor_copy(wdt_bf[:, 1024:2048], wstg[:, :]),
              reads=["wstg"], writes=["wdtb"])

        for lc in range(2):
            for n in range(16):
                p.dma("sync", bfl[0:1, :], bc_h[n:n + 1, lc * LC:(lc + 1) * LC],
                      reads=["bch"], writes=["bfl"])
                for c2 in range(2):
                    p.mm(pf[5][:, :], ones1[0:1, :], bfl[0:1, c2 * 512:(c2 + 1) * 512],
                         start=True, stop=True, reads=["ones", "bfl", "PF5r"], writes=["PF5"])
                    p.vop(lambda e, o=BB[:, n * LC + c2 * 512: n * LC + c2 * 512 + 512],
                          i=pf[5][:, :]: e.tensor_copy(o, i),
                          reads=["PF5"], writes=[f"BB{n}", "PF5r"])
                p.dma("sync", bfl[0:1, :], cc_h[n:n + 1, lc * LC:(lc + 1) * LC],
                      reads=["cch"], writes=["bfl"])
                for c2 in range(2):
                    p.mm(pf[5][:, :], ones1[0:1, :], bfl[0:1, c2 * 512:(c2 + 1) * 512],
                         start=True, stop=True, reads=["ones", "bfl", "PF5r"], writes=["PF5"])
                    p.vop(lambda e, o=CB[:, n * LC + c2 * 512: n * LC + c2 * 512 + 512],
                          i=pf[5][:, :]: e.tensor_copy(o, i),
                          reads=["PF5"], writes=[f"CB{n}", "PF5r"])
            for dxt in range(16):
                p.dma("sync", xct[:, :], xc_i[dxt][:, lc * LC:(lc + 1) * LC], writes=["xct"])
                p.dma("sync", zst[:, :], zs_i[dxt][:, lc * LC:(lc + 1) * LC], writes=["zst"])
                for c2 in range(2):
                    bank = pf[c2 % 2]
                    bk = f"PF{c2 % 2}"
                    p.mm(bank[:, :], wdt_bf[:, dxt * 128:(dxt + 1) * 128],
                         dt_bf[:, lc * LC + c2 * 512: lc * LC + c2 * 512 + 512],
                         start=True, stop=True,
                         reads=["wdtb", "dtbf", bk + "r"], writes=[bk])
                    bcol = bdt_sb[:, dxt:dxt + 1]
                    p.act(absx[:, :], bank[:, :], AF.Abs, bias=bcol,
                          reads=[bk, "bdt"], writes=["absx"])
                    p.vop(lambda e, i=bank[:, :], bc=bcol: e.tensor_scalar(
                          relux[:, :], i, bc, 0.0, ALU.add, ALU.max),
                          reads=[bk, "bdt", "absx"], writes=["relux", bk + "r"])
                    p.act(expx[:, :], absx[:, :], AF.Exp, scale=stats["cst"][:, 3:4],
                          reads=["absx", "cst"], writes=["expx"])
                    p.act(expx[:, :], expx[:, :], AF.Ln, bias=1.0,
                          reads=["expx"], writes=["expx"])
                    p.vop(lambda e, o=delta[:, c2 * 512:(c2 + 1) * 512]:
                          e.tensor_add(o, relux[:, :], expx[:, :]),
                          reads=["relux", "expx"], writes=["delta"])
                p.vop(lambda e: e.tensor_mul(dxb[:, :], delta[:, :], xct[:, :]),
                      reads=["delta", "xct"], writes=["dxb"])
                for n in range(16):
                    da = dA[n % 2]
                    dak = f"dA{n % 2}"
                    u = ub[n % 2]
                    uk = f"ub{n % 2}"
                    s_ = sn[n % 2]
                    sk_ = f"sn{n % 2}"
                    p.act(da[:, :], delta[:, :], AF.Exp,
                          scale=acol[:, dxt * 16 + n:dxt * 16 + n + 1],
                          reads=["delta", "acol", dak], writes=[dak])
                    p.vop(lambda e, bb=BB[:, n * LC:(n + 1) * LC], u_=u:
                          e.tensor_mul(u_[:, :], dxb[:, :], bb),
                          reads=["dxb", f"BB{n}", uk], writes=[uk])
                    ini = 0.0 if lc == 0 else stc[:, dxt * 16 + n:dxt * 16 + n + 1]
                    p.vop(lambda e, ini=ini, s=s_, d=da, u_=u: e.tensor_tensor_scan(
                          s[:, :], d[:, :], u_[:, :], ini, ALU.mult, ALU.add),
                          reads=[dak, uk, "stc", sk_], writes=[sk_])
                    p.vop(lambda e, o=stc[:, dxt * 16 + n:dxt * 16 + n + 1], s=s_:
                          e.tensor_copy(o, s[:, LC - 1:LC]),
                          reads=[sk_], writes=["stc"])
                    p.vop(lambda e, cb=CB[:, n * LC:(n + 1) * LC], s=s_:
                          e.tensor_mul(csb[:, :], s[:, :], cb),
                          reads=[sk_, f"CB{n}", "csb"], writes=["csb"])
                    if n == 0:
                        p.vop(lambda e: e.tensor_copy(yacc[:, :], csb[:, :]),
                              reads=["csb"], writes=["yacc"])
                    else:
                        p.vop(lambda e: e.tensor_add(yacc[:, :], yacc[:, :], csb[:, :]),
                              reads=["csb", "yacc"], writes=["yacc"])
                p.vop(lambda e, dc=dsk_sb[:, dxt:dxt + 1]: e.scalar_tensor_tensor(
                      y2[:, :], xct[:, :], dc, yacc[:, :],
                      ALU.mult, ALU.add),
                      reads=["xct", "dsk", "yacc", "y2"], writes=["y2"])
                p.vop(lambda e: e.tensor_mul(y2[:, :], y2[:, :], zst[:, :]),
                      reads=["zst", "y2"], writes=["y2"])
                p.dma("gpsimd", yg_o[dxt][:, lc * LC:(lc + 1) * LC], y2[:, :],
                      reads=["y2"], writes=[f"yg{dxt}_{lc}"], q="dA")
        p.finalize(blk)
    return nc



def build_e():
    nc = bass.Bass("TRN2", target_bir_lowering=False, debug=False)
    dtin = lambda n, sh, ty: nc.dram_tensor(n, sh, ty, kind="ExternalInput").ap()
    x2own = dtin("x2own", [8, 128, D], F32)
    ygf = dtin("ygf", [32, 128, HALF], BF16)
    wout = dtin("wout", [16, 32, 128, 128], F32)
    bout = dtin("bout", [128, 16], F32)
    w1 = dtin("w1", [32, 16, 128, 128], F32)
    b1 = dtin("b1", [128, 32], F32)
    w2 = dtin("w2", [32, 32, 128, 128], F32)
    b2 = dtin("b2", [128, 32], F32)
    w3 = dtin("w3", [16, 32, 128, 128], F32)
    b3 = dtin("b3", [128, 16], F32)
    ident_i = dtin("ident_i", [128, 128], BF16)
    x4out = nc.dram_tensor("x4out", [8, 128, D], F32, kind="ExternalOutput").ap()
    x3hbm = nc.dram_tensor("x3hbm", [8, 128, D], F32, kind="Internal").ap()

    with ExitStack() as stack:
        sb = lambda n, sh, ty: stack.enter_context(nc.sbuf_tensor(n, sh, ty))
        psa = lambda n, sh, ty: stack.enter_context(nc.psum_tensor(n, sh, ty))
        AY = sb("AY", [128, 32768], BF16)
        AH = sb("AH", [128, 32768], BF16)
        A3 = sb("A3", [128, 16384], BF16)
        xt = sb("xt", [128, D], F32)
        xt2 = sb("xt2", [128, D], F32)
        xn_tm = sb("xn_tm", [128, D], BF16)
        stats = ln_stats_tensors(nc, stack)
        wstg = sb("wstg", [128, 1024], F32)
        wbf = sb("wbf", [128, 4096], BF16)
        of_sb = sb("of_sb", [128, 512], BF16)
        bcols = sb("bcols", [128, 96], F32)
        ident = sb("ident", [128, 128], BF16)
        otm = AH[:, 0:8192]
        pf = [psa(f"pf{i}", [128, 512], F32) for i in range(6)]
        pb = [psa(f"pb{i}", [128, 1024], BF16) for i in range(2)]

        blk = stack.enter_context(nc.Block())
        p = Prog(nc, stack)
        emit_consts(p, stats)

        p.dma("sync", ident[:, :], ident_i[:, :], writes=["ident"])
        p.dma("sync", bcols[:, 0:16], bout[:, :], writes=["bc"])
        p.dma("sync", bcols[:, 16:48], b1[:, :], writes=["bc1"])
        p.dma("sync", bcols[:, 48:80], b2[:, :], writes=["bc2"])
        p.dma("sync", bcols[:, 80:96], b3[:, :], writes=["bc3"])
        for k in range(32):
            p.dma("sync", AY[:, k * 1024:(k + 1) * 1024], ygf[k], writes=[f"yg{k}"])

        def stream_tiles(wdram, m, nk, key="wbf"):
            for hh in range(nk // 8):
                p.dma("sync", wstg[:, :].rearrange("p (k j) -> p k j", k=8),
                      wdram[m][hh * 8:(hh + 1) * 8].rearrange("k pp j -> pp k j"),
                      writes=["wstg"])
                p.vop(lambda e, o=wbf[:, hh * 1024:(hh + 1) * 1024]:
                      e.tensor_copy(o, wstg[:, :]),
                      reads=["wstg"], writes=[key])

        def mlp_proj(wdram, nk, nm, src_fn, src_keys, dst_fn, dst_key_fn, func, bcol_fn,
                     bias_reads, ntc=2):
            for m in range(nm):
                stream_tiles(wdram, m, nk)
                for tc in range(ntc):
                    bank = pf[tc % 2]
                    bk = f"PF{tc % 2}"
                    for k in range(nk):
                        p.mm(bank[:, :], wbf[:, k * 128:(k + 1) * 128],
                             src_fn(k, tc),
                             start=(k == 0), stop=(k == nk - 1),
                             reads=["wbf", src_keys(k), bk + "r"],
                             writes=[bk] if k == nk - 1 else [])
                    p.act(dst_fn(m, tc), bank[:, :], func, bias=bcol_fn(m),
                          reads=[bk] + bias_reads, writes=[dst_key_fn(m), bk + "r"])

        for tc in range(2):
            for m in range(16):
                stream_tiles(wout, m, 32)
                bank = pf[m % 2]
                bk = f"PF{m % 2}"
                for k in range(32):
                    p.mm(bank[:, :], wbf[:, k * 128:(k + 1) * 128],
                         AY[:, k * HALF + tc * 512: k * HALF + tc * 512 + 512],
                         start=(k == 0), stop=(k == 31),
                         reads=["wbf", f"yg{k}", bk + "r"],
                         writes=[bk] if k == 31 else [])
                p.act(of_sb[:, :], bank[:, :], AF.Identity, bias=bcols[:, m:m + 1],
                      reads=[bk, "bc"], writes=["of", bk + "r"])
                for sub in range(4):
                    pbk = f"pb{sub % 2}"
                    pbt = pb[sub % 2]
                    p.op("tensor", lambda e, s=of_sb[:, sub * 128:(sub + 1) * 128],
                         o=pbt[:, 0:128]: e.transpose(o, s, ident[:, :]),
                         reads=["of", "ident", pbk + "r"], writes=[pbk])
                    p.vop(lambda e, i=pbt[:, 0:128],
                          o=otm[:, sub * 2048 + m * 128: sub * 2048 + m * 128 + 128]:
                          e.tensor_copy(o, i),
                          reads=[pbk], writes=["otm", pbk + "r"])
            for sub in range(4):
                rb = tc * 4 + sub
                p.dma("sync", xt[:, :], x2own[rb], writes=["xt"])
                p.vop(lambda e, i=otm[:, sub * 2048:(sub + 1) * 2048]:
                      e.tensor_add(xt2[:, :], xt[:, :], i),
                      reads=["xt", "otm", "xt2r"], writes=["xt2"])
                p.dma("gpsimd", x3hbm[rb], xt2[:, :], reads=["xt2"], writes=[f"x3h{rb}", "xt2r"], q="dP")
                emit_ln(p, xt2[:, :], "xt2", xn_tm[:, :], "xn_tm", stats, rb % 2)
                for dtl in range(16):
                    pbk = f"pb{dtl % 2}"
                    pbt = pb[dtl % 2]
                    p.op("tensor", lambda e, s=xn_tm[:, dtl * 128:(dtl + 1) * 128],
                         o=pbt[:, 0:128]: e.transpose(o, s, ident[:, :]),
                         reads=["xn_tm", "ident", pbk + "r"], writes=[pbk])
                    p.vop(lambda e, i=pbt[:, 0:128],
                          o=A3[:, dtl * HALF + rb * 128: dtl * HALF + rb * 128 + 128]:
                          e.tensor_copy(o, i),
                          reads=[pbk], writes=[f"x3f{dtl}", pbk + "r"])

        mlp_proj(w1, 16, 32,
                 lambda k, tc: A3[:, k * HALF + tc * 512: k * HALF + tc * 512 + 512],
                 lambda k: f"x3f{k}",
                 lambda m, tc: AH[:, m * HALF + tc * 512: m * HALF + tc * 512 + 512],
                 lambda m: f"h1_{m}", AF.Relu, lambda m: bcols[:, 16 + m:17 + m], ["bc1"])
        p.merge_key("h2A", [f"yg{k}" for k in range(32)])
        for m in range(32):
            stream_tiles(w2, m, 32)
            for tc in range(2):
                bank = pf[tc % 2]
                bk = f"PF{tc % 2}"
                for k in range(32):
                    p.mm(bank[:, :], wbf[:, k * 128:(k + 1) * 128],
                         AH[:, k * HALF + tc * 512: k * HALF + tc * 512 + 512],
                         start=(k == 0), stop=(k == 31),
                         reads=["wbf", f"h1_{k}", bk + "r"],
                         writes=[bk] if k == 31 else [])
                p.act(AY[:, m * HALF + tc * 512: m * HALF + tc * 512 + 512],
                      bank[:, :], AF.Relu, bias=bcols[:, 48 + m:49 + m],
                      reads=[bk, "bc2"], writes=[f"h2_{m}", bk + "r"]
                      + (["h2A"] if False else []))
        p.merge_key("otm3", [f"h1_{m}" for m in range(32)] + ["otm"])
        for tc in range(2):
            for m in range(16):
                stream_tiles(w3, m, 32)
                bank = pf[m % 2]
                bk = f"PF{m % 2}"
                for k in range(32):
                    p.mm(bank[:, :], wbf[:, k * 128:(k + 1) * 128],
                         AY[:, k * HALF + tc * 512: k * HALF + tc * 512 + 512],
                         start=(k == 0), stop=(k == 31),
                         reads=["wbf", f"h2_{k}", bk + "r"],
                         writes=[bk] if k == 31 else [])
                p.act(of_sb[:, :], bank[:, :], AF.Identity, bias=bcols[:, 80 + m:81 + m],
                      reads=[bk, "bc3"], writes=["of", bk + "r"])
                for sub in range(4):
                    pbk = f"pb{sub % 2}"
                    pbt = pb[sub % 2]
                    p.op("tensor", lambda e, s=of_sb[:, sub * 128:(sub + 1) * 128],
                         o=pbt[:, 0:128]: e.transpose(o, s, ident[:, :]),
                         reads=["of", "ident", pbk + "r"], writes=[pbk])
                    p.vop(lambda e, i=pbt[:, 0:128],
                          o=AH[:, sub * 2048 + m * 128: sub * 2048 + m * 128 + 128]:
                          e.tensor_copy(o, i),
                          reads=[pbk], writes=["otm3", pbk + "r"])
            for sub in range(4):
                rb = tc * 4 + sub
                p.dma("sync", xt[:, :], x3hbm[rb], reads=[f"x3h{rb}"], writes=["xt"])
                p.vop(lambda e, i=AH[:, sub * 2048:(sub + 1) * 2048]:
                      e.tensor_add(xt2[:, :], xt[:, :], i),
                      reads=["xt", "otm3", "xt2r"], writes=["xt2"])
                p.dma("gpsimd", x4out[rb], xt2[:, :], reads=["xt2"], writes=["xt2r"], q="dA")
        p.finalize(blk)
    return nc



def build_fused(groups):
    nc = bass.Bass("TRN2", target_bir_lowering=False, debug=False)
    dtin = lambda n, sh, ty: nc.dram_tensor(n, sh, ty, kind="ExternalInput").ap()
    x0p = dtin("x0p", [S, D], F32)
    wq = dtin("wq", [16, 16, 128, 128], F32)
    wk = dtin("wk", [16, 16, 128, 128], F32)
    wv = dtin("wv", [16, 128, D], F32)
    wo = dtin("wo", [16, 16, 128, 128], F32)
    wbase = dtin("wbase", [16, 16, 128, 128], F32)
    bbase = dtin("bbase", [128, 16], F32)
    rwt = dtin("rwt", [128, 128], F32)
    at4 = dtin("at4", [16, 128, 128], F32)
    bwt = dtin("bwt", [16, 128, 128], F32)
    cosq = dtin("cosq", [16, 128, HALF], BF16)
    sinq = dtin("sinq", [16, 128, HALF], BF16)
    cosk = dtin("cosk", [16, 128, S], BF16)
    sink = dtin("sink", [16, 128, S], BF16)
    maskq = dtin("maskq", [8, 128, S], BF16)
    ident_i = dtin("ident_i", [128, 128], BF16)
    perm_i = dtin("perm_i", [128, 128], BF16)
    rep8_i = dtin("rep8_i", [8, 128], BF16)
    winx = dtin("winx", [16, 16, 128, 128], F32)
    winz = dtin("winz", [16, 16, 128, 128], F32)
    wxdt = dtin("wxdt", [16, 128, 128], F32)
    wxB = dtin("wxB", [16, 128, 16], F32)
    wxC = dtin("wxC", [16, 128, 16], F32)
    convw = dtin("convw", [128, 64], F32)
    convb = dtin("convb", [128, 16], F32)
    wdt = dtin("wdt", [128, S], F32)
    bdt = dtin("bdt", [128, 16], F32)
    alog = dtin("alog", [128, 256], F32)
    dsk = dtin("dsk", [128, 16], F32)
    wout = dtin("wout", [16, 16, 128, 128], F32)
    bout = dtin("bout", [128, 16], F32)
    w1 = dtin("w1", [32, 16, 128, 128], F32)
    b1 = dtin("b1", [128, 32], F32)
    w2 = dtin("w2", [32, 32, 128, 128], F32)
    b2 = dtin("b2", [128, 32], F32)
    w3 = dtin("w3", [16, 32, 128, 128], F32)
    b3 = dtin("b3", [128, 16], F32)
    x4out = nc.dram_tensor("x4out", [8, 128, D], F32, kind="ExternalOutput").ap()
    it = lambda n, sh, ty: nc.dram_tensor(n, sh, ty, kind="Internal").ap()
    vhbm = it("vhbm", [16, 128, D], BF16)
    x1hbm = it("x1hbm", [8, 128, D], F32)
    x2h = it("x2h", [8, 128, D], F32)
    ag_in = it("ag_in", [8 * 128, D], BF16)
    ag_out = it("ag_out", [16 * 128, D], BF16)
    xc_h = it("xc_h", [16, 128, S], BF16)
    zs_h = it("zs_h", [16, 128, S], BF16)
    xdbl_in = it("xdbl_in", [160, S], F32)
    xdbl_out = it("xdbl_out", [160, S], F32)
    bc16_h = it("bc16_h", [32, S], BF16)
    yg_h = it("yg_h", [16, 128, S], BF16)
    rs_in = it("rs_in", [4096, HALF], F32)
    rs_out = it("rs_out", [2048, HALF], F32)

    LC = 1024

    with ExitStack() as stack:
        sb = lambda n, sh, ty: stack.enter_context(nc.sbuf_tensor(n, sh, ty))
        psa = lambda n, sh, ty: stack.enter_context(nc.psum_tensor(n, sh, ty))
        A1 = sb("A1", [128, 32768], BF16)
        A2 = sb("A2", [128, 32768], BF16)
        A3 = sb("A3", [128, 16384], BF16)
        xt = sb("xt", [128, D], F32)
        xt2 = sb("xt2", [128, D], F32)
        xn_tm = sb("xn_tm", [128, D], BF16)
        stats = ln_stats_tensors(nc, stack)
        wsg = [sb("wstgA", [128, 1024], F32), sb("wstgB", [128, 1024], F32)]
        wbf = sb("wbf", [128, 4096], BF16)
        bw_f32 = sb("bw_f32", [128, 128], F32)
        vt0 = sb("vt0", [128, D], BF16)
        qk_raw = sb("qk_raw", [128, 512], BF16)
        rope_t1 = sb("rope_t1", [128, 512], BF16)
        colv = sb("colv", [128, 8], F32)
        bbase_sb = sb("bbase_sb", [128, 16], F32)
        rw_sb = sb("rw_sb", [128, 16 * 8], BF16)
        oh_tm = sb("oh_tm", [128, 8], BF16)
        ohT = A1[0:8, 0:1024]
        of_sb = A1[:, 14336:14848]
        bwbf = A1[:, 14848:14976]
        ident = sb("ident", [128, 128], BF16)
        perm = sb("perm", [128, 128], BF16)
        repm = sb("repm", [8, 128], BF16)
        wsm = A2[:, 0:2560]
        cw_sb = sb("cw_sb", [128, 64], F32)
        cb_sb = sb("cb_sb", [128, 16], F32)
        bdt_sb = sb("bdt_sb", [128, 16], F32)
        dsk_sb = sb("dsk_sb", [128, 16], F32)
        acol = xt2[:, 1536:1792]
        stc = xt2[:, 1792:2048]
        bcols = sb("bcols", [128, 96], F32)
        of2 = A3[:, 8192:8704]
        pf = [psa(f"pf{i}", [128, 512], F32) for i in range(6)]
        pb = [psa(f"pb{i}", [128, 1024], BF16) for i in range(2)]

        blk = stack.enter_context(nc.Block())
        p = Prog(nc, stack)
        emit_consts(p, stats)

        wunit = [0]

        def wfeed(src_ap, tiled=True):
            u = wunit[0] % 4
            sg = wunit[0] % 2
            wunit[0] += 1
            if tiled:
                p.dma("sync", wsg[sg][:, :].rearrange("p (k j) -> p k j", k=8),
                      src_ap, writes=[f"wsg{sg}"])
            else:
                p.dma("sync", wsg[sg][:, :], src_ap, writes=[f"wsg{sg}"])
            p.vop(lambda e, o=wbf[:, u * 1024:(u + 1) * 1024], s=wsg[sg]:
                  e.tensor_copy(o, s[:, :]),
                  reads=[f"wsg{sg}"], writes=[f"wbfu{u}"])
            return u

        def wlhs(us, k):
            u = us[k // 8]
            return wbf[:, u * 1024 + (k % 8) * 128: u * 1024 + (k % 8) * 128 + 128], f"wbfu{u}"

        p.dma("sync", ident[:, :], ident_i[:, :], writes=["ident"])
        p.dma("sync", perm[:, :], perm_i[:, :], writes=["perm"])
        p.dma("sync", repm[:, :], rep8_i[:, :], writes=["repm"])
        p.dma("sync", bbase_sb[:, :], bbase[:, :], writes=["bbase"])
        p.dma("sync", wsg[0][:, 0:128], rwt[:, :], writes=["wsg0"])
        p.vop(lambda e: e.tensor_copy(rw_sb[:, :], wsg[0][:, 0:128]), reads=["wsg0"], writes=["rw"])

        for rb in range(16):
            p.dma("sync", xt[:, :], x0p[rb * 128:(rb + 1) * 128, :], writes=["xt"])
            emit_ln(p, xt[:, :], "xt", xn_tm[:, :], "xn_tm", stats, rb % 2)
            for dtl in range(16):
                pbk = f"pb{dtl % 2}"
                pbt = pb[dtl % 2]
                p.op("tensor", lambda e, s=xn_tm[:, dtl * 128:(dtl + 1) * 128],
                     o=pbt[:, 0:128]: e.transpose(o, s, ident[:, :]),
                     reads=["xn_tm", "ident", pbk + "r"], writes=[pbk])
                p.vop(lambda e, i=pbt[:, 0:128],
                      o=A1[:, dtl * S + rb * 128: dtl * S + (rb + 1) * 128]:
                      e.tensor_copy(o, i),
                      reads=[pbk], writes=[f"xnf{dtl}", pbk + "r"])

        for k in range(16):
            for hh in range(2):
                sg = (2 * k + hh) % 2
                p.dma("sync", wsg[sg][:, :], wv[k][:, hh * 1024:(hh + 1) * 1024],
                      writes=[f"wsg{sg}"])
                p.vop(lambda e, o=A2[:, k * S + hh * 1024: k * S + (hh + 1) * 1024],
                      s=wsg[sg]: e.tensor_copy(o, s[:, :]),
                      reads=[f"wsg{sg}"], writes=[f"wvb{k}"])
        for rb in range(16):
            vbuf = vt0
            vkey = "vt0"
            for dvc in range(4):
                bank = pf[dvc % 2]
                bk = f"PF{dvc % 2}"
                for k in range(16):
                    p.mm(bank[:, :], A1[:, k * S + rb * 128: k * S + rb * 128 + 128],
                         A2[:, k * S + dvc * 512: k * S + dvc * 512 + 512],
                         start=(k == 0), stop=(k == 15),
                         reads=[f"xnf{k}", f"wvb{k}", bk + "r"],
                         writes=[bk] if k == 15 else [])
                p.vop(lambda e, i=bank[:, :], o=vbuf[:, dvc * 512:(dvc + 1) * 512]:
                      e.tensor_copy(o, i),
                      reads=[bk], writes=[vkey + "w", bk + "r"])
            p.dma("gpsimd", vhbm[rb], vbuf[:, :], reads=[vkey + "w"], writes=[f"vhbm{rb}", vkey], q="dP")

        def proj_rope(wdram, cosd, sind, ntc, dstA, dstCol, dstKey, cs_t, sn_t):
            for m in range(16):
                us = [wfeed(wdram[m][hh * 8:(hh + 1) * 8].rearrange("k pp j -> pp k j"))
                      for hh in range(2)]
                p.dma("sync", cs_t[:, 0:ntc * 512], cosd[m], writes=["cs"])
                p.dma("sync", sn_t[:, 0:ntc * 512], sind[m], writes=["sn"])
                for tc in range(ntc):
                    bank = pf[tc % 2]
                    bk = f"PF{tc % 2}"
                    for k in range(16):
                        lh, lk = wlhs(us, k)
                        p.mm(bank[:, :], lh,
                             A1[:, k * S + tc * 512: k * S + tc * 512 + 512],
                             start=(k == 0), stop=(k == 15),
                             reads=[lk, f"xnf{k}", bk + "r"],
                             writes=[bk] if k == 15 else [])
                    p.act(qk_raw[:, :], bank[:, :], AF.Copy,
                          reads=[bk], writes=["qkr", bk + "r"])
                    p.op("tensor", lambda e, o=pf[3][:, :]:
                         e.matmul(o, perm[:, :], qk_raw[:, :], start=True, stop=True),
                         reads=["perm", "qkr", "PF3r"], writes=["PF3"])
                    p.vop(lambda e, c=cs_t[:, tc * 512:(tc + 1) * 512]:
                          e.tensor_mul(rope_t1[:, :], qk_raw[:, :], c),
                          reads=["qkr", "cs"], writes=["rt1"])
                    p.vop(lambda e, sl=sn_t[:, tc * 512:(tc + 1) * 512],
                          i=pf[3][:, :]: e.tensor_mul(qk_raw[:, :], i, sl),
                          reads=["PF3", "sn"], writes=["qkr", "PF3r"])
                    p.vop(lambda e, o=dstA[:, m * dstCol + tc * 512:
                                          m * dstCol + tc * 512 + 512]:
                          e.tensor_add(o, rope_t1[:, :], qk_raw[:, :]),
                          reads=["rt1", "qkr"], writes=[dstKey(m)])

        p.merge_key("kfA2", [f"wvb{k}" for k in range(16)])
        proj_rope(wk, cosk, sink, 4, A2, S, lambda m: f"kf{m}",
                  A3[:, 0:2048], A3[:, 2048:4096])
        p.merge_key("cs", ["cs", "vt0", "vt0w"])
        proj_rope(wq, cosq, sinq, 2, A3, HALF, lambda m: f"qf{m}",
                  vt0[:, 0:1024], vt0[:, 1024:2048])
        _old = [p.deps.pop(k, None) for k in ("cs", "sn")]
        _rs = []
        for _d in _old:
            if _d:
                if _d["w"]:
                    _rs.append(_d["w"])
                _rs.extend(_d["r"])
        for _k in ("vtA", "vtB"):
            p.deps[_k] = {"w": None, "r": list(_rs)}

        p.merge_key("attA1", [f"xnf{k}" for k in range(16)] + ["xn_tm"])
        PT = lambda kt: slice(kt * 512, kt * 512 + 512)
        PRB = slice(8192, 10240)
        PRB2 = slice(10240, 12288)
        MSK = slice(12288, 14336)

        first_att = ["attA1"]
        for qs in range(2):
            for qc4 in range(4):
                qc = qs * 4 + qc4
                p.dma("sync", A1[:, MSK], maskq[qc], writes=["mask"] + first_att)
                first_att = []
                for kc in range(4):
                    bank = pf[kc]
                    bk = f"PF{kc}"
                    for k in range(16):
                        p.mm(bank[:, :], A3[:, k * HALF + qc * 128: k * HALF + qc * 128 + 128],
                             A2[:, k * S + kc * 512: k * S + kc * 512 + 512],
                             start=(k == 0), stop=(k == 15),
                             reads=[f"qf{k}", f"kf{k}", bk + "r"],
                             writes=[bk] if k == 15 else [])
                    p.vop(lambda e, i=bank[:, :], o=colv[:, kc:kc + 1]:
                          e.reduce_max(o, i, axis=AX.X),
                          reads=[bk], writes=[f"mx{kc}"])
                p.vop(lambda e: e.tensor_max(colv[:, 4:5], colv[:, 0:1], colv[:, 1:2]),
                      reads=["mx0", "mx1"], writes=["mx01"])
                p.vop(lambda e: e.tensor_max(colv[:, 5:6], colv[:, 2:3], colv[:, 3:4]),
                      reads=["mx2", "mx3"], writes=["mx23"])
                p.vop(lambda e: e.tensor_max(colv[:, 6:7], colv[:, 4:5], colv[:, 5:6]),
                      reads=["mx01", "mx23"], writes=["mx"])
                p.vop(lambda e: e.tensor_scalar_mul(colv[:, 1:2], colv[:, 6:7], -SCALE_ATT),
                      reads=["mx"], writes=["nmx", "mx1"])
                for kc in range(4):
                    bank = pf[kc]
                    bk = f"PF{kc}"
                    p.act(A1[:, 8192 + kc * 512:8192 + (kc + 1) * 512], bank[:, :],
                          AF.Exp, bias=colv[:, 1:2], scale=stats["cst"][:, 2:3],
                          reads=[bk, "nmx", "cst"], writes=["prb", bk + "r"])
                p.vop(lambda e, i=A1[:, PRB], mk=A1[:, MSK], o=A1[:, PRB2]:
                      e.tensor_mul(o, i, mk),
                      reads=["prb", "mask"], writes=["prb2"])
                p.vop(lambda e, i=A1[:, PRB2]: e.reduce_sum(colv[:, 2:3], i, axis=AX.X),
                      reads=["prb2"], writes=["den", "mx2"])
                p.vop(lambda e: e.reciprocal(colv[:, 3:4], colv[:, 2:3]),
                      reads=["den"], writes=["rden", "mx3"])
                p.vop(lambda e, i=A1[:, PRB2], o=A1[:, PRB]:
                      e.tensor_scalar_mul(o, i, colv[:, 3:4]),
                      reads=["prb2", "rden", "prb"], writes=["prb"])
                for kt in range(16):
                    pbk = f"pb{kt % 2}"
                    pbt = pb[kt % 2]
                    p.op("tensor", lambda e, s=A1[:, 8192 + kt * 128:8192 + kt * 128 + 128],
                         o=pbt[:, 0:128]: e.transpose(o, s, ident[:, :]),
                         reads=["prb", "ident", pbk + "r"], writes=[pbk])
                    p.vop(lambda e, i=pbt[:, 0:128],
                          o=A1[:, kt * 512 + qc4 * 128: kt * 512 + qc4 * 128 + 128]:
                          e.tensor_copy(o, i),
                          reads=[pbk], writes=[f"pT{kt}", pbk + "r"])
            for dvg in range(4):
                for kt in range(16):
                    vbuf = vt0[:, (kt % 2) * 512:(kt % 2) * 512 + 512]
                    vkey = "vtA" if kt % 2 == 0 else "vtB"
                    p.dma("sync", vbuf, vhbm[kt][:, dvg * 512:(dvg + 1) * 512],
                          reads=[f"vhbm{kt}"], writes=[vkey])
                    for dvq in range(4):
                        bank = pf[2 + dvq]
                        bk = f"PF{2 + dvq}"
                        p.mm(bank[:, :], vbuf[:, dvq * 128:(dvq + 1) * 128],
                             A1[:, PT(kt)],
                             start=(kt == 0), stop=(kt == 15),
                             reads=[vkey, f"pT{kt}", bk + "r"],
                             writes=[bk] if kt == 15 else [])
                for dvq in range(4):
                    dvt = dvg * 4 + dvq
                    bank = pf[2 + dvq]
                    bk = f"PF{2 + dvq}"
                    p.vop(lambda e, i=bank[:, :],
                          o=A1[:, 16384 + dvt * HALF + qs * 512:16384 + dvt * HALF + qs * 512 + 512]:
                          e.tensor_copy(o, i),
                          reads=[bk], writes=[f"att{dvt}", bk + "r"])

        for tc in range(2):
            p.merge_key(f"otm{tc}", [f"pT{kt}" for kt in range(16)] if tc == 0 else ["otmdone0"])
            for m in range(16):
                us = [wfeed(wo[m][hh * 8:(hh + 1) * 8].rearrange("k pp j -> pp k j"))
                      for hh in range(2)]
                bank = pf[m % 2]
                bk = f"PF{m % 2}"
                for k in range(16):
                    lh, lk = wlhs(us, k)
                    p.mm(bank[:, :], lh,
                         A1[:, 16384 + k * HALF + tc * 512:16384 + k * HALF + tc * 512 + 512],
                         start=(k == 0), stop=(k == 15),
                         reads=[lk, f"att{k}", bk + "r"],
                         writes=[bk] if k == 15 else [])
                p.act(of_sb[:, :], bank[:, :], AF.Copy, reads=[bk], writes=["of", bk + "r"])
                for sub in range(4):
                    pbk = f"pb{sub % 2}"
                    pbt = pb[sub % 2]
                    p.op("tensor", lambda e, s=of_sb[:, sub * 128:(sub + 1) * 128],
                         o=pbt[:, 0:128]: e.transpose(o, s, ident[:, :]),
                         reads=["of", "ident", pbk + "r"], writes=[pbk])
                    p.vop(lambda e, i=pbt[:, 0:128],
                          o=A1[:, sub * 2048 + m * 128: sub * 2048 + m * 128 + 128]:
                          e.tensor_copy(o, i),
                          reads=[pbk], writes=[f"otm{tc}", pbk + "r"])
            for sub in range(4):
                rb = tc * 4 + sub
                p.dma("sync", xt[:, :], x0p[rb * 128:(rb + 1) * 128, :], writes=["xt"])
                p.vop(lambda e, i=A1[:, sub * 2048:(sub + 1) * 2048]:
                      e.tensor_add(xt2[:, :], xt[:, :], i),
                      reads=["xt", f"otm{tc}", "xt2r"], writes=["xt2"])
                p.dma("gpsimd", x1hbm[rb], xt2[:, :], reads=["xt2"], writes=[f"x1h{rb}", "xt2r"], q="dP")
            p.deps["otmdone0"] = {"w": None, "r": list(p.deps[f"otm{tc}"]["r"])}

        p.merge_key("xnf1A3", [f"qf{m}" for m in range(16)])
        p.merge_key("ffnA2", [f"kf{m}" for m in range(16)])
        for rb in range(8):
            p.dma("sync", xt[:, :], x1hbm[rb], reads=[f"x1h{rb}", "xt2r"], writes=["xt"])
            emit_ln(p, xt[:, :], "xt", xn_tm[:, :], "xn_tm", stats, rb % 2)
            for dtl in range(16):
                pbk = f"pb{dtl % 2}"
                pbt = pb[dtl % 2]
                p.op("tensor", lambda e, s=xn_tm[:, dtl * 128:(dtl + 1) * 128],
                     o=pbt[:, 0:128]: e.transpose(o, s, ident[:, :]),
                     reads=["xn_tm", "ident", pbk + "r"], writes=[pbk])
                p.vop(lambda e, i=pbt[:, 0:128],
                      o=A3[:, dtl * HALF + rb * 128: dtl * HALF + rb * 128 + 128]:
                      e.tensor_copy(o, i),
                      reads=[pbk], writes=["xnf1A3", pbk + "r"])
            bank = pf[4]
            bk = "PF4"
            for k in range(16):
                p.mm(bank[:, 0:8], A3[:, k * HALF + rb * 128: k * HALF + rb * 128 + 128],
                     rw_sb[:, k * 8:(k + 1) * 8],
                     start=(k == 0), stop=(k == 15),
                     reads=["xnf1A3", "rw", bk + "r"],
                     writes=[bk] if k == 15 else [])
            p.vop(lambda e, i=bank[:, 0:8]: e.reduce_max(colv[:, 4:5], i, axis=AX.X),
                  reads=[bk], writes=["rmx"])
            p.vop(lambda e, i=bank[:, 0:8]: e.tensor_scalar(oh_tm[:, :], i, colv[:, 4:5], 2.0,
                                                            ALU.is_ge, ALU.mult),
                  reads=[bk, "rmx"], writes=["oh", bk + "r"])
            p.op("tensor", lambda e, o=pb[0][0:8, 0:128]:
                 e.transpose(o, oh_tm[:, :], ident[:, :]),
                 reads=["oh", "ident", "pb0r"], writes=["pb0"])
            p.vop(lambda e, i=pb[0][0:8, 0:128], o=ohT[:, rb * 128:(rb + 1) * 128]:
                  e.tensor_copy(o, i),
                  reads=["pb0"], writes=["ohT", "pb0r"])
        for rc in range(2):
            p.op("tensor", lambda e, o=pf[3][:, :],
                 r=ohT[:, rc * 512:(rc + 1) * 512]:
                 e.matmul(o, repm[:, :], r, start=True, stop=True),
                 reads=["repm", "ohT", "PF3r"], writes=["PF3"])
            p.vop(lambda e, i=pf[3][:, :], o=A2[:, rc * 512:rc * 512 + 512]:
                  e.tensor_copy(o, i),
                  reads=["PF3"], writes=["maskfm", "PF3r"] + (["ffnA2"] if rc == 0 else []))
        us_at = [wfeed(at4[hh * 8:(hh + 1) * 8].rearrange("k pp j -> pp k j"))
                 for hh in range(2)]
        for rc in range(2):
            bank = pf[4]
            bk = "PF4"
            for k in range(16):
                lh, lk = wlhs(us_at, k)
                p.mm(bank[:, :], lh,
                     A3[:, k * HALF + rc * 512: k * HALF + rc * 512 + 512],
                     start=(k == 0), stop=(k == 15),
                     reads=[lk, "xnf1A3", bk + "r"],
                     writes=[bk] if k == 15 else [])
            p.vop(lambda e, i=bank[:, :], mk=A2[:, rc * 512:rc * 512 + 512],
                  o=A2[:, 1024 + rc * 512:1024 + rc * 512 + 512]:
                  e.tensor_mul(o, i, mk),
                  reads=[bk, "maskfm"], writes=["hm", bk + "r"])
        for tc in range(2):
            p.merge_key(f"otm2_{tc}", [] if tc == 0 else ["otm2done"])
            for m in range(16):
                us = [wfeed(wbase[m][hh * 8:(hh + 1) * 8].rearrange("k pp j -> pp k j"))
                      for hh in range(2)]
                p.dma("sync", bw_f32[:, :], bwt[m], writes=["bwstg"])
                p.vop(lambda e: e.tensor_copy(bwbf[:, :], bw_f32[:, :]),
                      reads=["bwstg"], writes=["bwbf"])
                bank = pf[m % 2]
                bk = f"PF{m % 2}"
                for k in range(16):
                    lh, lk = wlhs(us, k)
                    p.mm(bank[:, :], lh,
                         A3[:, k * HALF + tc * 512: k * HALF + tc * 512 + 512],
                         start=(k == 0), stop=False,
                         reads=[lk, "xnf1A3", bk + "r"], writes=[])
                p.mm(bank[:, :], bwbf[:, :],
                     A2[:, 1024 + tc * 512:1024 + tc * 512 + 512],
                     start=False, stop=True,
                     reads=["bwbf", "hm", bk + "r"], writes=[bk])
                p.act(of_sb[:, :], bank[:, :], AF.Identity, bias=bbase_sb[:, m:m + 1],
                      reads=[bk, "bbase"], writes=["of", bk + "r"])
                for sub in range(4):
                    pbk = f"pb{sub % 2}"
                    pbt = pb[sub % 2]
                    p.op("tensor", lambda e, s=of_sb[:, sub * 128:(sub + 1) * 128],
                         o=pbt[:, 0:128]: e.transpose(o, s, ident[:, :]),
                         reads=["of", "ident", pbk + "r"], writes=[pbk])
                    p.vop(lambda e, i=pbt[:, 0:128],
                          o=A2[:, 2048 + sub * 2048 + m * 128: 2048 + sub * 2048 + m * 128 + 128]:
                          e.tensor_copy(o, i),
                          reads=[pbk], writes=[f"otm2_{tc}", pbk + "r"])
            for sub in range(4):
                rb = tc * 4 + sub
                p.dma("sync", xt[:, :], x1hbm[rb], reads=[f"x1h{rb}"], writes=["xt"])
                p.vop(lambda e, i=A2[:, 2048 + sub * 2048: 2048 + (sub + 1) * 2048]:
                      e.tensor_add(xt2[:, :], xt[:, :], i),
                      reads=["xt", f"otm2_{tc}", "xt2r"], writes=["xt2"])
                p.dma("gpsimd", x2h[rb], xt2[:, :], reads=["xt2"], writes=[f"x2h{rb}"], q="dP")
                emit_ln(p, xt2[:, :], "xt2", xn_tm[:, :], "xn_tm", stats, rb % 2)
                p.dma("gpsimd", ag_in[rb * 128:(rb + 1) * 128, :], xn_tm[:, :],
                      reads=["xn_tm"], writes=[f"agi{rb}", "xt2r"], q="dP")
            p.deps["otm2done"] = {"w": None, "r": list(p.deps[f"otm2_{tc}"]["r"])}

        p.cc(0, "AllGather", ALU.bypass, groups, ag_in[:, :].opt(), ag_out[:, :].opt(),
             reads=[f"agi{rb}" for rb in range(8)], writes=["agout"])

        p.dma("sync", cw_sb[:, :], convw[:, :], writes=["cw"])
        p.dma("sync", cb_sb[:, :], convb[:, :], writes=["cb"])
        p.merge_key("cxnA1", ["mask", "prb", "prb2", "of", "bwbf"]
                    + [f"otm{t}" for t in range(2)] + [f"pT{k}" for k in range(16)]
                    + [f"att{k}" for k in range(16)] + ["ohT"])
        first_c = ["cxnA1"]
        for rb in range(16):
            p.dma("sync", xn_tm[:, :], ag_out[rb * 128:(rb + 1) * 128, :],
                  reads=["agout"], writes=["xn_tm"])
            for dtl in range(16):
                pbk = f"pb{dtl % 2}"
                pbt = pb[dtl % 2]
                p.op("tensor", lambda e, s=xn_tm[:, dtl * 128:(dtl + 1) * 128],
                     o=pbt[:, 0:128]: e.transpose(o, s, ident[:, :]),
                     reads=["xn_tm", "ident", pbk + "r"], writes=[pbk])
                p.vop(lambda e, i=pbt[:, 0:128],
                      o=A1[:, dtl * S + rb * 128: dtl * S + (rb + 1) * 128]:
                      e.tensor_copy(o, i),
                      reads=[pbk], writes=[f"cxnf{dtl}"] + first_c)
                first_c = []
        xrawv = A3[:, 0:2048]
        cnvA = A3[:, 2048:4096]
        cnvB = A3[:, 4096:6144]
        xctv = A3[:, 6144:8192]
        zstv = A3[:, 8192:10240]
        p.merge_key("cA3", ["xnf1A3"])

        def projc(wdram, m, dstT, dstKey, func, biascol, bias_reads, extra_w=()):
            us = [wfeed(wdram[m][hh * 8:(hh + 1) * 8].rearrange("k pp j -> pp k j"))
                  for hh in range(2)]
            for lch in range(4):
                bank = pf[lch % 2]
                bk = f"PF{lch % 2}"
                for k in range(16):
                    lh, lk = wlhs(us, k)
                    p.mm(bank[:, :], lh,
                         A1[:, k * S + lch * 512: k * S + lch * 512 + 512],
                         start=(k == 0), stop=(k == 15),
                         reads=[lk, f"cxnf{k}", bk + "r"],
                         writes=[bk] if k == 15 else [])
                p.act(dstT[:, lch * 512:(lch + 1) * 512], bank[:, :], func,
                      bias=biascol if biascol is not None else 0.0,
                      reads=[bk] + bias_reads, writes=[dstKey, bk + "r"] + list(extra_w))

        first_cA3 = ["cA3"]
        for dxt in range(16):
            projc(winx, dxt, xrawv, "xraw", AF.Copy, None, [], extra_w=first_cA3)
            first_cA3 = []
            wc = lambda j: cw_sb[:, dxt * 4 + j:dxt * 4 + j + 1]
            p.vop(lambda e, w=wc(3): e.tensor_scalar_mul(cnvA[:, :], xrawv[:, :], w),
                  reads=["xraw", "cw"], writes=["cA"])
            p.vop(lambda e, w=wc(2): e.scalar_tensor_tensor(cnvB[:, 1:S], xrawv[:, 0:S - 1], w,
                                                            cnvA[:, 1:S], ALU.mult, ALU.add),
                  reads=["xraw", "cA", "cw"], writes=["cB"])
            p.vop(lambda e: e.tensor_copy(cnvB[:, 0:1], cnvA[:, 0:1]),
                  reads=["cA"], writes=["cB"])
            p.vop(lambda e, w=wc(1): e.scalar_tensor_tensor(cnvA[:, 2:S], xrawv[:, 0:S - 2], w,
                                                            cnvB[:, 2:S], ALU.mult, ALU.add),
                  reads=["xraw", "cB", "cw", "cA"], writes=["cA"])
            p.vop(lambda e: e.tensor_copy(cnvA[:, 0:2], cnvB[:, 0:2]),
                  reads=["cB"], writes=["cA"])
            p.vop(lambda e, w=wc(0): e.scalar_tensor_tensor(cnvB[:, 3:S], xrawv[:, 0:S - 3], w,
                                                            cnvA[:, 3:S], ALU.mult, ALU.add),
                  reads=["xraw", "cA", "cw", "cB"], writes=["cB"])
            p.vop(lambda e: e.tensor_copy(cnvB[:, 0:3], cnvA[:, 0:3]),
                  reads=["cA"], writes=["cB"])
            p.act(xctv[:, :], cnvB[:, :], AF.Silu, bias=cb_sb[:, dxt:dxt + 1],
                  reads=["cB", "cb"], writes=["xct"])
            p.dma("gpsimd", xc_h[dxt], xctv[:, :], reads=["xct"], writes=[f"xch{dxt}"], q="dP")
            projc(winz, dxt, zstv, "zst", AF.Silu, None, [])
            p.dma("gpsimd", zs_h[dxt], zstv[:, :], reads=["zst"], writes=[f"zsh{dxt}"], q="dP")
        for dxt in range(16):
            p.dma("sync", wsg[0][:, 0:128], wxdt[dxt], writes=["wsg0"])
            p.dma("sync", wsg[0][:, 128:144], wxB[dxt], writes=["wsg0b"])
            p.dma("sync", wsg[0][:, 144:160], wxC[dxt], writes=["wsg0c"])
            p.vop(lambda e, o=wsm[:, dxt * 160:(dxt + 1) * 160]:
                  e.tensor_copy(o, wsg[0][:, 0:160]),
                  reads=["wsg0", "wsg0b", "wsg0c"], writes=["wsm"])
        for lch in range(4):
            xs2 = [A3[:, 10240:10752], A3[:, 10752:11264]]
            for dxt in range(16):
                xsv = xs2[dxt % 2]
                xk = f"xcs{dxt % 2}"
                p.dma("sync", xsv, xc_h[dxt][:, lch * 512:(lch + 1) * 512],
                      reads=[f"xch{dxt}"], writes=[xk])
                p.mm(pf[3][:, :], wsm[:, dxt * 160:dxt * 160 + 128], xsv,
                     start=(dxt == 0), stop=(dxt == 15),
                     reads=["wsm", xk, "PF3r"], writes=["PF3"] if dxt == 15 else [])
                p.mm(pf[4][0:16, :], wsm[:, dxt * 160 + 128:dxt * 160 + 144], xsv,
                     start=(dxt == 0), stop=(dxt == 15),
                     reads=["wsm", xk, "PF4r"], writes=["PF4"] if dxt == 15 else [])
                p.mm(pf[5][0:16, :], wsm[:, dxt * 160 + 144:dxt * 160 + 160], xsv,
                     start=(dxt == 0), stop=(dxt == 15),
                     reads=["wsm", xk, "PF5r"], writes=["PF5"] if dxt == 15 else [])
            p.vop(lambda e, o=xt[:, 0:512], i=pf[3][:, :]: e.tensor_copy(o, i),
                  reads=["PF3"], writes=["xtd", "PF3r", "xt"])
            p.vop(lambda e, o=xt[0:16, 512:1024], i=pf[4][0:16, :]: e.tensor_copy(o, i),
                  reads=["PF4", "xtd"], writes=["xtd", "PF4r"])
            p.vop(lambda e, o=xt[0:16, 1024:1536], i=pf[5][0:16, :]: e.tensor_copy(o, i),
                  reads=["PF5", "xtd"], writes=["xtd", "PF5r"])
            p.dma("gpsimd", xdbl_in[0:128, lch * 512:(lch + 1) * 512], xt[:, 0:512],
                  reads=["xtd"], writes=[f"xdi{lch}a"], q="dP")
            p.dma("gpsimd", xdbl_in[128:144, lch * 512:(lch + 1) * 512], xt[0:16, 512:1024],
                  reads=["xtd"], writes=[f"xdi{lch}b"], q="dP")
            p.dma("gpsimd", xdbl_in[144:160, lch * 512:(lch + 1) * 512], xt[0:16, 1024:1536],
                  reads=["xtd"], writes=[f"xdi{lch}c", "xt"], q="dP")

        p.cc(1, "AllReduce", ALU.add, groups, xdbl_in[:, :].opt(), xdbl_out[:, :].opt(),
             reads=[f"xdi{l}{c}" for l in range(4) for c in "abc"], writes=["xdo"])

        dt_bf = A1[:, 0:2048]
        xct_b = [A1[:, 2048:3072], A1[:, 14336:15360]]
        zst_b = [A1[:, 3072:4096], A1[:, 15360:16384]]
        dxb = A1[:, 4096:5120]
        dA = [A1[:, 5120:6144], A1[:, 6144:7168]]
        ub = [A1[:, 7168:8192], A1[:, 8192:9216]]
        sn = [A1[:, 9216:10240], A1[:, 10240:11264]]
        csb = A1[:, 11264:12288]
        yacc = A1[:, 12288:13312]
        y2 = A1[:, 13312:14336]
        bc16s = A1[0:32, 14336:16384]
        BB = A2[:, 0:16384]
        CB = A2[:, 16384:32768]
        bfl = A3[0:1, 0:1024]
        onesr = A3[0:1, 1024:1152]
        wdt_bf = vt0[:, 0:2048]
        deltav = xt2[:, 0:1024]
        absx = xt[:, 0:512]
        relux = xt[:, 512:1024]
        expx = xt[:, 1024:1536]
        _old = [p.deps.pop(k, None) for k in ("xtd", "xt")]
        _rs = []
        for _d in _old:
            if _d:
                if _d["w"]:
                    _rs.append(_d["w"])
                _rs.extend(_d["r"])
        for _k in ("absx", "relux", "expx"):
            p.deps[_k] = {"w": None, "r": list(_rs)}
        p.merge_key("dA1", [f"cxnf{k}" for k in range(16)] + ["xn_tm"])
        p.merge_key("dA2", ["maskfm", "hm", "wsm"] + [f"otm2_{t}" for t in range(2)])
        p.merge_key("wdtb0", ["vtA", "vtB"])
        _old = [p.deps.pop(k, None) for k in ("xraw", "cA")]
        _rs = []
        for _d in _old:
            if _d:
                if _d["w"]:
                    _rs.append(_d["w"])
                _rs.extend(_d["r"])
        for _k in ("bfl", "onesr"):
            p.deps[_k] = {"w": None, "r": list(_rs)}
        p.vop(lambda e: e.memset(onesr, 1.0), writes=["onesr"])
        p.dma("sync", bdt_sb[:, :], bdt[:, :], writes=["bdt"])
        p.dma("sync", dsk_sb[:, :], dsk[:, :], writes=["dsk"])
        p.dma("sync", xt2[:, :], xdbl_out[0:128, :], reads=["xdo"], writes=["xt2"])
        p.vop(lambda e, o=dt_bf: e.tensor_copy(o, xt2[:, :]),
              reads=["xt2"], writes=["dtbf", "dA1"])
        p.dma("sync", xt2[0:32, :], xdbl_out[128:160, :], reads=["xdo"], writes=["xt2"])
        p.vop(lambda e, o=bc16s: e.tensor_copy(o, xt2[0:32, :]),
              reads=["xt2"], writes=["bc16"])
        p.dma("gpsimd", bc16_h[:, :], bc16s, reads=["bc16"], writes=["bch"], q="dP")
        p.dma("sync", acol[:, :], alog[:, :], reads=["bc16"], writes=["acol", "xt2"])
        p.act(acol[:, :], acol[:, :], AF.Exp, reads=["acol"], writes=["acol"])
        p.vop(lambda e: e.tensor_scalar_mul(acol[:, :], acol[:, :], -1.0),
              reads=["acol"], writes=["acol"])
        p.merge_key("delta", ["xt2"])
        for hh in range(2):
            sg = hh % 2
            p.dma("sync", wsg[sg][:, :], wdt[:, hh * 1024:(hh + 1) * 1024],
                  writes=[f"wsg{sg}"])
            p.vop(lambda e, o=wdt_bf[:, hh * 1024:(hh + 1) * 1024], s=wsg[sg]:
                  e.tensor_copy(o, s[:, :]),
                  reads=[f"wsg{sg}", "wdtb0" if hh == 0 else "wdtb"], writes=["wdtb"])

        for lc in range(2):
            for n in range(16):
                p.dma("sync", bfl, bc16_h[n:n + 1, lc * LC:(lc + 1) * LC],
                      reads=["bch"], writes=["bfl"])
                for c2 in range(2):
                    p.mm(pf[5][:, :], onesr,
                         bfl[0:1, c2 * 512:(c2 + 1) * 512],
                         start=True, stop=True, reads=["onesr", "bfl", "PF5r"], writes=["PF5"])
                    p.vop(lambda e, o=BB[:, n * LC + c2 * 512: n * LC + c2 * 512 + 512],
                          i=pf[5][:, :]: e.tensor_copy(o, i),
                          reads=["PF5"], writes=[f"BB{n}", "PF5r"] + (["dA2"] if (n == 0 and c2 == 0 and lc == 0) else []))
                p.dma("sync", bfl, bc16_h[16 + n:17 + n, lc * LC:(lc + 1) * LC],
                      reads=["bch"], writes=["bfl"])
                for c2 in range(2):
                    p.mm(pf[5][:, :], onesr,
                         bfl[0:1, c2 * 512:(c2 + 1) * 512],
                         start=True, stop=True, reads=["onesr", "bfl", "PF5r"], writes=["PF5"])
                    p.vop(lambda e, o=CB[:, n * LC + c2 * 512: n * LC + c2 * 512 + 512],
                          i=pf[5][:, :]: e.tensor_copy(o, i),
                          reads=["PF5"], writes=[f"CB{n}", "PF5r"])
            for dxt in range(16):
                pr_ = dxt % 2
                xct_d = xct_b[pr_]
                zst_d = zst_b[pr_]
                xk_ = f"xctd{pr_}"
                zk_ = f"zstd{pr_}"
                p.dma("sync", xct_d, xc_h[dxt][:, lc * LC:(lc + 1) * LC],
                      reads=[f"xch{dxt}"] + (["bc16"] if (lc == 0 and dxt == 1) else []),
                      writes=[xk_])
                p.dma("sync", zst_d, zs_h[dxt][:, lc * LC:(lc + 1) * LC],
                      reads=[f"zsh{dxt}"], writes=[zk_])
                for c2 in range(2):
                    bank = pf[c2 % 2]
                    bk = f"PF{c2 % 2}"
                    p.mm(bank[:, :], wdt_bf[:, dxt * 128:(dxt + 1) * 128],
                         dt_bf[:, lc * LC + c2 * 512: lc * LC + c2 * 512 + 512],
                         start=True, stop=True,
                         reads=["wdtb", "dtbf", bk + "r"], writes=[bk])
                    bcol = bdt_sb[:, dxt:dxt + 1]
                    p.act(absx, bank[:, :], AF.Abs, bias=bcol,
                          reads=[bk, "bdt"], writes=["absx"])
                    p.vop(lambda e, i=bank[:, :], bc=bcol: e.tensor_scalar(
                          relux, i, bc, 0.0, ALU.add, ALU.max),
                          reads=[bk, "bdt", "absx"], writes=["relux", bk + "r"])
                    p.act(expx, absx, AF.Exp, scale=stats["cst"][:, 3:4],
                          reads=["absx", "cst"], writes=["expx"])
                    p.act(expx, expx, AF.Ln, bias=1.0,
                          reads=["expx"], writes=["expx"])
                    p.vop(lambda e, o=deltav[:, c2 * 512:(c2 + 1) * 512]:
                          e.tensor_add(o, relux, expx),
                          reads=["relux", "expx"], writes=["delta"])
                p.vop(lambda e, x=xct_d: e.tensor_mul(dxb, deltav, x),
                      reads=["delta", xk_], writes=["dxb"])
                for n in range(16):
                    da = dA[n % 2]
                    dak = f"dA{n % 2}"
                    u = ub[n % 2]
                    uk = f"ub{n % 2}"
                    s_ = sn[n % 2]
                    sk_ = f"sn{n % 2}"
                    p.act(da, deltav, AF.Exp,
                          scale=acol[:, dxt * 16 + n:dxt * 16 + n + 1],
                          reads=["delta", "acol", dak], writes=[dak])
                    p.vop(lambda e, bb=BB[:, n * LC:(n + 1) * LC], u_=u:
                          e.tensor_mul(u_, dxb, bb),
                          reads=["dxb", f"BB{n}", uk], writes=[uk])
                    ini = 0.0 if lc == 0 else stc[:, dxt * 16 + n:dxt * 16 + n + 1]
                    p.vop(lambda e, ini=ini, s=s_, d=da, u_=u: e.tensor_tensor_scan(
                          s, d, u_, ini, ALU.mult, ALU.add),
                          reads=[dak, uk, "stc", sk_], writes=[sk_])
                    p.vop(lambda e, o=stc[:, dxt * 16 + n:dxt * 16 + n + 1], s=s_:
                          e.tensor_copy(o, s[:, LC - 1:LC]),
                          reads=[sk_], writes=["stc"])
                    p.vop(lambda e, cb=CB[:, n * LC:(n + 1) * LC], s=s_:
                          e.tensor_mul(csb, s, cb),
                          reads=[sk_, f"CB{n}", "csb"], writes=["csb"])
                    if n == 0:
                        p.vop(lambda e: e.tensor_copy(yacc, csb),
                              reads=["csb"], writes=["yacc"])
                    else:
                        p.vop(lambda e: e.tensor_add(yacc, yacc, csb),
                              reads=["csb", "yacc"], writes=["yacc"])
                p.vop(lambda e, dc=dsk_sb[:, dxt:dxt + 1], x=xct_d: e.scalar_tensor_tensor(
                      y2, x, dc, yacc, ALU.mult, ALU.add),
                      reads=[xk_, "dsk", "yacc", "y2"], writes=["y2"])
                p.vop(lambda e, z=zst_d: e.tensor_mul(y2, y2, z),
                      reads=[zk_, "y2"], writes=["y2"])
                p.dma("gpsimd", yg_h[dxt][:, lc * LC:(lc + 1) * LC], y2,
                      reads=["y2"], writes=[f"ygh{dxt}_{lc}"], q="dA")

        p.merge_key("xt", ["absx", "relux", "expx", "xt"])
        p.merge_key("xt2", ["delta", "acol", "stc"])
        p.merge_key("of2", ["zst"])
        p.merge_key("eAY", [f"BB{n}" for n in range(16)] + [f"CB{n}" for n in range(16)])
        first_e = ["eAY"]
        for k in range(16):
            p.dma("sync", A2[:, k * S:(k + 1) * S], yg_h[k],
                  reads=[f"ygh{k}_0", f"ygh{k}_1"], writes=[f"eyg{k}"] + first_e)
            first_e = []
        for m in range(16):
            us = [wfeed(wout[m][hh * 8:(hh + 1) * 8].rearrange("k pp j -> pp k j"))
                  for hh in range(2)]
            for tc in range(4):
                bank = pf[tc % 2]
                bk = f"PF{tc % 2}"
                for k in range(16):
                    lh, lk = wlhs(us, k)
                    p.mm(bank[:, :], lh,
                         A2[:, k * S + tc * 512: k * S + tc * 512 + 512],
                         start=(k == 0), stop=(k == 15),
                         reads=[lk, f"eyg{k}", bk + "r"],
                         writes=[bk] if k == 15 else [])
                p.act(xt[:, 0:512], bank[:, :], AF.Copy,
                      reads=[bk], writes=["epart", bk + "r", "xt"])
                p.dma("gpsimd", rs_in[(tc // 2) * 2048 + m * 128:(tc // 2) * 2048 + (m + 1) * 128,
                                      (tc % 2) * 512:(tc % 2) * 512 + 512],
                      xt[:, 0:512], reads=["epart"], writes=[f"rsi{m}_{tc}"], q="dP")

        p.cc(2, "ReduceScatter", ALU.add, groups, rs_in[:, :].opt(), rs_out[:, :].opt(),
             reads=[f"rsi{m}_{tc}" for m in range(16) for tc in range(4)], writes=["rso"])

        p.dma("sync", bcols[:, 0:16], bout[:, :], writes=["bc"])
        p.dma("sync", bcols[:, 16:48], b1[:, :], writes=["bc1"])
        p.dma("sync", bcols[:, 48:80], b2[:, :], writes=["bc2"])
        p.dma("sync", bcols[:, 80:96], b3[:, :], writes=["bc3"])
        p.merge_key("eotm", ["dtbf", "xctd0", "xctd1", "zstd0", "zstd1", "dxb",
                             "dA0", "dA1", "ub0", "ub1",
                             "sn0", "sn1", "csb", "yacc", "y2", "bc16"])
        first_eo = ["eotm"]
        for m in range(16):
            p.dma("sync", xt2[:, 0:1024], rs_out[m * 128:(m + 1) * 128, :],
                  reads=["rso", "xt2"], writes=["xt2"])
            for ch in range(2):
                p.act(of2[:, :], xt2[:, ch * 512:(ch + 1) * 512], AF.Identity,
                      bias=bcols[:, m:m + 1],
                      reads=["xt2", "bc", "of2"], writes=["of2"])
                for sub in range(4):
                    rb8 = ch * 4 + sub
                    pbk = f"pb{sub % 2}"
                    pbt = pb[sub % 2]
                    p.op("tensor", lambda e, s=of2[:, sub * 128:(sub + 1) * 128],
                         o=pbt[:, 0:128]: e.transpose(o, s, ident[:, :]),
                         reads=["of2", "ident", pbk + "r"], writes=[pbk])
                    p.vop(lambda e, i=pbt[:, 0:128],
                          o=A1[:, rb8 * 2048 + m * 128: rb8 * 2048 + m * 128 + 128]:
                          e.tensor_copy(o, i),
                          reads=[pbk], writes=["eotm", pbk + "r"] if first_eo == [] else ["eotm", pbk + "r"])
        for rb in range(8):
            p.dma("sync", xt[:, :], x2h[rb], reads=[f"x2h{rb}"], writes=["xt", "epart"])
            p.vop(lambda e, i=A1[:, rb * 2048:(rb + 1) * 2048]:
                  e.tensor_add(xt2[:, :], xt[:, :], i),
                  reads=["xt", "eotm", "xt2"], writes=["xt2"])
            p.dma("gpsimd", x1hbm[rb], xt2[:, :], reads=["xt2"], writes=[f"x3h{rb}"], q="dP")
            emit_ln(p, xt2[:, :], "xt2", xn_tm[:, :], "xn_tm", stats, rb % 2)
            for dtl in range(16):
                pbk = f"pb{dtl % 2}"
                pbt = pb[dtl % 2]
                p.op("tensor", lambda e, s=xn_tm[:, dtl * 128:(dtl + 1) * 128],
                     o=pbt[:, 0:128]: e.transpose(o, s, ident[:, :]),
                     reads=["xn_tm", "ident", pbk + "r"], writes=[pbk])
                p.vop(lambda e, i=pbt[:, 0:128],
                      o=A3[:, dtl * HALF + rb * 128: dtl * HALF + rb * 128 + 128]:
                      e.tensor_copy(o, i),
                      reads=[pbk], writes=["x3f", pbk + "r"] +
                      (["cB", "xct", "xcs0", "xcs1", "bfl", "onesr", "of2"]
                       if (rb == 0 and dtl == 0) else []))
        for m in range(32):
            us = [wfeed(w1[m][hh * 8:(hh + 1) * 8].rearrange("k pp j -> pp k j"))
                  for hh in range(2)]
            for tc in range(2):
                bank = pf[tc % 2]
                bk = f"PF{tc % 2}"
                for k in range(16):
                    lh, lk = wlhs(us, k)
                    p.mm(bank[:, :], lh,
                         A3[:, k * HALF + tc * 512: k * HALF + tc * 512 + 512],
                         start=(k == 0), stop=(k == 15),
                         reads=[lk, "x3f", bk + "r"],
                         writes=[bk] if k == 15 else [])
                p.act(A1[:, m * HALF + tc * 512: m * HALF + tc * 512 + 512],
                      bank[:, :], AF.Relu, bias=bcols[:, 16 + m:17 + m],
                      reads=[bk, "bc1"],
                      writes=[f"h1_{m}", bk + "r"] + (["eotm"] if (m == 0 and tc == 0) else []))
        for m in range(32):
            us = [wfeed(w2[m][hh * 8:(hh + 1) * 8].rearrange("k pp j -> pp k j"))
                  for hh in range(4)]
            for tc in range(2):
                bank = pf[tc % 2]
                bk = f"PF{tc % 2}"
                for k in range(32):
                    lh, lk = wlhs(us, k)
                    p.mm(bank[:, :], lh,
                         A1[:, k * HALF + tc * 512: k * HALF + tc * 512 + 512],
                         start=(k == 0), stop=(k == 31),
                         reads=[lk, f"h1_{k}", bk + "r"],
                         writes=[bk] if k == 31 else [])
                p.act(A2[:, m * HALF + tc * 512: m * HALF + tc * 512 + 512],
                      bank[:, :], AF.Relu, bias=bcols[:, 48 + m:49 + m],
                      reads=[bk, "bc2"],
                      writes=[f"h2_{m}", bk + "r"] + ([f"eyg{k}" for k in range(16)]
                                                      if (m == 0 and tc == 0) else []))
        p.merge_key("eo3", ["x3f"])
        for tc in range(2):
            for m in range(16):
                us = [wfeed(w3[m][hh * 8:(hh + 1) * 8].rearrange("k pp j -> pp k j"))
                      for hh in range(4)]
                bank = pf[m % 2]
                bk = f"PF{m % 2}"
                for k in range(32):
                    lh, lk = wlhs(us, k)
                    p.mm(bank[:, :], lh,
                         A2[:, k * HALF + tc * 512: k * HALF + tc * 512 + 512],
                         start=(k == 0), stop=(k == 31),
                         reads=[lk, f"h2_{k}", bk + "r"],
                         writes=[bk] if k == 31 else [])
                p.act(of2[:, :], bank[:, :], AF.Identity, bias=bcols[:, 80 + m:81 + m],
                      reads=[bk, "bc3", "of2"], writes=["of2", bk + "r"])
                for sub in range(4):
                    pbk = f"pb{sub % 2}"
                    pbt = pb[sub % 2]
                    p.op("tensor", lambda e, s=of2[:, sub * 128:(sub + 1) * 128],
                         o=pbt[:, 0:128]: e.transpose(o, s, ident[:, :]),
                         reads=["of2", "ident", pbk + "r"], writes=[pbk])
                    p.vop(lambda e, i=pbt[:, 0:128],
                          o=A3[:, sub * 2048 + m * 128: sub * 2048 + m * 128 + 128]:
                          e.tensor_copy(o, i),
                          reads=[pbk], writes=["eo3", pbk + "r"])
            for sub in range(4):
                rb = tc * 4 + sub
                p.dma("sync", xt[:, :], x1hbm[rb], reads=[f"x3h{rb}", "xt"], writes=["xt"])
                p.vop(lambda e, i=A3[:, sub * 2048:(sub + 1) * 2048]:
                      e.tensor_add(xt2[:, :], xt[:, :], i),
                      reads=["xt", "eo3", "xt2"], writes=["xt2"])
                p.dma("gpsimd", x4out[rb], xt2[:, :], reads=["xt2"], writes=["xt2"], q="dA")
        p.finalize(blk)
    return nc



def tile4(wt):
    K_, M_ = wt.shape
    return np.ascontiguousarray(
        wt.reshape(K_ // 128, 128, M_ // 128, 128).transpose(2, 0, 1, 3))


_ROPE = {}


def rope_tables():
    if "cos" not in _ROPE:
        j = np.arange(0, D, 2, dtype=np.float64)
        inv = 1.0 / (10000.0 ** (j / D))
        pos = np.arange(S, dtype=np.float64)
        ang = pos[None, :] * inv[:, None]
        cosv, sinv = np.cos(ang), np.sin(ang)
        cosT = np.empty((D, S), np.float32)
        sinT = np.empty((D, S), np.float32)
        cosT[0::2] = cosv
        cosT[1::2] = cosv
        sinT[0::2] = -sinv
        sinT[1::2] = sinv
        _ROPE["cos"] = cosT.astype(bf16)
        _ROPE["sin"] = sinT.astype(bf16)
    return _ROPE["cos"], _ROPE["sin"]


def perm_pairswap():
    p = np.zeros((128, 128), bf16)
    for i in range(128):
        p[i ^ 1, i] = 1.0
    return p


def rep8_mat():
    r = np.zeros((8, 128), bf16)
    for e in range(8):
        r[e, e * 16:(e + 1) * 16] = 1.0
    return r


IDENT = np.eye(128, dtype=bf16)


def colmat(v, ntiles):
    return np.ascontiguousarray(v.reshape(ntiles, 128).T).astype(f32)


def stage_l0(inputs, b, h):
    perm = np.r_[np.arange(h * HALF, (h + 1) * HALF),
                 np.arange((1 - h) * HALF, (2 - h) * HALF)]
    x0 = np.asarray(inputs["L0_attn_q"], f32)[b]
    cosT, sinT = rope_tables()
    cq = np.ascontiguousarray(cosT[:, h * HALF:(h + 1) * HALF]).reshape(16, 128, HALF)
    sq = np.ascontiguousarray(sinT[:, h * HALF:(h + 1) * HALF]).reshape(16, 128, HALF)
    ck = np.ascontiguousarray(cosT[:, perm]).reshape(16, 128, S)
    sk = np.ascontiguousarray(sinT[:, perm]).reshape(16, 128, S)
    qglob = h * HALF + np.arange(HALF)
    mask = np.where(perm[None, :] <= qglob[:, None], 1.0, 0.0).astype(bf16)
    A = np.asarray(inputs["L0_ffn_A"], f32)
    Bw = np.asarray(inputs["L0_ffn_Bw"], f32)
    at = np.ascontiguousarray(A.transpose(2, 0, 1).reshape(D, 128))
    bw_t = np.ascontiguousarray(Bw.transpose(0, 2, 1).reshape(128, D))
    return {
        "x0p": np.ascontiguousarray(x0[perm]),
        "wq": tile4(np.asarray(inputs["L0_attn_W_q"], f32)),
        "wk": tile4(np.asarray(inputs["L0_attn_W_k"], f32)),
        "wv": np.ascontiguousarray(np.asarray(inputs["L0_attn_W_v"], f32).reshape(16, 128, D)),
        "wo": tile4(np.asarray(inputs["L0_attn_W_o"], f32)),
        "wbase": tile4(np.ascontiguousarray(np.asarray(inputs["L0_ffn_W_base"], f32).T)),
        "bbase": colmat(np.asarray(inputs["L0_ffn_b_base"], f32), 16),
        "rwt": np.ascontiguousarray(np.asarray(inputs["L0_ffn_router_W"], f32).T
                                    .reshape(16, 128, 8).transpose(1, 0, 2).reshape(128, 128)),
        "at4": np.ascontiguousarray(at.reshape(16, 128, 128)),
        "bwt": np.ascontiguousarray(bw_t.reshape(128, 16, 128).transpose(1, 0, 2)),
        "cosq": cq, "sinq": sq, "cosk": ck, "sink": sk,
        "maskq": np.ascontiguousarray(mask.reshape(8, 128, S)),
        "ident_i": IDENT, "perm_i": perm_pairswap(), "rep8_i": rep8_mat(),
    }


def stage_c(inputs, x2_full, b, h):
    win = np.asarray(inputs["L1_attn_W_in"], f32)
    if x2_full is None:
        x2f = None
    wx = np.asarray(inputs["L1_attn_W_x"], f32)
    cw = np.asarray(inputs["L1_attn_conv_w"], f32)[h * 2048:(h + 1) * 2048]
    cb = np.asarray(inputs["L1_attn_conv_b"], f32)[h * 2048:(h + 1) * 2048]
    sl = slice(h * 2048, (h + 1) * 2048)
    return {
        **({} if x2_full is None else
           {"x2f": np.ascontiguousarray(x2_full[b].reshape(16, 128, D))}),
        "winx": tile4(np.ascontiguousarray(win[sl].T)),
        "winz": tile4(np.ascontiguousarray(win[4096 + h * 2048: 4096 + (h + 1) * 2048].T)),
        "wxdt": np.ascontiguousarray(wx[0:128, sl].T.reshape(16, 128, 128)),
        "wxB": np.ascontiguousarray(wx[128:144, sl].T.reshape(16, 128, 16)),
        "wxC": np.ascontiguousarray(wx[144:160, sl].T.reshape(16, 128, 16)),
        "convw": np.ascontiguousarray(cw.reshape(16, 128, 4).transpose(1, 0, 2).reshape(128, 64)),
        "convb": colmat(cb, 16),
        "ident_i": IDENT,
    }


def stage_d(inputs, c_outs, b, h):
    me, other = 2 * b + h, 2 * b + (1 - h)
    sl = slice(h * 2048, (h + 1) * 2048)
    alog = np.asarray(inputs["L1_attn_A_log"], f32)[sl]
    return {
        "xc_i": c_outs[me]["xc_o"],
        "zs_i": c_outs[me]["zs_o"],
        "xdt0": c_outs[2 * b]["xdt_o"],
        "xdt1": c_outs[2 * b + 1]["xdt_o"],
        "xB0": c_outs[2 * b]["xB_o"],
        "xB1": c_outs[2 * b + 1]["xB_o"],
        "xC0": c_outs[2 * b]["xC_o"],
        "xC1": c_outs[2 * b + 1]["xC_o"],
        "wdt": np.ascontiguousarray(np.asarray(inputs["L1_attn_W_dt"], f32)[sl].T),
        "bdt": colmat(np.asarray(inputs["L1_attn_b_dt"], f32)[sl], 16),
        "alog": np.ascontiguousarray(alog.reshape(16, 128, 16).transpose(1, 0, 2).reshape(128, 256)),
        "dsk": colmat(np.asarray(inputs["L1_attn_D"], f32)[sl], 16),
        "ones_i": np.ones((1, 128), bf16),
    }


def stage_e(inputs, x2_full, yg_list, b, h):
    ygf = np.concatenate([yg_list[2 * b], yg_list[2 * b + 1]], axis=0)
    ygf = np.ascontiguousarray(ygf[:, :, h * HALF:(h + 1) * HALF])
    return {
        "x2own": np.ascontiguousarray(x2_full[b, h * HALF:(h + 1) * HALF].reshape(8, 128, D)),
        "ygf": ygf,
        "wout": tile4(np.ascontiguousarray(np.asarray(inputs["L1_attn_W_out"], f32).T)),
        "bout": colmat(np.asarray(inputs["L1_attn_b_out"], f32), 16),
        "w1": tile4(np.ascontiguousarray(np.asarray(inputs["L1_ffn_W1"], f32).T)),
        "b1": colmat(np.asarray(inputs["L1_ffn_b1"], f32), 32),
        "w2": tile4(np.ascontiguousarray(np.asarray(inputs["L1_ffn_W2"], f32).T)),
        "b2": colmat(np.asarray(inputs["L1_ffn_b2"], f32), 32),
        "w3": tile4(np.ascontiguousarray(np.asarray(inputs["L1_ffn_W3"], f32).T)),
        "b3": colmat(np.asarray(inputs["L1_ffn_b3"], f32), 16),
        "ident_i": IDENT,
    }


def stage_fused(inputs, b, h):
    st = stage_l0(inputs, b, h)
    st.pop("x2f", None)
    c = stage_c(inputs, None, b, h)
    c.pop("ident_i")
    st.update(c)
    d = {
        "wdt": np.ascontiguousarray(np.asarray(inputs["L1_attn_W_dt"], f32)[h * 2048:(h + 1) * 2048].T),
        "bdt": colmat(np.asarray(inputs["L1_attn_b_dt"], f32)[h * 2048:(h + 1) * 2048], 16),
        "alog": np.ascontiguousarray(np.asarray(inputs["L1_attn_A_log"], f32)[h * 2048:(h + 1) * 2048]
                                     .reshape(16, 128, 16).transpose(1, 0, 2).reshape(128, 256)),
        "dsk": colmat(np.asarray(inputs["L1_attn_D"], f32)[h * 2048:(h + 1) * 2048], 16),
        "wout": tile4(np.ascontiguousarray(
            np.asarray(inputs["L1_attn_W_out"], f32).T[h * 2048:(h + 1) * 2048, :])),
        "bout": colmat(np.asarray(inputs["L1_attn_b_out"], f32), 16),
        "w1": tile4(np.ascontiguousarray(np.asarray(inputs["L1_ffn_W1"], f32).T)),
        "b1": colmat(np.asarray(inputs["L1_ffn_b1"], f32), 32),
        "w2": tile4(np.ascontiguousarray(np.asarray(inputs["L1_ffn_W2"], f32).T)),
        "b2": colmat(np.asarray(inputs["L1_ffn_b2"], f32), 32),
        "w3": tile4(np.ascontiguousarray(np.asarray(inputs["L1_ffn_W3"], f32).T)),
        "b3": colmat(np.asarray(inputs["L1_ffn_b3"], f32), 16),
    }
    st.update(d)
    return st


GROUPS = [[0, 1], [2, 3], [4, 5], [6, 7]]


def kernel(**inputs):
    cores = list(range(NCORES))
    nc = build_fused(GROUPS)
    res = run_bass_kernel_spmd(
        nc, [stage_fused(inputs, c // 2, c % 2) for c in cores], core_ids=cores)
    x4 = np.zeros((B, S, D), f32)
    for c in cores:
        b, h = c // 2, c % 2
        x4[b, h * HALF:(h + 1) * HALF] = res.results[c]["x4out"].reshape(HALF, D)
    return x4


def kernel_unfused(**inputs):
    cores = list(range(NCORES))
    nc0 = build_l0()
    r0 = run_bass_kernel_spmd(
        nc0, [stage_l0(inputs, c // 2, c % 2) for c in cores], core_ids=cores)
    x2 = np.zeros((B, S, D), f32)
    for c in cores:
        b, h = c // 2, c % 2
        x2[b, h * HALF:(h + 1) * HALF] = r0.results[c]["x2out"].reshape(HALF, D)
    ncc = build_c()
    rc = run_bass_kernel_spmd(
        ncc, [stage_c(inputs, x2, c // 2, c % 2) for c in cores], core_ids=cores)
    c_outs = rc.results
    ncd = build_d()
    rd = run_bass_kernel_spmd(
        ncd, [stage_d(inputs, c_outs, c // 2, c % 2) for c in cores], core_ids=cores)
    yg_list = [rd.results[c]["yg_o"] for c in cores]
    nce = build_e()
    re_ = run_bass_kernel_spmd(
        nce, [stage_e(inputs, x2, yg_list, c // 2, c % 2) for c in cores], core_ids=cores)
    x4 = np.zeros((B, S, D), f32)
    for c in cores:
        b, h = c // 2, c % 2
        x4[b, h * HALF:(h + 1) * HALF] = re_.results[c]["x4out"].reshape(HALF, D)
    return x4


# version 13
# speedup vs baseline: 1.0127x; 1.0127x over previous
import math
from contextlib import ExitStack

import numpy as np
import ml_dtypes

import concourse.bass as bass
import concourse.mybir as mybir
from concourse.bass_utils import run_bass_kernel_spmd

F32 = mybir.dt.float32
BF16 = mybir.dt.bfloat16
AF = mybir.ActivationFunctionType
ALU = mybir.AluOpType
AX = mybir.AxisListType
bf16 = ml_dtypes.bfloat16
f32 = np.float32

NCORES = 8
B, S, D = 4, 2048, 2048
HALF = S // 2
SCALE_ATT = 1.0 / math.sqrt(D)
EPS = 1e-5

ENG_SEM = {"tensor": "sT", "vector": "sV", "scalar": "sS", "gpsimd": "sP", "sync": "sY"}


class Prog:

    DMA_POOL = {"dS": 10, "dP": 6, "dV": 4, "dA": 4}

    def __init__(self, nc, stack):
        self.nc = nc
        self.q = {e: [] for e in ENG_SEM}
        self.sems = {}
        names = ["sT", "sV", "sS", "sP", "sY"] + [f"cc{i}" for i in range(4)]
        for qn, n in self.DMA_POOL.items():
            names += [f"{qn}{i}" for i in range(n)]
        for name in names:
            self.sems[name] = stack.enter_context(nc.semaphore(name))
        self.cnt = {k: 0 for k in self.sems}
        self.dma_idx = {qn: 0 for qn in self.DMA_POOL}
        self.deps = {}
        self.known = {e: {} for e in ENG_SEM}

    def _waits_for(self, eng, reads, writes):
        want = {}
        for k in reads:
            d = self.deps.get(k)
            if d and d["w"]:
                s, c = d["w"]
                want[s] = max(want.get(s, 0), c)
        for k in writes:
            d = self.deps.get(k)
            if d:
                if d["w"]:
                    s, c = d["w"]
                    want[s] = max(want.get(s, 0), c)
                for s, c in d["r"]:
                    want[s] = max(want.get(s, 0), c)
        kn = self.known[eng]
        out = {}
        for s, c in want.items():
            if kn.get(s, 0) < c:
                out[s] = c
                kn[s] = c
        return out

    def op(self, eng, build, reads=(), writes=(), dma=None):
        if dma:
            i = self.dma_idx[dma]
            self.dma_idx[dma] = (i + 1) % self.DMA_POOL[dma]
            sem_name = f"{dma}{i}"
            inc = 16
            reads = list(reads) + [f"__{sem_name}"]
            writes = list(writes) + [f"__{sem_name}"]
        else:
            sem_name = ENG_SEM[eng]
            inc = 1
        waits = self._waits_for(eng, reads, writes)
        self.cnt[sem_name] += inc
        myc = self.cnt[sem_name]
        sems = self.sems

        def closure(e, waits=waits, build=build, sem=sems[sem_name], inc=inc):
            for s, c in waits.items():
                e.wait_ge(sems[s], c)
            build(e).then_inc(sem, inc)

        self.q[eng].append(closure)
        for k in reads:
            self.deps.setdefault(k, {"w": None, "r": []})["r"].append((sem_name, myc))
        for k in writes:
            self.deps[k] = {"w": (sem_name, myc), "r": []}
        return (sem_name, myc)

    def merge_key(self, new, olds):
        rs = []
        for k in olds:
            d = self.deps.pop(k, None)
            if d:
                if d["w"]:
                    rs.append(d["w"])
                rs.extend(d["r"])
        self.deps[new] = {"w": None, "r": rs}

    def barrier_wait(self, eng, dep):
        s, c = dep
        if self.known[eng].get(s, 0) >= c:
            return
        self.known[eng][s] = c
        sems = self.sems
        self.q[eng].append(lambda e, s=s, c=c: e.wait_ge(sems[s], c))

    def finalize(self, blk, final_eng="sync"):
        for qn in ("dA", "dP"):
            for i in range(self.DMA_POOL[qn]):
                sn = f"{qn}{i}"
                if self.cnt[sn]:
                    self.barrier_wait(final_eng, (sn, self.cnt[sn]))
        for eng in ENG_SEM:
            lst = self.q[eng]
            if not lst:
                continue
            getattr(blk, eng)(lambda e, lst=list(lst): [f(e) for f in lst])

    def dma(self, eng, out, in_, reads=(), writes=(), q="dS"):
        return self.op(eng, lambda e: e.dma_start(out=out, in_=in_),
                       reads=reads, writes=writes, dma=q)

    def mm(self, out, lhsT, rhs, start, stop, reads=(), writes=()):
        return self.op("tensor",
                       lambda e: e.matmul(out, lhsT, rhs, start=start, stop=stop),
                       reads=reads, writes=writes)

    def act(self, out, in_, func, bias=0.0, scale=1.0, accum=None, reads=(), writes=()):
        return self.op("scalar",
                       lambda e: e.activation(out, in_, func, bias=bias, scale=scale,
                                              accum_out=accum),
                       reads=reads, writes=writes)

    def vop(self, fn, reads=(), writes=()):
        return self.op("vector", fn, reads=reads, writes=writes)

    def cc(self, idx, kind, op, groups, in_ap, out_ap, reads=(), writes=()):
        sem_name = f"cc{idx}"
        waits = self._waits_for("gpsimd", reads, writes)
        sems = self.sems

        def closure(e, waits=waits):
            for s, c in waits.items():
                e.wait_ge(sems[s], c)
            e.collective_compute(kind, op, replica_groups=groups,
                                 ins=[in_ap], outs=[out_ap]).then_inc(sems[sem_name], 1)

        self.q["gpsimd"].append(closure)
        for k in reads:
            self.deps.setdefault(k, {"w": None, "r": []})["r"].append((sem_name, 1))
        for k in writes:
            self.deps[k] = {"w": (sem_name, 1), "r": []}


def emit_ln(p, x_ap, x_key, xn_ap, xn_key, stats, slot):
    c = lambda t: stats[t][:, slot:slot + 1]
    ks = f"ln{slot}_"
    p.vop(lambda e: e.reduce_sum(c("sum"), x_ap, axis=AX.X),
          reads=[x_key], writes=[ks + "s"])
    p.act(xn_ap, x_ap, AF.Square, accum=c("sumsq"),
          reads=[x_key], writes=[ks + "q", xn_key])
    p.vop(lambda e: e.tensor_scalar_mul(c("nm"), c("sum"), -1.0 / D),
          reads=[ks + "s"], writes=[ks + "nm"])
    p.vop(lambda e: e.tensor_mul(c("msq"), c("nm"), c("nm")),
          reads=[ks + "nm"], writes=[ks + "m2"])
    p.vop(lambda e: e.scalar_tensor_tensor(c("var"), c("sumsq"), 1.0 / D, c("msq"),
                                           ALU.mult, ALU.subtract),
          reads=[ks + "q", ks + "m2"], writes=[ks + "v"])
    p.act(c("var"), c("var"), AF.Ln, bias=stats["cst"][:, 0:1],
          reads=[ks + "v", "cst"], writes=[ks + "v"])
    p.act(c("rstd"), c("var"), AF.Exp, scale=stats["cst"][:, 1:2],
          reads=[ks + "v", "cst"], writes=[ks + "r"])
    p.vop(lambda e: e.tensor_scalar(xn_ap, x_ap, c("nm"), c("rstd"), ALU.add, ALU.mult),
          reads=[x_key, ks + "nm", ks + "r"], writes=[xn_key])


def ln_stats_tensors(nc, stack, nslots=2):
    names = ["sum", "sumsq", "nm", "msq", "var", "rstd"]
    d = {n: stack.enter_context(nc.sbuf_tensor(f"lns_{n}", [128, nslots], F32))
         for n in names}
    d["cst"] = stack.enter_context(nc.sbuf_tensor("lns_cst", [128, 4], F32))
    return d


def emit_consts(p, stats):
    for i, v in enumerate([EPS, -0.5, SCALE_ATT, -1.0]):
        p.vop(lambda e, i=i, v=v: e.memset(stats["cst"][:, i:i + 1], v),
              writes=["cst"] if i == 3 else [])



def build_l0():
    nc = bass.Bass("TRN2", target_bir_lowering=False, debug=False)
    dtin = lambda n, sh, ty: nc.dram_tensor(n, sh, ty, kind="ExternalInput").ap()
    x0p = dtin("x0p", [S, D], F32)
    wq = dtin("wq", [16, 16, 128, 128], F32)
    wk = dtin("wk", [16, 16, 128, 128], F32)
    wv = dtin("wv", [16, 128, D], F32)
    wo = dtin("wo", [16, 16, 128, 128], F32)
    wbase = dtin("wbase", [16, 16, 128, 128], F32)
    bbase = dtin("bbase", [128, 16], F32)
    rwt = dtin("rwt", [128, 128], F32)
    at4 = dtin("at4", [16, 128, 128], F32)
    bwt = dtin("bwt", [16, 128, 128], F32)
    cosq = dtin("cosq", [16, 128, HALF], BF16)
    sinq = dtin("sinq", [16, 128, HALF], BF16)
    cosk = dtin("cosk", [16, 128, S], BF16)
    sink = dtin("sink", [16, 128, S], BF16)
    maskq = dtin("maskq", [8, 128, S], BF16)
    ident_i = dtin("ident_i", [128, 128], BF16)
    perm_i = dtin("perm_i", [128, 128], BF16)
    rep8_i = dtin("rep8_i", [8, 128], BF16)
    x2out = nc.dram_tensor("x2out", [8, 128, D], F32, kind="ExternalOutput").ap()
    vhbm = nc.dram_tensor("vhbm", [16, 128, D], BF16, kind="Internal").ap()
    x1hbm = nc.dram_tensor("x1hbm", [8, 128, D], F32, kind="Internal").ap()

    with ExitStack() as stack:
        sb = lambda n, sh, ty: stack.enter_context(nc.sbuf_tensor(n, sh, ty))
        psa = lambda n, sh, ty: stack.enter_context(nc.psum_tensor(n, sh, ty))
        A1 = sb("A1", [128, 32768], BF16)
        A2 = sb("A2", [128, 32768], BF16)
        A3 = sb("A3", [128, 16384], BF16)
        xt = sb("xt", [128, D], F32)
        xt2 = sb("xt2", [128, D], F32)
        xn_tm = sb("xn_tm", [128, D], BF16)
        stats = ln_stats_tensors(nc, stack)
        wstg = sb("wstg", [128, 1024], F32)
        wbf = sb("wbf", [128, 2048], BF16)
        bw_f32 = sb("bw_f32", [128, 128], F32)
        vt_st = [sb(f"vt{i}", [128, D], BF16) for i in range(2)]
        qk_raw = sb("qk_raw", [128, 512], BF16)
        rope_t1 = sb("rope_t1", [128, 512], BF16)
        cs_t = sb("cs_t", [128, S], BF16)
        sn_t = sb("sn_t", [128, S], BF16)
        colv = sb("colv", [128, 8], F32)
        bbase_sb = sb("bbase_sb", [128, 16], F32)
        rw_sb = sb("rw_sb", [128, 16 * 8], BF16)
        oh_tm = sb("oh_tm", [128, 8], BF16)
        ohT = A1[0:8, 0:1024]
        of_sb = A1[:, 14336:14848]
        bwbf = A1[:, 14848:14976]
        ident = sb("ident", [128, 128], BF16)
        perm = sb("perm", [128, 128], BF16)
        repm = sb("repm", [8, 128], BF16)
        pf = [psa(f"pf{i}", [128, 512], F32) for i in range(6)]
        pb = [psa(f"pb{i}", [128, 1024], BF16) for i in range(2)]

        blk = stack.enter_context(nc.Block())
        p = Prog(nc, stack)
        emit_consts(p, stats)

        xnf = lambda k: slice(k * S, (k + 1) * S)
        KF = lambda k: slice(k * S, (k + 1) * S)
        QF = lambda k: slice(k * HALF, (k + 1) * HALF)

        p.dma("sync", ident[:, :], ident_i[:, :], writes=["ident"])
        p.dma("sync", perm[:, :], perm_i[:, :], writes=["perm"])
        p.dma("sync", repm[:, :], rep8_i[:, :], writes=["repm"])
        p.dma("sync", bbase_sb[:, :], bbase[:, :], writes=["bbase"])
        p.dma("sync", wstg[:, 0:128], rwt[:, :], writes=["wstg"])
        p.vop(lambda e: e.tensor_copy(rw_sb[:, :], wstg[:, 0:128]), reads=["wstg"], writes=["rw"])

        for rb in range(16):
            p.dma("sync", xt[:, :], x0p[rb * 128:(rb + 1) * 128, :], writes=["xt"])
            emit_ln(p, xt[:, :], "xt", xn_tm[:, :], "xn_tm", stats, rb % 2)
            for dtl in range(16):
                pbk = f"pb{dtl % 2}"
                pbt = pb[dtl % 2]
                p.op("tensor", lambda e, s=xn_tm[:, dtl * 128:(dtl + 1) * 128],
                     o=pbt[:, 0:128]: e.transpose(o, s, ident[:, :]),
                     reads=["xn_tm", "ident", pbk + "r"], writes=[pbk])
                p.vop(lambda e, i=pbt[:, 0:128],
                      o=A1[:, dtl * S + rb * 128: dtl * S + (rb + 1) * 128]:
                      e.tensor_copy(o, i),
                      reads=[pbk], writes=[f"xnf{dtl}", pbk + "r"])

        for k in range(16):
            for hh in range(2):
                p.dma("sync", wstg[:, :], wv[k][:, hh * 1024:(hh + 1) * 1024],
                      writes=["wstg"])
                p.vop(lambda e, o=A2[:, k * S + hh * 1024: k * S + (hh + 1) * 1024]:
                      e.tensor_copy(o, wstg[:, :]),
                      reads=["wstg"], writes=[f"wvb{k}"])
        for rb in range(16):
            vbuf = vt_st[rb % 2]
            vkey = f"vt{rb % 2}"
            for dvc in range(4):
                bank = pf[dvc % 2]
                bk = f"PF{dvc % 2}"
                for k in range(16):
                    p.mm(bank[:, :], A1[:, k * S + rb * 128: k * S + rb * 128 + 128],
                         A2[:, k * S + dvc * 512: k * S + dvc * 512 + 512],
                         start=(k == 0), stop=(k == 15),
                         reads=[f"xnf{k}", f"wvb{k}", bk + "r"],
                         writes=[bk] if k == 15 else [])
                p.vop(lambda e, i=bank[:, :], o=vbuf[:, dvc * 512:(dvc + 1) * 512]:
                      e.tensor_copy(o, i),
                      reads=[bk], writes=[vkey + "w", bk + "r"])
            p.dma("gpsimd", vhbm[rb], vbuf[:, :], reads=[vkey + "w"], writes=[f"vhbm{rb}", vkey], q="dP")

        p.merge_key("kfA2", [f"wvb{k}" for k in range(16)])

        def proj_rope(wdram, cosd, sind, ntc, dstA, dstCol, dstKey, tag):
            for m in range(16):
                for hh in range(2):
                    p.dma("sync", wstg[:, :].rearrange("p (k j) -> p k j", k=8),
                          wdram[m][hh * 8:(hh + 1) * 8].rearrange("k pp j -> pp k j"),
                          writes=["wstg"])
                    p.vop(lambda e, o=wbf[:, hh * 1024:(hh + 1) * 1024]:
                          e.tensor_copy(o, wstg[:, :]),
                          reads=["wstg"], writes=["wbf"])
                p.dma("sync", cs_t[:, 0:ntc * 512], cosd[m], writes=["cs"])
                p.dma("sync", sn_t[:, 0:ntc * 512], sind[m], writes=["sn"])
                for tc in range(ntc):
                    bank = pf[tc % 2]
                    bk = f"PF{tc % 2}"
                    for k in range(16):
                        p.mm(bank[:, :], wbf[:, k * 128:(k + 1) * 128],
                             A1[:, k * S + tc * 512: k * S + tc * 512 + 512],
                             start=(k == 0), stop=(k == 15),
                             reads=["wbf", f"xnf{k}", bk + "r"],
                             writes=[bk] if k == 15 else [])
                    p.act(qk_raw[:, :], bank[:, :], AF.Copy,
                          reads=[bk], writes=["qkr", bk + "r"])
                    p.op("tensor", lambda e, o=pf[3][:, :]:
                         e.matmul(o, perm[:, :], qk_raw[:, :], start=True, stop=True),
                         reads=["perm", "qkr", "PF3r"], writes=["PF3"])
                    p.vop(lambda e, c=cs_t[:, tc * 512:(tc + 1) * 512]:
                          e.tensor_mul(rope_t1[:, :], qk_raw[:, :], c),
                          reads=["qkr", "cs"], writes=["rt1"])
                    p.vop(lambda e, sl=sn_t[:, tc * 512:(tc + 1) * 512],
                          i=pf[3][:, :]: e.tensor_mul(qk_raw[:, :], i, sl),
                          reads=["PF3", "sn"], writes=["qkr", "PF3r"])
                    p.vop(lambda e, o=dstA[:, m * dstCol + tc * 512:
                                          m * dstCol + tc * 512 + 512]:
                          e.tensor_add(o, rope_t1[:, :], qk_raw[:, :]),
                          reads=["rt1", "qkr"], writes=[dstKey(m)])

        proj_rope(wk, cosk, sink, 4, A2, S, lambda m: f"kf{m}", "K")
        proj_rope(wq, cosq, sinq, 2, A3, HALF, lambda m: f"qf{m}", "Q")

        p.merge_key("attA1", [f"xnf{k}" for k in range(16)] + ["xn_tm"])
        PT = lambda kt: slice(kt * 512, kt * 512 + 512)
        ATT = lambda k: slice(16384 + k * HALF, 16384 + (k + 1) * HALF)
        PRB = slice(8192, 10240)
        PRB2 = slice(10240, 12288)
        MSK = slice(12288, 14336)

        first_att = ["attA1"]
        for qs in range(2):
            for qc4 in range(4):
                qc = qs * 4 + qc4
                p.dma("sync", A1[:, MSK], maskq[qc], writes=["mask"] + first_att)
                first_att = []
                for kc in range(4):
                    bank = pf[kc]
                    bk = f"PF{kc}"
                    for k in range(16):
                        p.mm(bank[:, :], A3[:, k * HALF + qc * 128: k * HALF + qc * 128 + 128],
                             A2[:, k * S + kc * 512: k * S + kc * 512 + 512],
                             start=(k == 0), stop=(k == 15),
                             reads=[f"qf{k}", f"kf{k}", bk + "r"],
                             writes=[bk] if k == 15 else [])
                    p.vop(lambda e, i=bank[:, :], o=colv[:, kc:kc + 1]:
                          e.reduce_max(o, i, axis=AX.X),
                          reads=[bk], writes=[f"mx{kc}"])
                p.vop(lambda e: e.tensor_max(colv[:, 4:5], colv[:, 0:1], colv[:, 1:2]),
                      reads=["mx0", "mx1"], writes=["mx01"])
                p.vop(lambda e: e.tensor_max(colv[:, 5:6], colv[:, 2:3], colv[:, 3:4]),
                      reads=["mx2", "mx3"], writes=["mx23"])
                p.vop(lambda e: e.tensor_max(colv[:, 6:7], colv[:, 4:5], colv[:, 5:6]),
                      reads=["mx01", "mx23"], writes=["mx"])
                p.vop(lambda e: e.tensor_scalar_mul(colv[:, 1:2], colv[:, 6:7], -SCALE_ATT),
                      reads=["mx"], writes=["nmx", "mx1"])
                for kc in range(4):
                    bank = pf[kc]
                    bk = f"PF{kc}"
                    p.act(A1[:, 8192 + kc * 512:8192 + (kc + 1) * 512], bank[:, :],
                          AF.Exp, bias=colv[:, 1:2], scale=stats["cst"][:, 2:3],
                          reads=[bk, "nmx", "cst"], writes=["prb", bk + "r"])
                p.vop(lambda e, i=A1[:, PRB], mk=A1[:, MSK], o=A1[:, PRB2]:
                      e.tensor_mul(o, i, mk),
                      reads=["prb", "mask"], writes=["prb2"])
                p.vop(lambda e, i=A1[:, PRB2]: e.reduce_sum(colv[:, 2:3], i, axis=AX.X),
                      reads=["prb2"], writes=["den", "mx2"])
                p.vop(lambda e: e.reciprocal(colv[:, 3:4], colv[:, 2:3]),
                      reads=["den"], writes=["rden", "mx3"])
                p.vop(lambda e, i=A1[:, PRB2], o=A1[:, PRB]:
                      e.tensor_scalar_mul(o, i, colv[:, 3:4]),
                      reads=["prb2", "rden", "prb"], writes=["prb"])
                for kt in range(16):
                    pbk = f"pb{kt % 2}"
                    pbt = pb[kt % 2]
                    p.op("tensor", lambda e, s=A1[:, 8192 + kt * 128:8192 + kt * 128 + 128],
                         o=pbt[:, 0:128]: e.transpose(o, s, ident[:, :]),
                         reads=["prb", "ident", pbk + "r"], writes=[pbk])
                    p.vop(lambda e, i=pbt[:, 0:128],
                          o=A1[:, kt * 512 + qc4 * 128: kt * 512 + qc4 * 128 + 128]:
                          e.tensor_copy(o, i),
                          reads=[pbk], writes=[f"pT{kt}", pbk + "r"])
            for dvg in range(4):
                for kt in range(16):
                    vbuf = vt_st[kt % 2]
                    vkey = f"vt{kt % 2}"
                    p.dma("sync", vbuf[:, 0:512], vhbm[kt][:, dvg * 512:(dvg + 1) * 512],
                          reads=[f"vhbm{kt}"], writes=[vkey])
                    for dvq in range(4):
                        bank = pf[2 + dvq]
                        bk = f"PF{2 + dvq}"
                        p.mm(bank[:, :], vbuf[:, dvq * 128:(dvq + 1) * 128],
                             A1[:, PT(kt)],
                             start=(kt == 0), stop=(kt == 15),
                             reads=[vkey, f"pT{kt}", bk + "r"],
                             writes=[bk] if kt == 15 else [])
                for dvq in range(4):
                    dvt = dvg * 4 + dvq
                    bank = pf[2 + dvq]
                    bk = f"PF{2 + dvq}"
                    p.vop(lambda e, i=bank[:, :],
                          o=A1[:, 16384 + dvt * HALF + qs * 512:16384 + dvt * HALF + qs * 512 + 512]:
                          e.tensor_copy(o, i),
                          reads=[bk], writes=[f"att{dvt}", bk + "r"])

        for tc in range(2):
            p.merge_key(f"otm{tc}", [f"pT{kt}" for kt in range(16)] if tc == 0 else [f"otmdone{0}"])
            for m in range(16):
                for hh in range(2):
                    p.dma("sync", wstg[:, :].rearrange("p (k j) -> p k j", k=8),
                          wo[m][hh * 8:(hh + 1) * 8].rearrange("k pp j -> pp k j"),
                          writes=["wstg"])
                    p.vop(lambda e, o=wbf[:, hh * 1024:(hh + 1) * 1024]:
                          e.tensor_copy(o, wstg[:, :]),
                          reads=["wstg"], writes=["wbf"])
                bank = pf[m % 2]
                bk = f"PF{m % 2}"
                for k in range(16):
                    p.mm(bank[:, :], wbf[:, k * 128:(k + 1) * 128],
                         A1[:, 16384 + k * HALF + tc * 512:16384 + k * HALF + tc * 512 + 512],
                         start=(k == 0), stop=(k == 15),
                         reads=["wbf", f"att{k}", bk + "r"],
                         writes=[bk] if k == 15 else [])
                p.act(of_sb[:, :], bank[:, :], AF.Copy, reads=[bk], writes=["of", bk + "r"])
                for sub in range(4):
                    pbk = f"pb{sub % 2}"
                    pbt = pb[sub % 2]
                    p.op("tensor", lambda e, s=of_sb[:, sub * 128:(sub + 1) * 128],
                         o=pbt[:, 0:128]: e.transpose(o, s, ident[:, :]),
                         reads=["of", "ident", pbk + "r"], writes=[pbk])
                    p.vop(lambda e, i=pbt[:, 0:128],
                          o=A1[:, sub * 2048 + m * 128: sub * 2048 + m * 128 + 128]:
                          e.tensor_copy(o, i),
                          reads=[pbk], writes=[f"otm{tc}", pbk + "r"])
            for sub in range(4):
                rb = tc * 4 + sub
                p.dma("sync", xt[:, :], x0p[rb * 128:(rb + 1) * 128, :], writes=["xt"])
                p.vop(lambda e, i=A1[:, sub * 2048:(sub + 1) * 2048]:
                      e.tensor_add(xt2[:, :], xt[:, :], i),
                      reads=["xt", f"otm{tc}"], writes=["xt2"])
                p.dma("gpsimd", x1hbm[rb], xt2[:, :], reads=["xt2"], writes=[f"x1h{rb}", "xt2r"], q="dP")
            p.deps[f"otmdone{0}"] = {"w": None, "r": list(p.deps[f"otm{tc}"]["r"])}

        p.merge_key("xnf1A3", [f"qf{m}" for m in range(16)])
        p.merge_key("ffnA2", [f"kf{m}" for m in range(16)])
        for rb in range(8):
            p.dma("sync", xt[:, :], x1hbm[rb], reads=[f"x1h{rb}", "xt2r"], writes=["xt"])
            emit_ln(p, xt[:, :], "xt", xn_tm[:, :], "xn_tm", stats, rb % 2)
            for dtl in range(16):
                pbk = f"pb{dtl % 2}"
                pbt = pb[dtl % 2]
                p.op("tensor", lambda e, s=xn_tm[:, dtl * 128:(dtl + 1) * 128],
                     o=pbt[:, 0:128]: e.transpose(o, s, ident[:, :]),
                     reads=["xn_tm", "ident", pbk + "r"], writes=[pbk])
                p.vop(lambda e, i=pbt[:, 0:128],
                      o=A3[:, dtl * HALF + rb * 128: dtl * HALF + rb * 128 + 128]:
                      e.tensor_copy(o, i),
                      reads=[pbk], writes=["xnf1A3", pbk + "r"])
            bank = pf[4]
            bk = "PF4"
            for k in range(16):
                p.mm(bank[:, 0:8], A3[:, k * HALF + rb * 128: k * HALF + rb * 128 + 128],
                     rw_sb[:, k * 8:(k + 1) * 8],
                     start=(k == 0), stop=(k == 15),
                     reads=["xnf1A3", "rw", bk + "r"],
                     writes=[bk] if k == 15 else [])
            p.vop(lambda e, i=bank[:, 0:8]: e.reduce_max(colv[:, 4:5], i, axis=AX.X),
                  reads=[bk], writes=["rmx"])
            p.vop(lambda e, i=bank[:, 0:8]: e.tensor_scalar(oh_tm[:, :], i, colv[:, 4:5], 2.0,
                                                            ALU.is_ge, ALU.mult),
                  reads=[bk, "rmx"], writes=["oh", bk + "r"])
            p.op("tensor", lambda e, o=pb[0][0:8, 0:128]:
                 e.transpose(o, oh_tm[:, :], ident[:, :]),
                 reads=["oh", "ident", "pb0r"], writes=["pb0"])
            p.vop(lambda e, i=pb[0][0:8, 0:128], o=ohT[:, rb * 128:(rb + 1) * 128]:
                  e.tensor_copy(o, i),
                  reads=["pb0"], writes=["ohT", "pb0r"])
        for rc in range(2):
            p.op("tensor", lambda e, o=pf[3][:, :],
                 r=ohT[:, rc * 512:(rc + 1) * 512]:
                 e.matmul(o, repm[:, :], r, start=True, stop=True),
                 reads=["repm", "ohT", "PF3r"], writes=["PF3"])
            p.vop(lambda e, i=pf[3][:, :], o=A2[:, rc * 512:rc * 512 + 512]:
                  e.tensor_copy(o, i),
                  reads=["PF3"], writes=["maskfm", "PF3r"] + (["ffnA2"] if rc == 0 else []))
        for hh in range(2):
            p.dma("sync", wstg[:, :].rearrange("p (k j) -> p k j", k=8),
                  at4[hh * 8:(hh + 1) * 8].rearrange("k pp j -> pp k j"),
                  writes=["wstg"])
            p.vop(lambda e, o=wbf[:, hh * 1024:(hh + 1) * 1024]:
                  e.tensor_copy(o, wstg[:, :]),
                  reads=["wstg"], writes=["wbf"])
        for rc in range(2):
            bank = pf[4]
            bk = "PF4"
            for k in range(16):
                p.mm(bank[:, :], wbf[:, k * 128:(k + 1) * 128],
                     A3[:, k * HALF + rc * 512: k * HALF + rc * 512 + 512],
                     start=(k == 0), stop=(k == 15),
                     reads=["wbf", "xnf1A3", bk + "r"],
                     writes=[bk] if k == 15 else [])
            p.vop(lambda e, i=bank[:, :], mk=A2[:, rc * 512:rc * 512 + 512],
                  o=A2[:, 1024 + rc * 512:1024 + rc * 512 + 512]:
                  e.tensor_mul(o, i, mk),
                  reads=[bk, "maskfm"], writes=["hm", bk + "r"])
        for tc in range(2):
            p.merge_key(f"otm2_{tc}", ["maskfm"] if tc == 0 else ["otm2done"])
            for m in range(16):
                for hh in range(2):
                    p.dma("sync", wstg[:, :].rearrange("p (k j) -> p k j", k=8),
                          wbase[m][hh * 8:(hh + 1) * 8].rearrange("k pp j -> pp k j"),
                          writes=["wstg"])
                    p.vop(lambda e, o=wbf[:, hh * 1024:(hh + 1) * 1024]:
                          e.tensor_copy(o, wstg[:, :]),
                          reads=["wstg"], writes=["wbf"])
                p.dma("sync", bw_f32[:, :], bwt[m], writes=["bwstg"])
                p.vop(lambda e: e.tensor_copy(bwbf[:, :], bw_f32[:, :]),
                      reads=["bwstg"], writes=["bwbf"])
                bank = pf[m % 2]
                bk = f"PF{m % 2}"
                for k in range(16):
                    p.mm(bank[:, :], wbf[:, k * 128:(k + 1) * 128],
                         A3[:, k * HALF + tc * 512: k * HALF + tc * 512 + 512],
                         start=(k == 0), stop=False,
                         reads=["wbf", "xnf1A3", bk + "r"], writes=[])
                p.mm(bank[:, :], bwbf[:, :],
                     A2[:, 1024 + tc * 512:1024 + tc * 512 + 512],
                     start=False, stop=True,
                     reads=["bwbf", "hm", bk + "r"], writes=[bk])
                p.act(of_sb[:, :], bank[:, :], AF.Identity, bias=bbase_sb[:, m:m + 1],
                      reads=[bk, "bbase"], writes=["of", bk + "r"])
                for sub in range(4):
                    pbk = f"pb{sub % 2}"
                    pbt = pb[sub % 2]
                    p.op("tensor", lambda e, s=of_sb[:, sub * 128:(sub + 1) * 128],
                         o=pbt[:, 0:128]: e.transpose(o, s, ident[:, :]),
                         reads=["of", "ident", pbk + "r"], writes=[pbk])
                    p.vop(lambda e, i=pbt[:, 0:128],
                          o=A2[:, 2048 + sub * 2048 + m * 128: 2048 + sub * 2048 + m * 128 + 128]:
                          e.tensor_copy(o, i),
                          reads=[pbk], writes=[f"otm2_{tc}", pbk + "r"])
            for sub in range(4):
                rb = tc * 4 + sub
                p.dma("sync", xt[:, :], x1hbm[rb], reads=[f"x1h{rb}"], writes=["xt"])
                p.vop(lambda e, i=A2[:, 2048 + sub * 2048: 2048 + (sub + 1) * 2048]:
                      e.tensor_add(xt2[:, :], xt[:, :], i),
                      reads=["xt", f"otm2_{tc}"], writes=["xt2"])
                p.dma("gpsimd", x2out[rb], xt2[:, :], reads=["xt2"], writes=["xt2r"], q="dP")
            p.deps["otm2done"] = {"w": None, "r": list(p.deps[f"otm2_{tc}"]["r"])}

        p.finalize(blk)
    return nc



def build_c():
    nc = bass.Bass("TRN2", target_bir_lowering=False, debug=False)
    dtin = lambda n, sh, ty: nc.dram_tensor(n, sh, ty, kind="ExternalInput").ap()
    x2f = dtin("x2f", [16, 128, D], F32)
    winx = dtin("winx", [16, 16, 128, 128], F32)
    winz = dtin("winz", [16, 16, 128, 128], F32)
    wxdt = dtin("wxdt", [16, 128, 128], F32)
    wxB = dtin("wxB", [16, 128, 16], F32)
    wxC = dtin("wxC", [16, 128, 16], F32)
    convw = dtin("convw", [128, 64], F32)
    convb = dtin("convb", [128, 16], F32)
    ident_i = dtin("ident_i", [128, 128], BF16)
    xc_o = nc.dram_tensor("xc_o", [16, 128, S], BF16, kind="ExternalOutput").ap()
    zs_o = nc.dram_tensor("zs_o", [16, 128, S], BF16, kind="ExternalOutput").ap()
    xdt_o = nc.dram_tensor("xdt_o", [128, S], F32, kind="ExternalOutput").ap()
    xB_o = nc.dram_tensor("xB_o", [16, S], F32, kind="ExternalOutput").ap()
    xC_o = nc.dram_tensor("xC_o", [16, S], F32, kind="ExternalOutput").ap()

    with ExitStack() as stack:
        sb = lambda n, sh, ty: stack.enter_context(nc.sbuf_tensor(n, sh, ty))
        psa = lambda n, sh, ty: stack.enter_context(nc.psum_tensor(n, sh, ty))
        A1 = sb("A1", [128, 32768], BF16)
        xt = sb("xt", [128, D], F32)
        xn_tm = sb("xn_tm", [128, D], BF16)
        stats = ln_stats_tensors(nc, stack)
        wstg = sb("wstg", [128, 1024], F32)
        wbf = sb("wbf", [128, 2048], BF16)
        wsm = sb("wsm", [128, 160], BF16)
        wsm_f = sb("wsm_f", [128, 160], F32)
        xraw = sb("xraw", [128, S], BF16)
        cnvA = sb("cnvA", [128, S], BF16)
        cnvB = sb("cnvB", [128, S], BF16)
        xct = sb("xct", [128, S], BF16)
        zst = sb("zst", [128, S], BF16)
        cw_sb = sb("cw_sb", [128, 64], F32)
        cb_sb = sb("cb_sb", [128, 16], F32)
        xdt_sb = sb("xdt_sb", [128, S], F32)
        xB_sb = sb("xB_sb", [16, S], F32)
        xC_sb = sb("xC_sb", [16, S], F32)
        ident = sb("ident", [128, 128], BF16)
        pf = [psa(f"pf{i}", [128, 512], F32) for i in range(6)]
        pb = [psa(f"pb{i}", [128, 1024], BF16) for i in range(2)]

        blk = stack.enter_context(nc.Block())
        p = Prog(nc, stack)
        emit_consts(p, stats)

        p.dma("sync", ident[:, :], ident_i[:, :], writes=["ident"])
        p.dma("sync", cw_sb[:, :], convw[:, :], writes=["cw"])
        p.dma("sync", cb_sb[:, :], convb[:, :], writes=["cb"])
        p.vop(lambda e: e.memset(xdt_sb[:, :], 0.0), writes=["xdt"])
        p.vop(lambda e: e.memset(xB_sb[:, :], 0.0), writes=["xB"])
        p.vop(lambda e: e.memset(xC_sb[:, :], 0.0), writes=["xC"])

        for rb in range(16):
            p.dma("sync", xt[:, :], x2f[rb], writes=["xt"])
            emit_ln(p, xt[:, :], "xt", xn_tm[:, :], "xn_tm", stats, rb % 2)
            for dtl in range(16):
                pbk = f"pb{dtl % 2}"
                pbt = pb[dtl % 2]
                p.op("tensor", lambda e, s=xn_tm[:, dtl * 128:(dtl + 1) * 128],
                     o=pbt[:, 0:128]: e.transpose(o, s, ident[:, :]),
                     reads=["xn_tm", "ident", pbk + "r"], writes=[pbk])
                p.vop(lambda e, i=pbt[:, 0:128],
                      o=A1[:, dtl * S + rb * 128: dtl * S + (rb + 1) * 128]:
                      e.tensor_copy(o, i),
                      reads=[pbk], writes=[f"xnf{dtl}", pbk + "r"])

        def proj(wdram, m, dstT, dstKey, func, biascol, bias_reads):
            for hh in range(2):
                p.dma("sync", wstg[:, :].rearrange("p (k j) -> p k j", k=8),
                      wdram[m][hh * 8:(hh + 1) * 8].rearrange("k pp j -> pp k j"),
                      writes=["wstg"])
                p.vop(lambda e, o=wbf[:, hh * 1024:(hh + 1) * 1024]:
                      e.tensor_copy(o, wstg[:, :]),
                      reads=["wstg"], writes=["wbf"])
            for lch in range(4):
                bank = pf[lch % 2]
                bk = f"PF{lch % 2}"
                for k in range(16):
                    p.mm(bank[:, :], wbf[:, k * 128:(k + 1) * 128],
                         A1[:, k * S + lch * 512: k * S + lch * 512 + 512],
                         start=(k == 0), stop=(k == 15),
                         reads=["wbf", f"xnf{k}", bk + "r"],
                         writes=[bk] if k == 15 else [])
                p.act(dstT[:, lch * 512:(lch + 1) * 512], bank[:, :], func,
                      bias=biascol if biascol is not None else 0.0,
                      reads=[bk] + bias_reads, writes=[dstKey, bk + "r"])

        for dxt in range(16):
            proj(winx, dxt, xraw, "xraw", AF.Copy, None, [])
            wc = lambda j: cw_sb[:, dxt * 4 + j:dxt * 4 + j + 1]
            p.vop(lambda e, w=wc(3): e.tensor_scalar_mul(cnvA[:, :], xraw[:, :], w),
                  reads=["xraw", "cw"], writes=["cA"])
            p.vop(lambda e, w=wc(2): e.scalar_tensor_tensor(cnvB[:, 1:S], xraw[:, 0:S - 1], w,
                                                            cnvA[:, 1:S], ALU.mult, ALU.add),
                  reads=["xraw", "cA", "cw"], writes=["cB"])
            p.vop(lambda e: e.tensor_copy(cnvB[:, 0:1], cnvA[:, 0:1]),
                  reads=["cA"], writes=["cB"])
            p.vop(lambda e, w=wc(1): e.scalar_tensor_tensor(cnvA[:, 2:S], xraw[:, 0:S - 2], w,
                                                            cnvB[:, 2:S], ALU.mult, ALU.add),
                  reads=["xraw", "cB", "cw", "cA"], writes=["cA"])
            p.vop(lambda e: e.tensor_copy(cnvA[:, 0:2], cnvB[:, 0:2]),
                  reads=["cB"], writes=["cA"])
            p.vop(lambda e, w=wc(0): e.scalar_tensor_tensor(cnvB[:, 3:S], xraw[:, 0:S - 3], w,
                                                            cnvA[:, 3:S], ALU.mult, ALU.add),
                  reads=["xraw", "cA", "cw", "cB"], writes=["cB"])
            p.vop(lambda e: e.tensor_copy(cnvB[:, 0:3], cnvA[:, 0:3]),
                  reads=["cA"], writes=["cB"])
            p.act(xct[:, :], cnvB[:, :], AF.Silu, bias=cb_sb[:, dxt:dxt + 1],
                  reads=["cB", "cb"], writes=["xct"])
            p.dma("gpsimd", xc_o[dxt], xct[:, :], reads=["xct"], writes=[f"xc{dxt}"], q="dP")
            proj(winz, dxt, zst, "zst", AF.Silu, None, [])
            p.dma("gpsimd", zs_o[dxt], zst[:, :], reads=["zst"], writes=[f"zs{dxt}"], q="dP")
            p.dma("sync", wsm_f[:, 0:128], wxdt[dxt], writes=["wsmf"])
            p.dma("sync", wsm_f[:, 128:144], wxB[dxt], writes=["wsmf2"])
            p.dma("sync", wsm_f[:, 144:160], wxC[dxt], writes=["wsmf3"])
            p.vop(lambda e: e.tensor_copy(wsm[:, :], wsm_f[:, :]),
                  reads=["wsmf", "wsmf2", "wsmf3"], writes=["wsm"])
            for lch in range(4):
                p.mm(pf[2][:, :], wsm[:, 0:128], xct[:, lch * 512:(lch + 1) * 512],
                     start=True, stop=True, reads=["wsm", "xct", "PF2r"], writes=["PF2"])
                p.vop(lambda e, o=xdt_sb[:, lch * 512:(lch + 1) * 512], i=pf[2][:, :]:
                      e.tensor_add(o, o, i),
                      reads=["PF2", "xdt"], writes=["xdt", "PF2r"])
                p.mm(pf[3][0:16, :], wsm[:, 128:144], xct[:, lch * 512:(lch + 1) * 512],
                     start=True, stop=True, reads=["wsm", "xct", "PF3r"], writes=["PF3"])
                p.vop(lambda e, o=xB_sb[:, lch * 512:(lch + 1) * 512], i=pf[3][0:16, :]:
                      e.tensor_add(o, o, i),
                      reads=["PF3", "xB"], writes=["xB", "PF3r"])
                p.mm(pf[4][0:16, :], wsm[:, 144:160], xct[:, lch * 512:(lch + 1) * 512],
                     start=True, stop=True, reads=["wsm", "xct", "PF4r"], writes=["PF4"])
                p.vop(lambda e, o=xC_sb[:, lch * 512:(lch + 1) * 512], i=pf[4][0:16, :]:
                      e.tensor_add(o, o, i),
                      reads=["PF4", "xC"], writes=["xC", "PF4r"])

        p.dma("gpsimd", xdt_o[:, :], xdt_sb[:, :], reads=["xdt"], writes=["xdto"], q="dA")
        p.dma("gpsimd", xB_o[:, :], xB_sb[:, :], reads=["xB"], writes=["xBo"], q="dA")
        p.dma("gpsimd", xC_o[:, :], xC_sb[:, :], reads=["xC"], writes=["xCo"], q="dA")
        p.finalize(blk)
    return nc



def build_d():
    nc = bass.Bass("TRN2", target_bir_lowering=False, debug=False)
    dtin = lambda n, sh, ty: nc.dram_tensor(n, sh, ty, kind="ExternalInput").ap()
    xc_i = dtin("xc_i", [16, 128, S], BF16)
    zs_i = dtin("zs_i", [16, 128, S], BF16)
    xdt0 = dtin("xdt0", [128, S], F32)
    xdt1 = dtin("xdt1", [128, S], F32)
    xB0 = dtin("xB0", [16, S], F32)
    xB1 = dtin("xB1", [16, S], F32)
    xC0 = dtin("xC0", [16, S], F32)
    xC1 = dtin("xC1", [16, S], F32)
    wdt = dtin("wdt", [128, S], F32)
    bdt = dtin("bdt", [128, 16], F32)
    alog = dtin("alog", [128, 256], F32)
    dsk = dtin("dsk", [128, 16], F32)
    ones_i = dtin("ones_i", [1, 128], BF16)
    yg_o = nc.dram_tensor("yg_o", [16, 128, S], BF16, kind="ExternalOutput").ap()
    bc_h = nc.dram_tensor("bc_h", [16, S], BF16, kind="Internal").ap()
    cc_h = nc.dram_tensor("cc_h", [16, S], BF16, kind="Internal").ap()

    LC = 1024

    with ExitStack() as stack:
        sb = lambda n, sh, ty: stack.enter_context(nc.sbuf_tensor(n, sh, ty))
        psa = lambda n, sh, ty: stack.enter_context(nc.psum_tensor(n, sh, ty))
        BB = sb("BB", [128, 16 * LC], BF16)
        CB = sb("CB", [128, 16 * LC], BF16)
        stats = ln_stats_tensors(nc, stack)
        dtc = sb("dtc", [128, S], F32)
        dt_bf = sb("dt_bf", [128, S], BF16)
        bcc = sb("bcc", [16, S], F32)
        ccc = sb("ccc", [16, S], F32)
        bcb = sb("bcb", [16, S], BF16)
        ccb = sb("ccb", [16, S], BF16)
        bfl = sb("bfl", [1, LC], BF16)
        wdt_bf = sb("wdt_bf", [128, S], BF16)
        wstg = sb("wstg", [128, 1024], F32)
        acol = sb("acol", [128, 256], F32)
        bdt_sb = sb("bdt_sb", [128, 16], F32)
        dsk_sb = sb("dsk_sb", [128, 16], F32)
        delta = sb("delta", [128, LC], F32)
        absx = sb("absx", [128, 512], F32)
        relux = sb("relux", [128, 512], F32)
        expx = sb("expx", [128, 512], F32)
        xct = sb("xct", [128, LC], BF16)
        zst = sb("zst", [128, LC], BF16)
        dxb = sb("dxb", [128, LC], BF16)
        dA = [sb(f"dA{i}", [128, LC], BF16) for i in range(2)]
        ub = [sb(f"ub{i}", [128, LC], BF16) for i in range(2)]
        sn = [sb(f"sn{i}", [128, LC], BF16) for i in range(2)]
        csb = sb("csb", [128, LC], BF16)
        yacc = sb("yacc", [128, LC], BF16)
        y2 = sb("y2", [128, LC], BF16)
        stc = sb("stc", [128, 256], F32)
        ones1 = sb("ones1", [1, 128], BF16)
        pf = [psa(f"pf{i}", [128, 512], F32) for i in range(6)]

        blk = stack.enter_context(nc.Block())
        p = Prog(nc, stack)
        emit_consts(p, stats)

        p.dma("sync", ones1[:, :], ones_i[:, :], writes=["ones"])
        p.dma("sync", bdt_sb[:, :], bdt[:, :], writes=["bdt"])
        p.dma("sync", dsk_sb[:, :], dsk[:, :], writes=["dsk"])
        p.dma("sync", acol[:, :], alog[:, :], writes=["acol"])
        p.act(acol[:, :], acol[:, :], AF.Exp, reads=["acol"], writes=["acol"])
        p.vop(lambda e: e.tensor_scalar_mul(acol[:, :], acol[:, :], -1.0),
              reads=["acol"], writes=["acol"])
        p.dma("sync", dtc[:, :], xdt0[:, :], writes=["dtc"])
        p.dma("sync", wstg[:, :], xdt1[:, 0:1024], writes=["wstg"])
        p.vop(lambda e: e.tensor_add(dtc[:, 0:1024], dtc[:, 0:1024], wstg[:, :]),
              reads=["wstg", "dtc"], writes=["dtc"])
        p.dma("sync", wstg[:, :], xdt1[:, 1024:2048], writes=["wstg"])
        p.vop(lambda e: e.tensor_add(dtc[:, 1024:2048], dtc[:, 1024:2048], wstg[:, :]),
              reads=["wstg", "dtc"], writes=["dtc"])
        p.vop(lambda e: e.tensor_copy(dt_bf[:, :], dtc[:, :]), reads=["dtc"], writes=["dtbf"])
        p.dma("sync", bcc[:, :], xB0[:, :], writes=["bcc"])
        p.dma("sync", ccc[:, :], xC0[:, :], writes=["ccc"])
        xB1_st = sb("xB1_st", [16, S], F32)
        p.dma("sync", xB1_st[:, :], xB1[:, :], writes=["xb1s"])
        p.vop(lambda e: e.tensor_add(bcc[:, :], bcc[:, :], xB1_st[:, :]),
              reads=["xb1s", "bcc"], writes=["bcc"])
        p.dma("sync", xB1_st[:, :], xC1[:, :], writes=["xb1s"])
        p.vop(lambda e: e.tensor_add(ccc[:, :], ccc[:, :], xB1_st[:, :]),
              reads=["xb1s", "ccc"], writes=["ccc"])
        p.vop(lambda e: e.tensor_copy(bcb[:, :], bcc[:, :]), reads=["bcc"], writes=["bcb"])
        p.vop(lambda e: e.tensor_copy(ccb[:, :], ccc[:, :]), reads=["ccc"], writes=["ccb"])
        p.dma("gpsimd", bc_h[:, :], bcb[:, :], reads=["bcb"], writes=["bch"], q="dP")
        p.dma("gpsimd", cc_h[:, :], ccb[:, :], reads=["ccb"], writes=["cch"], q="dP")
        p.dma("sync", wstg[:, :], wdt[:, 0:1024], writes=["wstg"])
        p.vop(lambda e: e.tensor_copy(wdt_bf[:, 0:1024], wstg[:, :]),
              reads=["wstg"], writes=["wdtb"])
        p.dma("sync", wstg[:, :], wdt[:, 1024:2048], writes=["wstg"])
        p.vop(lambda e: e.tensor_copy(wdt_bf[:, 1024:2048], wstg[:, :]),
              reads=["wstg"], writes=["wdtb"])

        for lc in range(2):
            for n in range(16):
                p.dma("sync", bfl[0:1, :], bc_h[n:n + 1, lc * LC:(lc + 1) * LC],
                      reads=["bch"], writes=["bfl"])
                for c2 in range(2):
                    p.mm(pf[5][:, :], ones1[0:1, :], bfl[0:1, c2 * 512:(c2 + 1) * 512],
                         start=True, stop=True, reads=["ones", "bfl", "PF5r"], writes=["PF5"])
                    p.vop(lambda e, o=BB[:, n * LC + c2 * 512: n * LC + c2 * 512 + 512],
                          i=pf[5][:, :]: e.tensor_copy(o, i),
                          reads=["PF5"], writes=[f"BB{n}", "PF5r"])
                p.dma("sync", bfl[0:1, :], cc_h[n:n + 1, lc * LC:(lc + 1) * LC],
                      reads=["cch"], writes=["bfl"])
                for c2 in range(2):
                    p.mm(pf[5][:, :], ones1[0:1, :], bfl[0:1, c2 * 512:(c2 + 1) * 512],
                         start=True, stop=True, reads=["ones", "bfl", "PF5r"], writes=["PF5"])
                    p.vop(lambda e, o=CB[:, n * LC + c2 * 512: n * LC + c2 * 512 + 512],
                          i=pf[5][:, :]: e.tensor_copy(o, i),
                          reads=["PF5"], writes=[f"CB{n}", "PF5r"])
            for dxt in range(16):
                p.dma("sync", xct[:, :], xc_i[dxt][:, lc * LC:(lc + 1) * LC], writes=["xct"])
                p.dma("sync", zst[:, :], zs_i[dxt][:, lc * LC:(lc + 1) * LC], writes=["zst"])
                for c2 in range(2):
                    bank = pf[c2 % 2]
                    bk = f"PF{c2 % 2}"
                    p.mm(bank[:, :], wdt_bf[:, dxt * 128:(dxt + 1) * 128],
                         dt_bf[:, lc * LC + c2 * 512: lc * LC + c2 * 512 + 512],
                         start=True, stop=True,
                         reads=["wdtb", "dtbf", bk + "r"], writes=[bk])
                    bcol = bdt_sb[:, dxt:dxt + 1]
                    p.act(absx[:, :], bank[:, :], AF.Abs, bias=bcol,
                          reads=[bk, "bdt"], writes=["absx"])
                    p.vop(lambda e, i=bank[:, :], bc=bcol: e.tensor_scalar(
                          relux[:, :], i, bc, 0.0, ALU.add, ALU.max),
                          reads=[bk, "bdt", "absx"], writes=["relux", bk + "r"])
                    p.act(expx[:, :], absx[:, :], AF.Exp, scale=stats["cst"][:, 3:4],
                          reads=["absx", "cst"], writes=["expx"])
                    p.act(expx[:, :], expx[:, :], AF.Ln, bias=1.0,
                          reads=["expx"], writes=["expx"])
                    p.vop(lambda e, o=delta[:, c2 * 512:(c2 + 1) * 512]:
                          e.tensor_add(o, relux[:, :], expx[:, :]),
                          reads=["relux", "expx"], writes=["delta"])
                p.vop(lambda e: e.tensor_mul(dxb[:, :], delta[:, :], xct[:, :]),
                      reads=["delta", "xct"], writes=["dxb"])
                for n in range(16):
                    da = dA[n % 2]
                    dak = f"dA{n % 2}"
                    u = ub[n % 2]
                    uk = f"ub{n % 2}"
                    s_ = sn[n % 2]
                    sk_ = f"sn{n % 2}"
                    p.act(da[:, :], delta[:, :], AF.Exp,
                          scale=acol[:, dxt * 16 + n:dxt * 16 + n + 1],
                          reads=["delta", "acol", dak], writes=[dak])
                    p.vop(lambda e, bb=BB[:, n * LC:(n + 1) * LC], u_=u:
                          e.tensor_mul(u_[:, :], dxb[:, :], bb),
                          reads=["dxb", f"BB{n}", uk], writes=[uk])
                    ini = 0.0 if lc == 0 else stc[:, dxt * 16 + n:dxt * 16 + n + 1]
                    p.vop(lambda e, ini=ini, s=s_, d=da, u_=u: e.tensor_tensor_scan(
                          s[:, :], d[:, :], u_[:, :], ini, ALU.mult, ALU.add),
                          reads=[dak, uk, "stc", sk_], writes=[sk_])
                    p.vop(lambda e, o=stc[:, dxt * 16 + n:dxt * 16 + n + 1], s=s_:
                          e.tensor_copy(o, s[:, LC - 1:LC]),
                          reads=[sk_], writes=["stc"])
                    p.vop(lambda e, cb=CB[:, n * LC:(n + 1) * LC], s=s_:
                          e.tensor_mul(csb[:, :], s[:, :], cb),
                          reads=[sk_, f"CB{n}", "csb"], writes=["csb"])
                    if n == 0:
                        p.vop(lambda e: e.tensor_copy(yacc[:, :], csb[:, :]),
                              reads=["csb"], writes=["yacc"])
                    else:
                        p.vop(lambda e: e.tensor_add(yacc[:, :], yacc[:, :], csb[:, :]),
                              reads=["csb", "yacc"], writes=["yacc"])
                p.vop(lambda e, dc=dsk_sb[:, dxt:dxt + 1]: e.scalar_tensor_tensor(
                      y2[:, :], xct[:, :], dc, yacc[:, :],
                      ALU.mult, ALU.add),
                      reads=["xct", "dsk", "yacc", "y2"], writes=["y2"])
                p.vop(lambda e: e.tensor_mul(y2[:, :], y2[:, :], zst[:, :]),
                      reads=["zst", "y2"], writes=["y2"])
                p.dma("gpsimd", yg_o[dxt][:, lc * LC:(lc + 1) * LC], y2[:, :],
                      reads=["y2"], writes=[f"yg{dxt}_{lc}"], q="dA")
        p.finalize(blk)
    return nc



def build_e():
    nc = bass.Bass("TRN2", target_bir_lowering=False, debug=False)
    dtin = lambda n, sh, ty: nc.dram_tensor(n, sh, ty, kind="ExternalInput").ap()
    x2own = dtin("x2own", [8, 128, D], F32)
    ygf = dtin("ygf", [32, 128, HALF], BF16)
    wout = dtin("wout", [16, 32, 128, 128], F32)
    bout = dtin("bout", [128, 16], F32)
    w1 = dtin("w1", [32, 16, 128, 128], F32)
    b1 = dtin("b1", [128, 32], F32)
    w2 = dtin("w2", [32, 32, 128, 128], F32)
    b2 = dtin("b2", [128, 32], F32)
    w3 = dtin("w3", [16, 32, 128, 128], F32)
    b3 = dtin("b3", [128, 16], F32)
    ident_i = dtin("ident_i", [128, 128], BF16)
    x4out = nc.dram_tensor("x4out", [8, 128, D], F32, kind="ExternalOutput").ap()
    x3hbm = nc.dram_tensor("x3hbm", [8, 128, D], F32, kind="Internal").ap()

    with ExitStack() as stack:
        sb = lambda n, sh, ty: stack.enter_context(nc.sbuf_tensor(n, sh, ty))
        psa = lambda n, sh, ty: stack.enter_context(nc.psum_tensor(n, sh, ty))
        AY = sb("AY", [128, 32768], BF16)
        AH = sb("AH", [128, 32768], BF16)
        A3 = sb("A3", [128, 16384], BF16)
        xt = sb("xt", [128, D], F32)
        xt2 = sb("xt2", [128, D], F32)
        xn_tm = sb("xn_tm", [128, D], BF16)
        stats = ln_stats_tensors(nc, stack)
        wstg = sb("wstg", [128, 1024], F32)
        wbf = sb("wbf", [128, 4096], BF16)
        of_sb = sb("of_sb", [128, 512], BF16)
        bcols = sb("bcols", [128, 96], F32)
        ident = sb("ident", [128, 128], BF16)
        otm = AH[:, 0:8192]
        pf = [psa(f"pf{i}", [128, 512], F32) for i in range(6)]
        pb = [psa(f"pb{i}", [128, 1024], BF16) for i in range(2)]

        blk = stack.enter_context(nc.Block())
        p = Prog(nc, stack)
        emit_consts(p, stats)

        p.dma("sync", ident[:, :], ident_i[:, :], writes=["ident"])
        p.dma("sync", bcols[:, 0:16], bout[:, :], writes=["bc"])
        p.dma("sync", bcols[:, 16:48], b1[:, :], writes=["bc1"])
        p.dma("sync", bcols[:, 48:80], b2[:, :], writes=["bc2"])
        p.dma("sync", bcols[:, 80:96], b3[:, :], writes=["bc3"])
        for k in range(32):
            p.dma("sync", AY[:, k * 1024:(k + 1) * 1024], ygf[k], writes=[f"yg{k}"])

        def stream_tiles(wdram, m, nk, key="wbf"):
            for hh in range(nk // 8):
                p.dma("sync", wstg[:, :].rearrange("p (k j) -> p k j", k=8),
                      wdram[m][hh * 8:(hh + 1) * 8].rearrange("k pp j -> pp k j"),
                      writes=["wstg"])
                p.vop(lambda e, o=wbf[:, hh * 1024:(hh + 1) * 1024]:
                      e.tensor_copy(o, wstg[:, :]),
                      reads=["wstg"], writes=[key])

        def mlp_proj(wdram, nk, nm, src_fn, src_keys, dst_fn, dst_key_fn, func, bcol_fn,
                     bias_reads, ntc=2):
            for m in range(nm):
                stream_tiles(wdram, m, nk)
                for tc in range(ntc):
                    bank = pf[tc % 2]
                    bk = f"PF{tc % 2}"
                    for k in range(nk):
                        p.mm(bank[:, :], wbf[:, k * 128:(k + 1) * 128],
                             src_fn(k, tc),
                             start=(k == 0), stop=(k == nk - 1),
                             reads=["wbf", src_keys(k), bk + "r"],
                             writes=[bk] if k == nk - 1 else [])
                    p.act(dst_fn(m, tc), bank[:, :], func, bias=bcol_fn(m),
                          reads=[bk] + bias_reads, writes=[dst_key_fn(m), bk + "r"])

        for tc in range(2):
            for m in range(16):
                stream_tiles(wout, m, 32)
                bank = pf[m % 2]
                bk = f"PF{m % 2}"
                for k in range(32):
                    p.mm(bank[:, :], wbf[:, k * 128:(k + 1) * 128],
                         AY[:, k * HALF + tc * 512: k * HALF + tc * 512 + 512],
                         start=(k == 0), stop=(k == 31),
                         reads=["wbf", f"yg{k}", bk + "r"],
                         writes=[bk] if k == 31 else [])
                p.act(of_sb[:, :], bank[:, :], AF.Identity, bias=bcols[:, m:m + 1],
                      reads=[bk, "bc"], writes=["of", bk + "r"])
                for sub in range(4):
                    pbk = f"pb{sub % 2}"
                    pbt = pb[sub % 2]
                    p.op("tensor", lambda e, s=of_sb[:, sub * 128:(sub + 1) * 128],
                         o=pbt[:, 0:128]: e.transpose(o, s, ident[:, :]),
                         reads=["of", "ident", pbk + "r"], writes=[pbk])
                    p.vop(lambda e, i=pbt[:, 0:128],
                          o=otm[:, sub * 2048 + m * 128: sub * 2048 + m * 128 + 128]:
                          e.tensor_copy(o, i),
                          reads=[pbk], writes=["otm", pbk + "r"])
            for sub in range(4):
                rb = tc * 4 + sub
                p.dma("sync", xt[:, :], x2own[rb], writes=["xt"])
                p.vop(lambda e, i=otm[:, sub * 2048:(sub + 1) * 2048]:
                      e.tensor_add(xt2[:, :], xt[:, :], i),
                      reads=["xt", "otm", "xt2r"], writes=["xt2"])
                p.dma("gpsimd", x3hbm[rb], xt2[:, :], reads=["xt2"], writes=[f"x3h{rb}", "xt2r"], q="dP")
                emit_ln(p, xt2[:, :], "xt2", xn_tm[:, :], "xn_tm", stats, rb % 2)
                for dtl in range(16):
                    pbk = f"pb{dtl % 2}"
                    pbt = pb[dtl % 2]
                    p.op("tensor", lambda e, s=xn_tm[:, dtl * 128:(dtl + 1) * 128],
                         o=pbt[:, 0:128]: e.transpose(o, s, ident[:, :]),
                         reads=["xn_tm", "ident", pbk + "r"], writes=[pbk])
                    p.vop(lambda e, i=pbt[:, 0:128],
                          o=A3[:, dtl * HALF + rb * 128: dtl * HALF + rb * 128 + 128]:
                          e.tensor_copy(o, i),
                          reads=[pbk], writes=[f"x3f{dtl}", pbk + "r"])

        mlp_proj(w1, 16, 32,
                 lambda k, tc: A3[:, k * HALF + tc * 512: k * HALF + tc * 512 + 512],
                 lambda k: f"x3f{k}",
                 lambda m, tc: AH[:, m * HALF + tc * 512: m * HALF + tc * 512 + 512],
                 lambda m: f"h1_{m}", AF.Relu, lambda m: bcols[:, 16 + m:17 + m], ["bc1"])
        p.merge_key("h2A", [f"yg{k}" for k in range(32)])
        for m in range(32):
            stream_tiles(w2, m, 32)
            for tc in range(2):
                bank = pf[tc % 2]
                bk = f"PF{tc % 2}"
                for k in range(32):
                    p.mm(bank[:, :], wbf[:, k * 128:(k + 1) * 128],
                         AH[:, k * HALF + tc * 512: k * HALF + tc * 512 + 512],
                         start=(k == 0), stop=(k == 31),
                         reads=["wbf", f"h1_{k}", bk + "r"],
                         writes=[bk] if k == 31 else [])
                p.act(AY[:, m * HALF + tc * 512: m * HALF + tc * 512 + 512],
                      bank[:, :], AF.Relu, bias=bcols[:, 48 + m:49 + m],
                      reads=[bk, "bc2"], writes=[f"h2_{m}", bk + "r"]
                      + (["h2A"] if False else []))
        p.merge_key("otm3", [f"h1_{m}" for m in range(32)] + ["otm"])
        for tc in range(2):
            for m in range(16):
                stream_tiles(w3, m, 32)
                bank = pf[m % 2]
                bk = f"PF{m % 2}"
                for k in range(32):
                    p.mm(bank[:, :], wbf[:, k * 128:(k + 1) * 128],
                         AY[:, k * HALF + tc * 512: k * HALF + tc * 512 + 512],
                         start=(k == 0), stop=(k == 31),
                         reads=["wbf", f"h2_{k}", bk + "r"],
                         writes=[bk] if k == 31 else [])
                p.act(of_sb[:, :], bank[:, :], AF.Identity, bias=bcols[:, 80 + m:81 + m],
                      reads=[bk, "bc3"], writes=["of", bk + "r"])
                for sub in range(4):
                    pbk = f"pb{sub % 2}"
                    pbt = pb[sub % 2]
                    p.op("tensor", lambda e, s=of_sb[:, sub * 128:(sub + 1) * 128],
                         o=pbt[:, 0:128]: e.transpose(o, s, ident[:, :]),
                         reads=["of", "ident", pbk + "r"], writes=[pbk])
                    p.vop(lambda e, i=pbt[:, 0:128],
                          o=AH[:, sub * 2048 + m * 128: sub * 2048 + m * 128 + 128]:
                          e.tensor_copy(o, i),
                          reads=[pbk], writes=["otm3", pbk + "r"])
            for sub in range(4):
                rb = tc * 4 + sub
                p.dma("sync", xt[:, :], x3hbm[rb], reads=[f"x3h{rb}"], writes=["xt"])
                p.vop(lambda e, i=AH[:, sub * 2048:(sub + 1) * 2048]:
                      e.tensor_add(xt2[:, :], xt[:, :], i),
                      reads=["xt", "otm3", "xt2r"], writes=["xt2"])
                p.dma("gpsimd", x4out[rb], xt2[:, :], reads=["xt2"], writes=["xt2r"], q="dA")
        p.finalize(blk)
    return nc



def build_fused(groups):
    nc = bass.Bass("TRN2", target_bir_lowering=False, debug=False)
    dtin = lambda n, sh, ty: nc.dram_tensor(n, sh, ty, kind="ExternalInput").ap()
    x0p = dtin("x0p", [S, D], F32)
    wq = dtin("wq", [16, 16, 128, 128], F32)
    wk = dtin("wk", [16, 16, 128, 128], F32)
    wv = dtin("wv", [16, 128, D], F32)
    wo = dtin("wo", [16, 16, 128, 128], F32)
    wbase = dtin("wbase", [16, 16, 128, 128], F32)
    bbase = dtin("bbase", [128, 16], F32)
    rwt = dtin("rwt", [128, 128], F32)
    at4 = dtin("at4", [16, 128, 128], F32)
    bwt = dtin("bwt", [16, 128, 128], F32)
    cosq = dtin("cosq", [16, 128, HALF], BF16)
    sinq = dtin("sinq", [16, 128, HALF], BF16)
    cosk = dtin("cosk", [16, 128, S], BF16)
    sink = dtin("sink", [16, 128, S], BF16)
    maskq = dtin("maskq", [8, 128, S], BF16)
    ident_i = dtin("ident_i", [128, 128], BF16)
    perm_i = dtin("perm_i", [128, 128], BF16)
    rep8_i = dtin("rep8_i", [8, 128], BF16)
    winx = dtin("winx", [16, 16, 128, 128], F32)
    winz = dtin("winz", [16, 16, 128, 128], F32)
    wxdt = dtin("wxdt", [16, 128, 128], F32)
    wxB = dtin("wxB", [16, 128, 16], F32)
    wxC = dtin("wxC", [16, 128, 16], F32)
    convw = dtin("convw", [128, 64], F32)
    convb = dtin("convb", [128, 16], F32)
    wdt = dtin("wdt", [128, S], F32)
    bdt = dtin("bdt", [128, 16], F32)
    alog = dtin("alog", [128, 256], F32)
    dsk = dtin("dsk", [128, 16], F32)
    wout = dtin("wout", [16, 16, 128, 128], F32)
    bout = dtin("bout", [128, 16], F32)
    w1 = dtin("w1", [32, 16, 128, 128], F32)
    b1 = dtin("b1", [128, 32], F32)
    w2 = dtin("w2", [32, 32, 128, 128], F32)
    b2 = dtin("b2", [128, 32], F32)
    w3 = dtin("w3", [16, 32, 128, 128], F32)
    b3 = dtin("b3", [128, 16], F32)
    x4out = nc.dram_tensor("x4out", [8, 128, D], F32, kind="ExternalOutput").ap()
    it = lambda n, sh, ty: nc.dram_tensor(n, sh, ty, kind="Internal").ap()
    vhbm = it("vhbm", [16, 128, D], BF16)
    x1hbm = it("x1hbm", [8, 128, D], F32)
    x2h = it("x2h", [8, 128, D], F32)
    ag_in = it("ag_in", [8 * 128, D], BF16)
    ag_out = it("ag_out", [16 * 128, D], BF16)
    xc_h = it("xc_h", [16, 128, S], BF16)
    zs_h = it("zs_h", [16, 128, S], BF16)
    xdbl_in = it("xdbl_in", [160, S], F32)
    xdbl_out = it("xdbl_out", [160, S], F32)
    bc16_h = it("bc16_h", [32, S], BF16)
    yg_h = it("yg_h", [16, 128, S], BF16)
    rs_in = it("rs_in", [4096, HALF], F32)
    rs_out = it("rs_out", [2048, HALF], F32)

    LC = 1024

    with ExitStack() as stack:
        sb = lambda n, sh, ty: stack.enter_context(nc.sbuf_tensor(n, sh, ty))
        psa = lambda n, sh, ty: stack.enter_context(nc.psum_tensor(n, sh, ty))
        A1 = sb("A1", [128, 32768], BF16)
        A2 = sb("A2", [128, 32768], BF16)
        A3 = sb("A3", [128, 16384], BF16)
        xt = sb("xt", [128, D], F32)
        xt2 = sb("xt2", [128, D], F32)
        xn_tm = sb("xn_tm", [128, D], BF16)
        stats = ln_stats_tensors(nc, stack)
        wsg = [sb("wstgA", [128, 1024], F32), sb("wstgB", [128, 1024], F32)]
        wbf = sb("wbf", [128, 4096], BF16)
        bw_f32 = sb("bw_f32", [128, 128], F32)
        vt0 = sb("vt0", [128, D], BF16)
        qk_raw = sb("qk_raw", [128, 512], BF16)
        rope_t1 = sb("rope_t1", [128, 512], BF16)
        colv = sb("colv", [128, 8], F32)
        bbase_sb = sb("bbase_sb", [128, 16], F32)
        rw_sb = sb("rw_sb", [128, 16 * 8], BF16)
        oh_tm = sb("oh_tm", [128, 8], BF16)
        ohT = A1[0:8, 0:1024]
        of_sb = A1[:, 14336:14848]
        bwbf = A1[:, 14848:14976]
        ident = sb("ident", [128, 128], BF16)
        perm = sb("perm", [128, 128], BF16)
        repm = sb("repm", [8, 128], BF16)
        wsm = A2[:, 0:2560]
        cw_sb = sb("cw_sb", [128, 64], F32)
        cb_sb = sb("cb_sb", [128, 16], F32)
        bdt_sb = sb("bdt_sb", [128, 16], F32)
        dsk_sb = sb("dsk_sb", [128, 16], F32)
        acol = xt2[:, 1536:1792]
        stc = xt2[:, 1792:2048]
        bcols = sb("bcols", [128, 96], F32)
        of2 = A3[:, 8192:8704]
        pf = [psa(f"pf{i}", [128, 512], F32) for i in range(6)]
        pb = [psa(f"pb{i}", [128, 1024], BF16) for i in range(2)]

        blk = stack.enter_context(nc.Block())
        p = Prog(nc, stack)
        emit_consts(p, stats)

        wunit = [0]

        def wfeed(src_ap, tiled=True):
            u = wunit[0] % 4
            sg = wunit[0] % 2
            wunit[0] += 1
            if tiled:
                p.dma("sync", wsg[sg][:, :].rearrange("p (k j) -> p k j", k=8),
                      src_ap, writes=[f"wsg{sg}"])
            else:
                p.dma("sync", wsg[sg][:, :], src_ap, writes=[f"wsg{sg}"])
            p.vop(lambda e, o=wbf[:, u * 1024:(u + 1) * 1024], s=wsg[sg]:
                  e.tensor_copy(o, s[:, :]),
                  reads=[f"wsg{sg}"], writes=[f"wbfu{u}"])
            return u

        def wlhs(us, k):
            u = us[k // 8]
            return wbf[:, u * 1024 + (k % 8) * 128: u * 1024 + (k % 8) * 128 + 128], f"wbfu{u}"

        p.dma("sync", ident[:, :], ident_i[:, :], writes=["ident"])
        p.dma("sync", perm[:, :], perm_i[:, :], writes=["perm"])
        p.dma("sync", repm[:, :], rep8_i[:, :], writes=["repm"])
        p.dma("sync", bbase_sb[:, :], bbase[:, :], writes=["bbase"])
        p.dma("sync", wsg[0][:, 0:128], rwt[:, :], writes=["wsg0"])
        p.vop(lambda e: e.tensor_copy(rw_sb[:, :], wsg[0][:, 0:128]), reads=["wsg0"], writes=["rw"])

        for rb in range(16):
            xi = xt if rb % 2 == 0 else xt2
            xik = "xt" if rb % 2 == 0 else "xt2"
            p.dma("sync", xi[:, :], x0p[rb * 128:(rb + 1) * 128, :], writes=[xik])
            emit_ln(p, xi[:, :], xik, xn_tm[:, :], "xn_tm", stats, rb % 2)
            for dtl in range(16):
                pbk = f"pb{dtl % 2}"
                pbt = pb[dtl % 2]
                p.op("tensor", lambda e, s=xn_tm[:, dtl * 128:(dtl + 1) * 128],
                     o=pbt[:, 0:128]: e.transpose(o, s, ident[:, :]),
                     reads=["xn_tm", "ident", pbk + "r"], writes=[pbk])
                p.vop(lambda e, i=pbt[:, 0:128],
                      o=A1[:, dtl * S + rb * 128: dtl * S + (rb + 1) * 128]:
                      e.tensor_copy(o, i),
                      reads=[pbk], writes=[f"xnf{dtl}", pbk + "r"])

        for k in range(16):
            for hh in range(2):
                sg = (2 * k + hh) % 2
                p.dma("sync", wsg[sg][:, :], wv[k][:, hh * 1024:(hh + 1) * 1024],
                      writes=[f"wsg{sg}"])
                p.vop(lambda e, o=A2[:, k * S + hh * 1024: k * S + (hh + 1) * 1024],
                      s=wsg[sg]: e.tensor_copy(o, s[:, :]),
                      reads=[f"wsg{sg}"], writes=[f"wvb{k}"])
        for rb in range(16):
            vbuf = vt0
            vkey = "vt0"
            for dvc in range(4):
                bank = pf[dvc % 2]
                bk = f"PF{dvc % 2}"
                for k in range(16):
                    p.mm(bank[:, :], A1[:, k * S + rb * 128: k * S + rb * 128 + 128],
                         A2[:, k * S + dvc * 512: k * S + dvc * 512 + 512],
                         start=(k == 0), stop=(k == 15),
                         reads=[f"xnf{k}", f"wvb{k}", bk + "r"],
                         writes=[bk] if k == 15 else [])
                p.vop(lambda e, i=bank[:, :], o=vbuf[:, dvc * 512:(dvc + 1) * 512]:
                      e.tensor_copy(o, i),
                      reads=[bk], writes=[vkey + "w", bk + "r"])
            p.dma("gpsimd", vhbm[rb], vbuf[:, :], reads=[vkey + "w"], writes=[f"vhbm{rb}", vkey], q="dP")

        def proj_rope(wdram, cosd, sind, ntc, dstA, dstCol, dstKey, cs_t, sn_t):
            for m in range(16):
                us = [wfeed(wdram[m][hh * 8:(hh + 1) * 8].rearrange("k pp j -> pp k j"))
                      for hh in range(2)]
                p.dma("sync", cs_t[:, 0:ntc * 512], cosd[m], writes=["cs"])
                p.dma("sync", sn_t[:, 0:ntc * 512], sind[m], writes=["sn"])
                for tc in range(ntc):
                    bank = pf[tc % 2]
                    bk = f"PF{tc % 2}"
                    for k in range(16):
                        lh, lk = wlhs(us, k)
                        p.mm(bank[:, :], lh,
                             A1[:, k * S + tc * 512: k * S + tc * 512 + 512],
                             start=(k == 0), stop=(k == 15),
                             reads=[lk, f"xnf{k}", bk + "r"],
                             writes=[bk] if k == 15 else [])
                    p.act(qk_raw[:, :], bank[:, :], AF.Copy,
                          reads=[bk], writes=["qkr", bk + "r"])
                    p.op("tensor", lambda e, o=pf[3][:, :]:
                         e.matmul(o, perm[:, :], qk_raw[:, :], start=True, stop=True),
                         reads=["perm", "qkr", "PF3r"], writes=["PF3"])
                    p.vop(lambda e, c=cs_t[:, tc * 512:(tc + 1) * 512]:
                          e.tensor_mul(rope_t1[:, :], qk_raw[:, :], c),
                          reads=["qkr", "cs"], writes=["rt1"])
                    p.vop(lambda e, sl=sn_t[:, tc * 512:(tc + 1) * 512],
                          i=pf[3][:, :]: e.tensor_mul(qk_raw[:, :], i, sl),
                          reads=["PF3", "sn"], writes=["qkr", "PF3r"])
                    p.vop(lambda e, o=dstA[:, m * dstCol + tc * 512:
                                          m * dstCol + tc * 512 + 512]:
                          e.tensor_add(o, rope_t1[:, :], qk_raw[:, :]),
                          reads=["rt1", "qkr"], writes=[dstKey(m)])

        p.merge_key("kfA2", [f"wvb{k}" for k in range(16)])
        proj_rope(wk, cosk, sink, 4, A2, S, lambda m: f"kf{m}",
                  A3[:, 0:2048], A3[:, 2048:4096])
        p.merge_key("cs", ["cs", "vt0", "vt0w"])
        proj_rope(wq, cosq, sinq, 2, A3, HALF, lambda m: f"qf{m}",
                  vt0[:, 0:1024], vt0[:, 1024:2048])
        _old = [p.deps.pop(k, None) for k in ("cs", "sn")]
        _rs = []
        for _d in _old:
            if _d:
                if _d["w"]:
                    _rs.append(_d["w"])
                _rs.extend(_d["r"])
        for _k in ("vtA", "vtB"):
            p.deps[_k] = {"w": None, "r": list(_rs)}

        p.merge_key("attA1", [f"xnf{k}" for k in range(16)] + ["xn_tm"])
        PT = lambda kt: slice(kt * 512, kt * 512 + 512)
        PRB = slice(8192, 10240)
        PRB2 = slice(10240, 12288)
        MSK = slice(12288, 14336)

        first_att = ["attA1"]
        for qs in range(2):
            for qc4 in range(4):
                qc = qs * 4 + qc4
                p.dma("sync", A1[:, MSK], maskq[qc], writes=["mask"] + first_att)
                first_att = []
                for kc in range(4):
                    bank = pf[kc]
                    bk = f"PF{kc}"
                    for k in range(16):
                        p.mm(bank[:, :], A3[:, k * HALF + qc * 128: k * HALF + qc * 128 + 128],
                             A2[:, k * S + kc * 512: k * S + kc * 512 + 512],
                             start=(k == 0), stop=(k == 15),
                             reads=[f"qf{k}", f"kf{k}", bk + "r"],
                             writes=[bk] if k == 15 else [])
                    p.vop(lambda e, i=bank[:, :], o=colv[:, kc:kc + 1]:
                          e.reduce_max(o, i, axis=AX.X),
                          reads=[bk], writes=[f"mx{kc}"])
                p.vop(lambda e: e.tensor_max(colv[:, 4:5], colv[:, 0:1], colv[:, 1:2]),
                      reads=["mx0", "mx1"], writes=["mx01"])
                p.vop(lambda e: e.tensor_max(colv[:, 5:6], colv[:, 2:3], colv[:, 3:4]),
                      reads=["mx2", "mx3"], writes=["mx23"])
                p.vop(lambda e: e.tensor_max(colv[:, 6:7], colv[:, 4:5], colv[:, 5:6]),
                      reads=["mx01", "mx23"], writes=["mx"])
                p.vop(lambda e: e.tensor_scalar_mul(colv[:, 1:2], colv[:, 6:7], -SCALE_ATT),
                      reads=["mx"], writes=["nmx", "mx1"])
                for kc in range(4):
                    bank = pf[kc]
                    bk = f"PF{kc}"
                    p.act(A1[:, 8192 + kc * 512:8192 + (kc + 1) * 512], bank[:, :],
                          AF.Exp, bias=colv[:, 1:2], scale=stats["cst"][:, 2:3],
                          reads=[bk, "nmx", "cst"], writes=["prb", bk + "r"])
                p.vop(lambda e, i=A1[:, PRB], mk=A1[:, MSK], o=A1[:, PRB2]:
                      e.tensor_mul(o, i, mk),
                      reads=["prb", "mask"], writes=["prb2"])
                p.vop(lambda e, i=A1[:, PRB2]: e.reduce_sum(colv[:, 2:3], i, axis=AX.X),
                      reads=["prb2"], writes=["den", "mx2"])
                p.vop(lambda e: e.reciprocal(colv[:, 3:4], colv[:, 2:3]),
                      reads=["den"], writes=["rden", "mx3"])
                p.vop(lambda e, i=A1[:, PRB2], o=A1[:, PRB]:
                      e.tensor_scalar_mul(o, i, colv[:, 3:4]),
                      reads=["prb2", "rden", "prb"], writes=["prb"])
                for kt in range(16):
                    pbk = f"pb{kt % 2}"
                    pbt = pb[kt % 2]
                    p.op("tensor", lambda e, s=A1[:, 8192 + kt * 128:8192 + kt * 128 + 128],
                         o=pbt[:, 0:128]: e.transpose(o, s, ident[:, :]),
                         reads=["prb", "ident", pbk + "r"], writes=[pbk])
                    p.vop(lambda e, i=pbt[:, 0:128],
                          o=A1[:, kt * 512 + qc4 * 128: kt * 512 + qc4 * 128 + 128]:
                          e.tensor_copy(o, i),
                          reads=[pbk], writes=[f"pT{kt}", pbk + "r"])
            for dvg in range(4):
                for kt in range(16):
                    vbuf = vt0[:, (kt % 2) * 512:(kt % 2) * 512 + 512]
                    vkey = "vtA" if kt % 2 == 0 else "vtB"
                    p.dma("sync", vbuf, vhbm[kt][:, dvg * 512:(dvg + 1) * 512],
                          reads=[f"vhbm{kt}"], writes=[vkey])
                    for dvq in range(4):
                        bank = pf[2 + dvq]
                        bk = f"PF{2 + dvq}"
                        p.mm(bank[:, :], vbuf[:, dvq * 128:(dvq + 1) * 128],
                             A1[:, PT(kt)],
                             start=(kt == 0), stop=(kt == 15),
                             reads=[vkey, f"pT{kt}", bk + "r"],
                             writes=[bk] if kt == 15 else [])
                for dvq in range(4):
                    dvt = dvg * 4 + dvq
                    bank = pf[2 + dvq]
                    bk = f"PF{2 + dvq}"
                    p.vop(lambda e, i=bank[:, :],
                          o=A1[:, 16384 + dvt * HALF + qs * 512:16384 + dvt * HALF + qs * 512 + 512]:
                          e.tensor_copy(o, i),
                          reads=[bk], writes=[f"att{dvt}", bk + "r"])

        for tc in range(2):
            p.merge_key(f"otm{tc}", [f"pT{kt}" for kt in range(16)] if tc == 0 else ["otmdone0"])
            for m in range(16):
                us = [wfeed(wo[m][hh * 8:(hh + 1) * 8].rearrange("k pp j -> pp k j"))
                      for hh in range(2)]
                bank = pf[m % 2]
                bk = f"PF{m % 2}"
                for k in range(16):
                    lh, lk = wlhs(us, k)
                    p.mm(bank[:, :], lh,
                         A1[:, 16384 + k * HALF + tc * 512:16384 + k * HALF + tc * 512 + 512],
                         start=(k == 0), stop=(k == 15),
                         reads=[lk, f"att{k}", bk + "r"],
                         writes=[bk] if k == 15 else [])
                p.act(of_sb[:, :], bank[:, :], AF.Copy, reads=[bk], writes=["of", bk + "r"])
                for sub in range(4):
                    pbk = f"pb{sub % 2}"
                    pbt = pb[sub % 2]
                    p.op("tensor", lambda e, s=of_sb[:, sub * 128:(sub + 1) * 128],
                         o=pbt[:, 0:128]: e.transpose(o, s, ident[:, :]),
                         reads=["of", "ident", pbk + "r"], writes=[pbk])
                    p.vop(lambda e, i=pbt[:, 0:128],
                          o=A1[:, sub * 2048 + m * 128: sub * 2048 + m * 128 + 128]:
                          e.tensor_copy(o, i),
                          reads=[pbk], writes=[f"otm{tc}", pbk + "r"])
            for sub in range(4):
                rb = tc * 4 + sub
                p.dma("sync", xt[:, :], x0p[rb * 128:(rb + 1) * 128, :], writes=["xt"])
                p.vop(lambda e, i=A1[:, sub * 2048:(sub + 1) * 2048]:
                      e.tensor_add(xt2[:, :], xt[:, :], i),
                      reads=["xt", f"otm{tc}", "xt2r"], writes=["xt2"])
                p.dma("gpsimd", x1hbm[rb], xt2[:, :], reads=["xt2"], writes=[f"x1h{rb}", "xt2r"], q="dP")
            p.deps["otmdone0"] = {"w": None, "r": list(p.deps[f"otm{tc}"]["r"])}

        p.merge_key("xnf1A3", [f"qf{m}" for m in range(16)])
        p.merge_key("ffnA2", [f"kf{m}" for m in range(16)])
        for rb in range(8):
            xi = xt if rb % 2 == 0 else xt2
            xik = "xt" if rb % 2 == 0 else "xt2"
            p.dma("sync", xi[:, :], x1hbm[rb], reads=[f"x1h{rb}", "xt2r"], writes=[xik])
            emit_ln(p, xi[:, :], xik, xn_tm[:, :], "xn_tm", stats, rb % 2)
            for dtl in range(16):
                pbk = f"pb{dtl % 2}"
                pbt = pb[dtl % 2]
                p.op("tensor", lambda e, s=xn_tm[:, dtl * 128:(dtl + 1) * 128],
                     o=pbt[:, 0:128]: e.transpose(o, s, ident[:, :]),
                     reads=["xn_tm", "ident", pbk + "r"], writes=[pbk])
                p.vop(lambda e, i=pbt[:, 0:128],
                      o=A3[:, dtl * HALF + rb * 128: dtl * HALF + rb * 128 + 128]:
                      e.tensor_copy(o, i),
                      reads=[pbk], writes=["xnf1A3", pbk + "r"])
            bank = pf[4]
            bk = "PF4"
            for k in range(16):
                p.mm(bank[:, 0:8], A3[:, k * HALF + rb * 128: k * HALF + rb * 128 + 128],
                     rw_sb[:, k * 8:(k + 1) * 8],
                     start=(k == 0), stop=(k == 15),
                     reads=["xnf1A3", "rw", bk + "r"],
                     writes=[bk] if k == 15 else [])
            p.vop(lambda e, i=bank[:, 0:8]: e.reduce_max(colv[:, 4:5], i, axis=AX.X),
                  reads=[bk], writes=["rmx"])
            p.vop(lambda e, i=bank[:, 0:8]: e.tensor_scalar(oh_tm[:, :], i, colv[:, 4:5], 2.0,
                                                            ALU.is_ge, ALU.mult),
                  reads=[bk, "rmx"], writes=["oh", bk + "r"])
            p.op("tensor", lambda e, o=pb[0][0:8, 0:128]:
                 e.transpose(o, oh_tm[:, :], ident[:, :]),
                 reads=["oh", "ident", "pb0r"], writes=["pb0"])
            p.vop(lambda e, i=pb[0][0:8, 0:128], o=ohT[:, rb * 128:(rb + 1) * 128]:
                  e.tensor_copy(o, i),
                  reads=["pb0"], writes=["ohT", "pb0r"])
        for rc in range(2):
            p.op("tensor", lambda e, o=pf[3][:, :],
                 r=ohT[:, rc * 512:(rc + 1) * 512]:
                 e.matmul(o, repm[:, :], r, start=True, stop=True),
                 reads=["repm", "ohT", "PF3r"], writes=["PF3"])
            p.vop(lambda e, i=pf[3][:, :], o=A2[:, rc * 512:rc * 512 + 512]:
                  e.tensor_copy(o, i),
                  reads=["PF3"], writes=["maskfm", "PF3r"] + (["ffnA2"] if rc == 0 else []))
        us_at = [wfeed(at4[hh * 8:(hh + 1) * 8].rearrange("k pp j -> pp k j"))
                 for hh in range(2)]
        for rc in range(2):
            bank = pf[4]
            bk = "PF4"
            for k in range(16):
                lh, lk = wlhs(us_at, k)
                p.mm(bank[:, :], lh,
                     A3[:, k * HALF + rc * 512: k * HALF + rc * 512 + 512],
                     start=(k == 0), stop=(k == 15),
                     reads=[lk, "xnf1A3", bk + "r"],
                     writes=[bk] if k == 15 else [])
            p.vop(lambda e, i=bank[:, :], mk=A2[:, rc * 512:rc * 512 + 512],
                  o=A2[:, 1024 + rc * 512:1024 + rc * 512 + 512]:
                  e.tensor_mul(o, i, mk),
                  reads=[bk, "maskfm"], writes=["hm", bk + "r"])
        for tc in range(2):
            p.merge_key(f"otm2_{tc}", [] if tc == 0 else ["otm2done"])
            for m in range(16):
                us = [wfeed(wbase[m][hh * 8:(hh + 1) * 8].rearrange("k pp j -> pp k j"))
                      for hh in range(2)]
                p.dma("sync", bw_f32[:, :], bwt[m], writes=["bwstg"])
                p.vop(lambda e: e.tensor_copy(bwbf[:, :], bw_f32[:, :]),
                      reads=["bwstg"], writes=["bwbf"])
                bank = pf[m % 2]
                bk = f"PF{m % 2}"
                for k in range(16):
                    lh, lk = wlhs(us, k)
                    p.mm(bank[:, :], lh,
                         A3[:, k * HALF + tc * 512: k * HALF + tc * 512 + 512],
                         start=(k == 0), stop=False,
                         reads=[lk, "xnf1A3", bk + "r"], writes=[])
                p.mm(bank[:, :], bwbf[:, :],
                     A2[:, 1024 + tc * 512:1024 + tc * 512 + 512],
                     start=False, stop=True,
                     reads=["bwbf", "hm", bk + "r"], writes=[bk])
                p.act(of_sb[:, :], bank[:, :], AF.Identity, bias=bbase_sb[:, m:m + 1],
                      reads=[bk, "bbase"], writes=["of", bk + "r"])
                for sub in range(4):
                    pbk = f"pb{sub % 2}"
                    pbt = pb[sub % 2]
                    p.op("tensor", lambda e, s=of_sb[:, sub * 128:(sub + 1) * 128],
                         o=pbt[:, 0:128]: e.transpose(o, s, ident[:, :]),
                         reads=["of", "ident", pbk + "r"], writes=[pbk])
                    p.vop(lambda e, i=pbt[:, 0:128],
                          o=A2[:, 2048 + sub * 2048 + m * 128: 2048 + sub * 2048 + m * 128 + 128]:
                          e.tensor_copy(o, i),
                          reads=[pbk], writes=[f"otm2_{tc}", pbk + "r"])
            for sub in range(4):
                rb = tc * 4 + sub
                p.dma("sync", xt[:, :], x1hbm[rb], reads=[f"x1h{rb}"], writes=["xt"])
                p.vop(lambda e, i=A2[:, 2048 + sub * 2048: 2048 + (sub + 1) * 2048]:
                      e.tensor_add(xt2[:, :], xt[:, :], i),
                      reads=["xt", f"otm2_{tc}", "xt2r"], writes=["xt2"])
                p.dma("gpsimd", x2h[rb], xt2[:, :], reads=["xt2"], writes=[f"x2h{rb}"], q="dP")
                emit_ln(p, xt2[:, :], "xt2", xn_tm[:, :], "xn_tm", stats, rb % 2)
                p.dma("gpsimd", ag_in[rb * 128:(rb + 1) * 128, :], xn_tm[:, :],
                      reads=["xn_tm"], writes=[f"agi{rb}", "xt2r"], q="dP")
            p.deps["otm2done"] = {"w": None, "r": list(p.deps[f"otm2_{tc}"]["r"])}

        p.cc(0, "AllGather", ALU.bypass, groups, ag_in[:, :].opt(), ag_out[:, :].opt(),
             reads=[f"agi{rb}" for rb in range(8)], writes=["agout"])

        p.dma("sync", cw_sb[:, :], convw[:, :], writes=["cw"])
        p.dma("sync", cb_sb[:, :], convb[:, :], writes=["cb"])
        p.merge_key("cxnA1", ["mask", "prb", "prb2", "of", "bwbf"]
                    + [f"otm{t}" for t in range(2)] + [f"pT{k}" for k in range(16)]
                    + [f"att{k}" for k in range(16)] + ["ohT"])
        first_c = ["cxnA1"]
        p.merge_key("cvt0", ["vtA", "vtB"])
        for rb in range(16):
            xi = xn_tm if rb % 2 == 0 else vt0
            xik = "xn_tm" if rb % 2 == 0 else "cvt0"
            p.dma("sync", xi[:, :], ag_out[rb * 128:(rb + 1) * 128, :],
                  reads=["agout"], writes=[xik])
            for dtl in range(16):
                pbk = f"pb{dtl % 2}"
                pbt = pb[dtl % 2]
                p.op("tensor", lambda e, s=xi[:, dtl * 128:(dtl + 1) * 128],
                     o=pbt[:, 0:128]: e.transpose(o, s, ident[:, :]),
                     reads=[xik, "ident", pbk + "r"], writes=[pbk])
                p.vop(lambda e, i=pbt[:, 0:128],
                      o=A1[:, dtl * S + rb * 128: dtl * S + (rb + 1) * 128]:
                      e.tensor_copy(o, i),
                      reads=[pbk], writes=[f"cxnf{dtl}"] + first_c)
                first_c = []
        xrawv = A3[:, 0:2048]
        cnvA = A3[:, 2048:4096]
        cnvB = A3[:, 4096:6144]
        xctv = A3[:, 6144:8192]
        zstv = A3[:, 8192:10240]
        p.merge_key("cA3", ["xnf1A3"])

        def projc(wdram, m, dstT, dstKey, func, biascol, bias_reads, extra_w=()):
            us = [wfeed(wdram[m][hh * 8:(hh + 1) * 8].rearrange("k pp j -> pp k j"))
                  for hh in range(2)]
            for lch in range(4):
                bank = pf[lch % 2]
                bk = f"PF{lch % 2}"
                for k in range(16):
                    lh, lk = wlhs(us, k)
                    p.mm(bank[:, :], lh,
                         A1[:, k * S + lch * 512: k * S + lch * 512 + 512],
                         start=(k == 0), stop=(k == 15),
                         reads=[lk, f"cxnf{k}", bk + "r"],
                         writes=[bk] if k == 15 else [])
                p.act(dstT[:, lch * 512:(lch + 1) * 512], bank[:, :], func,
                      bias=biascol if biascol is not None else 0.0,
                      reads=[bk] + bias_reads, writes=[dstKey, bk + "r"] + list(extra_w))

        first_cA3 = ["cA3"]
        for dxt in range(16):
            projc(winx, dxt, xrawv, "xraw", AF.Copy, None, [], extra_w=first_cA3)
            first_cA3 = []
            wc = lambda j: cw_sb[:, dxt * 4 + j:dxt * 4 + j + 1]
            p.vop(lambda e, w=wc(3): e.tensor_scalar_mul(cnvA[:, :], xrawv[:, :], w),
                  reads=["xraw", "cw"], writes=["cA"])
            p.vop(lambda e, w=wc(2): e.scalar_tensor_tensor(cnvB[:, 1:S], xrawv[:, 0:S - 1], w,
                                                            cnvA[:, 1:S], ALU.mult, ALU.add),
                  reads=["xraw", "cA", "cw"], writes=["cB"])
            p.vop(lambda e: e.tensor_copy(cnvB[:, 0:1], cnvA[:, 0:1]),
                  reads=["cA"], writes=["cB"])
            p.vop(lambda e, w=wc(1): e.scalar_tensor_tensor(cnvA[:, 2:S], xrawv[:, 0:S - 2], w,
                                                            cnvB[:, 2:S], ALU.mult, ALU.add),
                  reads=["xraw", "cB", "cw", "cA"], writes=["cA"])
            p.vop(lambda e: e.tensor_copy(cnvA[:, 0:2], cnvB[:, 0:2]),
                  reads=["cB"], writes=["cA"])
            p.vop(lambda e, w=wc(0): e.scalar_tensor_tensor(cnvB[:, 3:S], xrawv[:, 0:S - 3], w,
                                                            cnvA[:, 3:S], ALU.mult, ALU.add),
                  reads=["xraw", "cA", "cw", "cB"], writes=["cB"])
            p.vop(lambda e: e.tensor_copy(cnvB[:, 0:3], cnvA[:, 0:3]),
                  reads=["cA"], writes=["cB"])
            p.act(xctv[:, :], cnvB[:, :], AF.Silu, bias=cb_sb[:, dxt:dxt + 1],
                  reads=["cB", "cb"], writes=["xct"])
            p.dma("gpsimd", xc_h[dxt], xctv[:, :], reads=["xct"], writes=[f"xch{dxt}"], q="dP")
            projc(winz, dxt, zstv, "zst", AF.Silu, None, [])
            p.dma("gpsimd", zs_h[dxt], zstv[:, :], reads=["zst"], writes=[f"zsh{dxt}"], q="dP")
        for dxt in range(16):
            p.dma("sync", wsg[0][:, 0:128], wxdt[dxt], writes=["wsg0"])
            p.dma("sync", wsg[0][:, 128:144], wxB[dxt], writes=["wsg0b"])
            p.dma("sync", wsg[0][:, 144:160], wxC[dxt], writes=["wsg0c"])
            p.vop(lambda e, o=wsm[:, dxt * 160:(dxt + 1) * 160]:
                  e.tensor_copy(o, wsg[0][:, 0:160]),
                  reads=["wsg0", "wsg0b", "wsg0c"], writes=["wsm"])
        for lch in range(4):
            xs2 = [A3[:, 10240:10752], A3[:, 10752:11264]]
            for dxt in range(16):
                xsv = xs2[dxt % 2]
                xk = f"xcs{dxt % 2}"
                p.dma("sync", xsv, xc_h[dxt][:, lch * 512:(lch + 1) * 512],
                      reads=[f"xch{dxt}"], writes=[xk])
                p.mm(pf[3][:, :], wsm[:, dxt * 160:dxt * 160 + 128], xsv,
                     start=(dxt == 0), stop=(dxt == 15),
                     reads=["wsm", xk, "PF3r"], writes=["PF3"] if dxt == 15 else [])
                p.mm(pf[4][0:16, :], wsm[:, dxt * 160 + 128:dxt * 160 + 144], xsv,
                     start=(dxt == 0), stop=(dxt == 15),
                     reads=["wsm", xk, "PF4r"], writes=["PF4"] if dxt == 15 else [])
                p.mm(pf[5][0:16, :], wsm[:, dxt * 160 + 144:dxt * 160 + 160], xsv,
                     start=(dxt == 0), stop=(dxt == 15),
                     reads=["wsm", xk, "PF5r"], writes=["PF5"] if dxt == 15 else [])
            p.vop(lambda e, o=xt[:, 0:512], i=pf[3][:, :]: e.tensor_copy(o, i),
                  reads=["PF3"], writes=["xtd", "PF3r", "xt"])
            p.vop(lambda e, o=xt[0:16, 512:1024], i=pf[4][0:16, :]: e.tensor_copy(o, i),
                  reads=["PF4", "xtd"], writes=["xtd", "PF4r"])
            p.vop(lambda e, o=xt[0:16, 1024:1536], i=pf[5][0:16, :]: e.tensor_copy(o, i),
                  reads=["PF5", "xtd"], writes=["xtd", "PF5r"])
            p.dma("gpsimd", xdbl_in[0:128, lch * 512:(lch + 1) * 512], xt[:, 0:512],
                  reads=["xtd"], writes=[f"xdi{lch}a"], q="dP")
            p.dma("gpsimd", xdbl_in[128:144, lch * 512:(lch + 1) * 512], xt[0:16, 512:1024],
                  reads=["xtd"], writes=[f"xdi{lch}b"], q="dP")
            p.dma("gpsimd", xdbl_in[144:160, lch * 512:(lch + 1) * 512], xt[0:16, 1024:1536],
                  reads=["xtd"], writes=[f"xdi{lch}c", "xt"], q="dP")

        p.cc(1, "AllReduce", ALU.add, groups, xdbl_in[:, :].opt(), xdbl_out[:, :].opt(),
             reads=[f"xdi{l}{c}" for l in range(4) for c in "abc"], writes=["xdo"])

        dt_bf = A1[:, 0:2048]
        xct_b = [A1[:, 2048:3072], A1[:, 14336:15360]]
        zst_b = [A1[:, 3072:4096], A1[:, 15360:16384]]
        dxb = A1[:, 4096:5120]
        dA = [A1[:, 5120:6144], A1[:, 6144:7168]]
        ub = [A1[:, 7168:8192], A1[:, 8192:9216]]
        sn = [A1[:, 9216:10240], A1[:, 10240:11264]]
        csb = A1[:, 11264:12288]
        yacc = A1[:, 12288:13312]
        y2 = A1[:, 13312:14336]
        bc16s = A1[0:32, 14336:16384]
        BB = A2[:, 0:16384]
        CB = A2[:, 16384:32768]
        bfl = A3[0:1, 0:1024]
        onesr = A3[0:1, 1024:1152]
        wdt_bf = vt0[:, 0:2048]
        deltav = xt2[:, 0:1024]
        absx = xt[:, 0:512]
        relux = xt[:, 512:1024]
        expx = xt[:, 1024:1536]
        _old = [p.deps.pop(k, None) for k in ("xtd", "xt")]
        _rs = []
        for _d in _old:
            if _d:
                if _d["w"]:
                    _rs.append(_d["w"])
                _rs.extend(_d["r"])
        for _k in ("absx", "relux", "expx"):
            p.deps[_k] = {"w": None, "r": list(_rs)}
        p.merge_key("dA1", [f"cxnf{k}" for k in range(16)] + ["xn_tm"])
        p.merge_key("dA2", ["maskfm", "hm", "wsm"] + [f"otm2_{t}" for t in range(2)])
        p.merge_key("wdtb0", ["cvt0"])
        _old = [p.deps.pop(k, None) for k in ("xraw", "cA")]
        _rs = []
        for _d in _old:
            if _d:
                if _d["w"]:
                    _rs.append(_d["w"])
                _rs.extend(_d["r"])
        for _k in ("bfl", "onesr"):
            p.deps[_k] = {"w": None, "r": list(_rs)}
        p.vop(lambda e: e.memset(onesr, 1.0), writes=["onesr"])
        p.dma("sync", bdt_sb[:, :], bdt[:, :], writes=["bdt"])
        p.dma("sync", dsk_sb[:, :], dsk[:, :], writes=["dsk"])
        p.dma("sync", xt2[:, :], xdbl_out[0:128, :], reads=["xdo"], writes=["xt2"])
        p.vop(lambda e, o=dt_bf: e.tensor_copy(o, xt2[:, :]),
              reads=["xt2"], writes=["dtbf", "dA1"])
        p.dma("sync", xt2[0:32, :], xdbl_out[128:160, :], reads=["xdo"], writes=["xt2"])
        p.vop(lambda e, o=bc16s: e.tensor_copy(o, xt2[0:32, :]),
              reads=["xt2"], writes=["bc16"])
        p.dma("gpsimd", bc16_h[:, :], bc16s, reads=["bc16"], writes=["bch"], q="dP")
        p.dma("sync", acol[:, :], alog[:, :], reads=["bc16"], writes=["acol", "xt2"])
        p.act(acol[:, :], acol[:, :], AF.Exp, reads=["acol"], writes=["acol"])
        p.vop(lambda e: e.tensor_scalar_mul(acol[:, :], acol[:, :], -1.0),
              reads=["acol"], writes=["acol"])
        p.merge_key("delta", ["xt2"])
        for hh in range(2):
            sg = hh % 2
            p.dma("sync", wsg[sg][:, :], wdt[:, hh * 1024:(hh + 1) * 1024],
                  writes=[f"wsg{sg}"])
            p.vop(lambda e, o=wdt_bf[:, hh * 1024:(hh + 1) * 1024], s=wsg[sg]:
                  e.tensor_copy(o, s[:, :]),
                  reads=[f"wsg{sg}", "wdtb0" if hh == 0 else "wdtb"], writes=["wdtb"])

        for lc in range(2):
            for n in range(16):
                p.dma("sync", bfl, bc16_h[n:n + 1, lc * LC:(lc + 1) * LC],
                      reads=["bch"], writes=["bfl"])
                for c2 in range(2):
                    p.mm(pf[5][:, :], onesr,
                         bfl[0:1, c2 * 512:(c2 + 1) * 512],
                         start=True, stop=True, reads=["onesr", "bfl", "PF5r"], writes=["PF5"])
                    p.vop(lambda e, o=BB[:, n * LC + c2 * 512: n * LC + c2 * 512 + 512],
                          i=pf[5][:, :]: e.tensor_copy(o, i),
                          reads=["PF5"], writes=[f"BB{n}", "PF5r"] + (["dA2"] if (n == 0 and c2 == 0 and lc == 0) else []))
                p.dma("sync", bfl, bc16_h[16 + n:17 + n, lc * LC:(lc + 1) * LC],
                      reads=["bch"], writes=["bfl"])
                for c2 in range(2):
                    p.mm(pf[5][:, :], onesr,
                         bfl[0:1, c2 * 512:(c2 + 1) * 512],
                         start=True, stop=True, reads=["onesr", "bfl", "PF5r"], writes=["PF5"])
                    p.vop(lambda e, o=CB[:, n * LC + c2 * 512: n * LC + c2 * 512 + 512],
                          i=pf[5][:, :]: e.tensor_copy(o, i),
                          reads=["PF5"], writes=[f"CB{n}", "PF5r"])
            for dxt in range(16):
                pr_ = dxt % 2
                xct_d = xct_b[pr_]
                zst_d = zst_b[pr_]
                xk_ = f"xctd{pr_}"
                zk_ = f"zstd{pr_}"
                p.dma("sync", xct_d, xc_h[dxt][:, lc * LC:(lc + 1) * LC],
                      reads=[f"xch{dxt}"] + (["bc16"] if (lc == 0 and dxt == 1) else []),
                      writes=[xk_])
                p.dma("sync", zst_d, zs_h[dxt][:, lc * LC:(lc + 1) * LC],
                      reads=[f"zsh{dxt}"], writes=[zk_])
                for c2 in range(2):
                    bank = pf[c2 % 2]
                    bk = f"PF{c2 % 2}"
                    p.mm(bank[:, :], wdt_bf[:, dxt * 128:(dxt + 1) * 128],
                         dt_bf[:, lc * LC + c2 * 512: lc * LC + c2 * 512 + 512],
                         start=True, stop=True,
                         reads=["wdtb", "dtbf", bk + "r"], writes=[bk])
                    bcol = bdt_sb[:, dxt:dxt + 1]
                    p.act(absx, bank[:, :], AF.Abs, bias=bcol,
                          reads=[bk, "bdt"], writes=["absx"])
                    p.vop(lambda e, i=bank[:, :], bc=bcol: e.tensor_scalar(
                          relux, i, bc, 0.0, ALU.add, ALU.max),
                          reads=[bk, "bdt", "absx"], writes=["relux", bk + "r"])
                    p.act(expx, absx, AF.Exp, scale=stats["cst"][:, 3:4],
                          reads=["absx", "cst"], writes=["expx"])
                    p.act(expx, expx, AF.Ln, bias=1.0,
                          reads=["expx"], writes=["expx"])
                    p.vop(lambda e, o=deltav[:, c2 * 512:(c2 + 1) * 512]:
                          e.tensor_add(o, relux, expx),
                          reads=["relux", "expx"], writes=["delta"])
                p.vop(lambda e, x=xct_d: e.tensor_mul(dxb, deltav, x),
                      reads=["delta", xk_], writes=["dxb"])
                for n in range(16):
                    da = dA[n % 2]
                    dak = f"dA{n % 2}"
                    u = ub[n % 2]
                    uk = f"ub{n % 2}"
                    s_ = sn[n % 2]
                    sk_ = f"sn{n % 2}"
                    p.act(da, deltav, AF.Exp,
                          scale=acol[:, dxt * 16 + n:dxt * 16 + n + 1],
                          reads=["delta", "acol", dak], writes=[dak])
                    p.vop(lambda e, bb=BB[:, n * LC:(n + 1) * LC], u_=u:
                          e.tensor_mul(u_, dxb, bb),
                          reads=["dxb", f"BB{n}", uk], writes=[uk])
                    ini = 0.0 if lc == 0 else stc[:, dxt * 16 + n:dxt * 16 + n + 1]
                    p.vop(lambda e, ini=ini, s=s_, d=da, u_=u: e.tensor_tensor_scan(
                          s, d, u_, ini, ALU.mult, ALU.add),
                          reads=[dak, uk, "stc", sk_], writes=[sk_])
                    p.vop(lambda e, o=stc[:, dxt * 16 + n:dxt * 16 + n + 1], s=s_:
                          e.tensor_copy(o, s[:, LC - 1:LC]),
                          reads=[sk_], writes=["stc"])
                    p.vop(lambda e, cb=CB[:, n * LC:(n + 1) * LC], s=s_:
                          e.tensor_mul(csb, s, cb),
                          reads=[sk_, f"CB{n}", "csb"], writes=["csb"])
                    if n == 0:
                        p.vop(lambda e: e.tensor_copy(yacc, csb),
                              reads=["csb"], writes=["yacc"])
                    else:
                        p.vop(lambda e: e.tensor_add(yacc, yacc, csb),
                              reads=["csb", "yacc"], writes=["yacc"])
                p.vop(lambda e, dc=dsk_sb[:, dxt:dxt + 1], x=xct_d: e.scalar_tensor_tensor(
                      y2, x, dc, yacc, ALU.mult, ALU.add),
                      reads=[xk_, "dsk", "yacc", "y2"], writes=["y2"])
                p.vop(lambda e, z=zst_d: e.tensor_mul(y2, y2, z),
                      reads=[zk_, "y2"], writes=["y2"])
                p.dma("gpsimd", yg_h[dxt][:, lc * LC:(lc + 1) * LC], y2,
                      reads=["y2"], writes=[f"ygh{dxt}_{lc}"], q="dA")

        p.merge_key("xt", ["absx", "relux", "expx", "xt"])
        p.merge_key("xt2", ["delta", "acol", "stc"])
        p.merge_key("of2", ["zst"])
        p.merge_key("eAY", [f"BB{n}" for n in range(16)] + [f"CB{n}" for n in range(16)])
        first_e = ["eAY"]
        for k in range(16):
            p.dma("sync", A2[:, k * S:(k + 1) * S], yg_h[k],
                  reads=[f"ygh{k}_0", f"ygh{k}_1"], writes=[f"eyg{k}"] + first_e)
            first_e = []
        for m in range(16):
            us = [wfeed(wout[m][hh * 8:(hh + 1) * 8].rearrange("k pp j -> pp k j"))
                  for hh in range(2)]
            for tc in range(4):
                bank = pf[tc % 2]
                bk = f"PF{tc % 2}"
                for k in range(16):
                    lh, lk = wlhs(us, k)
                    p.mm(bank[:, :], lh,
                         A2[:, k * S + tc * 512: k * S + tc * 512 + 512],
                         start=(k == 0), stop=(k == 15),
                         reads=[lk, f"eyg{k}", bk + "r"],
                         writes=[bk] if k == 15 else [])
                p.act(xt[:, 0:512], bank[:, :], AF.Copy,
                      reads=[bk], writes=["epart", bk + "r", "xt"])
                p.dma("gpsimd", rs_in[(tc // 2) * 2048 + m * 128:(tc // 2) * 2048 + (m + 1) * 128,
                                      (tc % 2) * 512:(tc % 2) * 512 + 512],
                      xt[:, 0:512], reads=["epart"], writes=[f"rsi{m}_{tc}"], q="dP")

        p.cc(2, "ReduceScatter", ALU.add, groups, rs_in[:, :].opt(), rs_out[:, :].opt(),
             reads=[f"rsi{m}_{tc}" for m in range(16) for tc in range(4)], writes=["rso"])

        p.dma("sync", bcols[:, 0:16], bout[:, :], writes=["bc"])
        p.dma("sync", bcols[:, 16:48], b1[:, :], writes=["bc1"])
        p.dma("sync", bcols[:, 48:80], b2[:, :], writes=["bc2"])
        p.dma("sync", bcols[:, 80:96], b3[:, :], writes=["bc3"])
        p.merge_key("eotm", ["dtbf", "xctd0", "xctd1", "zstd0", "zstd1", "dxb",
                             "dA0", "dA1", "ub0", "ub1",
                             "sn0", "sn1", "csb", "yacc", "y2", "bc16"])
        first_eo = ["eotm"]
        for m in range(16):
            mo = (m % 2) * 1024
            mk = f"xt2h{m % 2}"
            p.dma("sync", xt2[:, mo:mo + 1024], rs_out[m * 128:(m + 1) * 128, :],
                  reads=["rso"] + (["xt2"] if m < 2 else []), writes=[mk])
            for ch in range(2):
                p.act(of2[:, :], xt2[:, mo + ch * 512:mo + (ch + 1) * 512], AF.Identity,
                      bias=bcols[:, m:m + 1],
                      reads=[mk, "bc", "of2"], writes=["of2"])
                for sub in range(4):
                    rb8 = ch * 4 + sub
                    pbk = f"pb{sub % 2}"
                    pbt = pb[sub % 2]
                    p.op("tensor", lambda e, s=of2[:, sub * 128:(sub + 1) * 128],
                         o=pbt[:, 0:128]: e.transpose(o, s, ident[:, :]),
                         reads=["of2", "ident", pbk + "r"], writes=[pbk])
                    p.vop(lambda e, i=pbt[:, 0:128],
                          o=A1[:, rb8 * 2048 + m * 128: rb8 * 2048 + m * 128 + 128]:
                          e.tensor_copy(o, i),
                          reads=[pbk], writes=["eotm", pbk + "r"] if first_eo == [] else ["eotm", pbk + "r"])
        p.merge_key("xt2", ["xt2h0", "xt2h1"])
        for rb in range(8):
            p.dma("sync", xt[:, :], x2h[rb], reads=[f"x2h{rb}"], writes=["xt", "epart"])
            p.vop(lambda e, i=A1[:, rb * 2048:(rb + 1) * 2048]:
                  e.tensor_add(xt2[:, :], xt[:, :], i),
                  reads=["xt", "eotm", "xt2"], writes=["xt2"])
            p.dma("gpsimd", x1hbm[rb], xt2[:, :], reads=["xt2"], writes=[f"x3h{rb}"], q="dP")
            emit_ln(p, xt2[:, :], "xt2", xn_tm[:, :], "xn_tm", stats, rb % 2)
            for dtl in range(16):
                pbk = f"pb{dtl % 2}"
                pbt = pb[dtl % 2]
                p.op("tensor", lambda e, s=xn_tm[:, dtl * 128:(dtl + 1) * 128],
                     o=pbt[:, 0:128]: e.transpose(o, s, ident[:, :]),
                     reads=["xn_tm", "ident", pbk + "r"], writes=[pbk])
                p.vop(lambda e, i=pbt[:, 0:128],
                      o=A3[:, dtl * HALF + rb * 128: dtl * HALF + rb * 128 + 128]:
                      e.tensor_copy(o, i),
                      reads=[pbk], writes=["x3f", pbk + "r"] +
                      (["cB", "xct", "xcs0", "xcs1", "bfl", "onesr", "of2"]
                       if (rb == 0 and dtl == 0) else []))
        for m in range(32):
            us = [wfeed(w1[m][hh * 8:(hh + 1) * 8].rearrange("k pp j -> pp k j"))
                  for hh in range(2)]
            for tc in range(2):
                bank = pf[tc % 2]
                bk = f"PF{tc % 2}"
                for k in range(16):
                    lh, lk = wlhs(us, k)
                    p.mm(bank[:, :], lh,
                         A3[:, k * HALF + tc * 512: k * HALF + tc * 512 + 512],
                         start=(k == 0), stop=(k == 15),
                         reads=[lk, "x3f", bk + "r"],
                         writes=[bk] if k == 15 else [])
                p.act(A1[:, m * HALF + tc * 512: m * HALF + tc * 512 + 512],
                      bank[:, :], AF.Relu, bias=bcols[:, 16 + m:17 + m],
                      reads=[bk, "bc1"],
                      writes=[f"h1_{m}", bk + "r"] + (["eotm"] if (m == 0 and tc == 0) else []))
        for m in range(32):
            us = [wfeed(w2[m][hh * 8:(hh + 1) * 8].rearrange("k pp j -> pp k j"))
                  for hh in range(4)]
            for tc in range(2):
                bank = pf[tc % 2]
                bk = f"PF{tc % 2}"
                for k in range(32):
                    lh, lk = wlhs(us, k)
                    p.mm(bank[:, :], lh,
                         A1[:, k * HALF + tc * 512: k * HALF + tc * 512 + 512],
                         start=(k == 0), stop=(k == 31),
                         reads=[lk, f"h1_{k}", bk + "r"],
                         writes=[bk] if k == 31 else [])
                p.act(A2[:, m * HALF + tc * 512: m * HALF + tc * 512 + 512],
                      bank[:, :], AF.Relu, bias=bcols[:, 48 + m:49 + m],
                      reads=[bk, "bc2"],
                      writes=[f"h2_{m}", bk + "r"] + ([f"eyg{k}" for k in range(16)]
                                                      if (m == 0 and tc == 0) else []))
        p.merge_key("eo3", ["x3f"])
        for tc in range(2):
            for m in range(16):
                us = [wfeed(w3[m][hh * 8:(hh + 1) * 8].rearrange("k pp j -> pp k j"))
                      for hh in range(4)]
                bank = pf[m % 2]
                bk = f"PF{m % 2}"
                for k in range(32):
                    lh, lk = wlhs(us, k)
                    p.mm(bank[:, :], lh,
                         A2[:, k * HALF + tc * 512: k * HALF + tc * 512 + 512],
                         start=(k == 0), stop=(k == 31),
                         reads=[lk, f"h2_{k}", bk + "r"],
                         writes=[bk] if k == 31 else [])
                p.act(of2[:, :], bank[:, :], AF.Identity, bias=bcols[:, 80 + m:81 + m],
                      reads=[bk, "bc3", "of2"], writes=["of2", bk + "r"])
                for sub in range(4):
                    pbk = f"pb{sub % 2}"
                    pbt = pb[sub % 2]
                    p.op("tensor", lambda e, s=of2[:, sub * 128:(sub + 1) * 128],
                         o=pbt[:, 0:128]: e.transpose(o, s, ident[:, :]),
                         reads=["of2", "ident", pbk + "r"], writes=[pbk])
                    p.vop(lambda e, i=pbt[:, 0:128],
                          o=A3[:, sub * 2048 + m * 128: sub * 2048 + m * 128 + 128]:
                          e.tensor_copy(o, i),
                          reads=[pbk], writes=["eo3", pbk + "r"])
            for sub in range(4):
                rb = tc * 4 + sub
                p.dma("sync", xt[:, :], x1hbm[rb], reads=[f"x3h{rb}", "xt"], writes=["xt"])
                p.vop(lambda e, i=A3[:, sub * 2048:(sub + 1) * 2048]:
                      e.tensor_add(xt2[:, :], xt[:, :], i),
                      reads=["xt", "eo3", "xt2"], writes=["xt2"])
                p.dma("gpsimd", x4out[rb], xt2[:, :], reads=["xt2"], writes=["xt2"], q="dA")
        p.finalize(blk)
    return nc



def tile4(wt):
    K_, M_ = wt.shape
    return np.ascontiguousarray(
        wt.reshape(K_ // 128, 128, M_ // 128, 128).transpose(2, 0, 1, 3))


_ROPE = {}


def rope_tables():
    if "cos" not in _ROPE:
        j = np.arange(0, D, 2, dtype=np.float64)
        inv = 1.0 / (10000.0 ** (j / D))
        pos = np.arange(S, dtype=np.float64)
        ang = pos[None, :] * inv[:, None]
        cosv, sinv = np.cos(ang), np.sin(ang)
        cosT = np.empty((D, S), np.float32)
        sinT = np.empty((D, S), np.float32)
        cosT[0::2] = cosv
        cosT[1::2] = cosv
        sinT[0::2] = -sinv
        sinT[1::2] = sinv
        _ROPE["cos"] = cosT.astype(bf16)
        _ROPE["sin"] = sinT.astype(bf16)
    return _ROPE["cos"], _ROPE["sin"]


def perm_pairswap():
    p = np.zeros((128, 128), bf16)
    for i in range(128):
        p[i ^ 1, i] = 1.0
    return p


def rep8_mat():
    r = np.zeros((8, 128), bf16)
    for e in range(8):
        r[e, e * 16:(e + 1) * 16] = 1.0
    return r


IDENT = np.eye(128, dtype=bf16)


def colmat(v, ntiles):
    return np.ascontiguousarray(v.reshape(ntiles, 128).T).astype(f32)


def stage_l0(inputs, b, h):
    perm = np.r_[np.arange(h * HALF, (h + 1) * HALF),
                 np.arange((1 - h) * HALF, (2 - h) * HALF)]
    x0 = np.asarray(inputs["L0_attn_q"], f32)[b]
    cosT, sinT = rope_tables()
    cq = np.ascontiguousarray(cosT[:, h * HALF:(h + 1) * HALF]).reshape(16, 128, HALF)
    sq = np.ascontiguousarray(sinT[:, h * HALF:(h + 1) * HALF]).reshape(16, 128, HALF)
    ck = np.ascontiguousarray(cosT[:, perm]).reshape(16, 128, S)
    sk = np.ascontiguousarray(sinT[:, perm]).reshape(16, 128, S)
    qglob = h * HALF + np.arange(HALF)
    mask = np.where(perm[None, :] <= qglob[:, None], 1.0, 0.0).astype(bf16)
    A = np.asarray(inputs["L0_ffn_A"], f32)
    Bw = np.asarray(inputs["L0_ffn_Bw"], f32)
    at = np.ascontiguousarray(A.transpose(2, 0, 1).reshape(D, 128))
    bw_t = np.ascontiguousarray(Bw.transpose(0, 2, 1).reshape(128, D))
    return {
        "x0p": np.ascontiguousarray(x0[perm]),
        "wq": tile4(np.asarray(inputs["L0_attn_W_q"], f32)),
        "wk": tile4(np.asarray(inputs["L0_attn_W_k"], f32)),
        "wv": np.ascontiguousarray(np.asarray(inputs["L0_attn_W_v"], f32).reshape(16, 128, D)),
        "wo": tile4(np.asarray(inputs["L0_attn_W_o"], f32)),
        "wbase": tile4(np.ascontiguousarray(np.asarray(inputs["L0_ffn_W_base"], f32).T)),
        "bbase": colmat(np.asarray(inputs["L0_ffn_b_base"], f32), 16),
        "rwt": np.ascontiguousarray(np.asarray(inputs["L0_ffn_router_W"], f32).T
                                    .reshape(16, 128, 8).transpose(1, 0, 2).reshape(128, 128)),
        "at4": np.ascontiguousarray(at.reshape(16, 128, 128)),
        "bwt": np.ascontiguousarray(bw_t.reshape(128, 16, 128).transpose(1, 0, 2)),
        "cosq": cq, "sinq": sq, "cosk": ck, "sink": sk,
        "maskq": np.ascontiguousarray(mask.reshape(8, 128, S)),
        "ident_i": IDENT, "perm_i": perm_pairswap(), "rep8_i": rep8_mat(),
    }


def stage_c(inputs, x2_full, b, h):
    win = np.asarray(inputs["L1_attn_W_in"], f32)
    if x2_full is None:
        x2f = None
    wx = np.asarray(inputs["L1_attn_W_x"], f32)
    cw = np.asarray(inputs["L1_attn_conv_w"], f32)[h * 2048:(h + 1) * 2048]
    cb = np.asarray(inputs["L1_attn_conv_b"], f32)[h * 2048:(h + 1) * 2048]
    sl = slice(h * 2048, (h + 1) * 2048)
    return {
        **({} if x2_full is None else
           {"x2f": np.ascontiguousarray(x2_full[b].reshape(16, 128, D))}),
        "winx": tile4(np.ascontiguousarray(win[sl].T)),
        "winz": tile4(np.ascontiguousarray(win[4096 + h * 2048: 4096 + (h + 1) * 2048].T)),
        "wxdt": np.ascontiguousarray(wx[0:128, sl].T.reshape(16, 128, 128)),
        "wxB": np.ascontiguousarray(wx[128:144, sl].T.reshape(16, 128, 16)),
        "wxC": np.ascontiguousarray(wx[144:160, sl].T.reshape(16, 128, 16)),
        "convw": np.ascontiguousarray(cw.reshape(16, 128, 4).transpose(1, 0, 2).reshape(128, 64)),
        "convb": colmat(cb, 16),
        "ident_i": IDENT,
    }


def stage_d(inputs, c_outs, b, h):
    me, other = 2 * b + h, 2 * b + (1 - h)
    sl = slice(h * 2048, (h + 1) * 2048)
    alog = np.asarray(inputs["L1_attn_A_log"], f32)[sl]
    return {
        "xc_i": c_outs[me]["xc_o"],
        "zs_i": c_outs[me]["zs_o"],
        "xdt0": c_outs[2 * b]["xdt_o"],
        "xdt1": c_outs[2 * b + 1]["xdt_o"],
        "xB0": c_outs[2 * b]["xB_o"],
        "xB1": c_outs[2 * b + 1]["xB_o"],
        "xC0": c_outs[2 * b]["xC_o"],
        "xC1": c_outs[2 * b + 1]["xC_o"],
        "wdt": np.ascontiguousarray(np.asarray(inputs["L1_attn_W_dt"], f32)[sl].T),
        "bdt": colmat(np.asarray(inputs["L1_attn_b_dt"], f32)[sl], 16),
        "alog": np.ascontiguousarray(alog.reshape(16, 128, 16).transpose(1, 0, 2).reshape(128, 256)),
        "dsk": colmat(np.asarray(inputs["L1_attn_D"], f32)[sl], 16),
        "ones_i": np.ones((1, 128), bf16),
    }


def stage_e(inputs, x2_full, yg_list, b, h):
    ygf = np.concatenate([yg_list[2 * b], yg_list[2 * b + 1]], axis=0)
    ygf = np.ascontiguousarray(ygf[:, :, h * HALF:(h + 1) * HALF])
    return {
        "x2own": np.ascontiguousarray(x2_full[b, h * HALF:(h + 1) * HALF].reshape(8, 128, D)),
        "ygf": ygf,
        "wout": tile4(np.ascontiguousarray(np.asarray(inputs["L1_attn_W_out"], f32).T)),
        "bout": colmat(np.asarray(inputs["L1_attn_b_out"], f32), 16),
        "w1": tile4(np.ascontiguousarray(np.asarray(inputs["L1_ffn_W1"], f32).T)),
        "b1": colmat(np.asarray(inputs["L1_ffn_b1"], f32), 32),
        "w2": tile4(np.ascontiguousarray(np.asarray(inputs["L1_ffn_W2"], f32).T)),
        "b2": colmat(np.asarray(inputs["L1_ffn_b2"], f32), 32),
        "w3": tile4(np.ascontiguousarray(np.asarray(inputs["L1_ffn_W3"], f32).T)),
        "b3": colmat(np.asarray(inputs["L1_ffn_b3"], f32), 16),
        "ident_i": IDENT,
    }


def stage_fused(inputs, b, h):
    st = stage_l0(inputs, b, h)
    st.pop("x2f", None)
    c = stage_c(inputs, None, b, h)
    c.pop("ident_i")
    st.update(c)
    d = {
        "wdt": np.ascontiguousarray(np.asarray(inputs["L1_attn_W_dt"], f32)[h * 2048:(h + 1) * 2048].T),
        "bdt": colmat(np.asarray(inputs["L1_attn_b_dt"], f32)[h * 2048:(h + 1) * 2048], 16),
        "alog": np.ascontiguousarray(np.asarray(inputs["L1_attn_A_log"], f32)[h * 2048:(h + 1) * 2048]
                                     .reshape(16, 128, 16).transpose(1, 0, 2).reshape(128, 256)),
        "dsk": colmat(np.asarray(inputs["L1_attn_D"], f32)[h * 2048:(h + 1) * 2048], 16),
        "wout": tile4(np.ascontiguousarray(
            np.asarray(inputs["L1_attn_W_out"], f32).T[h * 2048:(h + 1) * 2048, :])),
        "bout": colmat(np.asarray(inputs["L1_attn_b_out"], f32), 16),
        "w1": tile4(np.ascontiguousarray(np.asarray(inputs["L1_ffn_W1"], f32).T)),
        "b1": colmat(np.asarray(inputs["L1_ffn_b1"], f32), 32),
        "w2": tile4(np.ascontiguousarray(np.asarray(inputs["L1_ffn_W2"], f32).T)),
        "b2": colmat(np.asarray(inputs["L1_ffn_b2"], f32), 32),
        "w3": tile4(np.ascontiguousarray(np.asarray(inputs["L1_ffn_W3"], f32).T)),
        "b3": colmat(np.asarray(inputs["L1_ffn_b3"], f32), 16),
    }
    st.update(d)
    return st


GROUPS = [[0, 1], [2, 3], [4, 5], [6, 7]]


def kernel(**inputs):
    cores = list(range(NCORES))
    nc = build_fused(GROUPS)
    res = run_bass_kernel_spmd(
        nc, [stage_fused(inputs, c // 2, c % 2) for c in cores], core_ids=cores)
    x4 = np.zeros((B, S, D), f32)
    for c in cores:
        b, h = c // 2, c % 2
        x4[b, h * HALF:(h + 1) * HALF] = res.results[c]["x4out"].reshape(HALF, D)
    return x4


def kernel_unfused(**inputs):
    cores = list(range(NCORES))
    nc0 = build_l0()
    r0 = run_bass_kernel_spmd(
        nc0, [stage_l0(inputs, c // 2, c % 2) for c in cores], core_ids=cores)
    x2 = np.zeros((B, S, D), f32)
    for c in cores:
        b, h = c // 2, c % 2
        x2[b, h * HALF:(h + 1) * HALF] = r0.results[c]["x2out"].reshape(HALF, D)
    ncc = build_c()
    rc = run_bass_kernel_spmd(
        ncc, [stage_c(inputs, x2, c // 2, c % 2) for c in cores], core_ids=cores)
    c_outs = rc.results
    ncd = build_d()
    rd = run_bass_kernel_spmd(
        ncd, [stage_d(inputs, c_outs, c // 2, c % 2) for c in cores], core_ids=cores)
    yg_list = [rd.results[c]["yg_o"] for c in cores]
    nce = build_e()
    re_ = run_bass_kernel_spmd(
        nce, [stage_e(inputs, x2, yg_list, c // 2, c % 2) for c in cores], core_ids=cores)
    x4 = np.zeros((B, S, D), f32)
    for c in cores:
        b, h = c // 2, c % 2
        x4[b, h * HALF:(h + 1) * HALF] = re_.results[c]["x4out"].reshape(HALF, D)
    return x4
